# version 23
# baseline (speedup 1.0000x reference)
import numpy as np
import ml_dtypes
from contextlib import ExitStack

import concourse.bass as bass
import concourse.mybir as mybir
from concourse.bass_utils import run_bass_kernel_spmd

F32 = mybir.dt.float32
BF16 = mybir.dt.bfloat16
AF = mybir.ActivationFunctionType
ALU = mybir.AluOpType
AX = mybir.AxisListType

S, D, H, DH, CH = 4096, 1024, 16, 64, 8
SCALE = 1.0 / 8.0
EPS = 1e-5

_a = 0.0
for _ in range(5):
    _a = _a * 0.9 + 1.0
TAUS = [c / _a for c in (1.0, 1.9, 2.81, 3.71, 4.439)]

NB, T0 = 16, 256
NT3 = 32

ENG_SEM = {"tensor": "pe", "scalar": "act", "vector": "dve", "gpsimd": "pool"}
SEM_NAMES = ("pe", "act", "dve", "pool", "lc",
             "lw0", "lw1", "lw2", "lw3", "lw4", "lw5", "lw6", "lw7",
             "ln0", "ln1", "ln2", "ln3", "ln4", "ln5", "ln6", "ln7",
             "l30", "l31", "l32",
             "lb0", "lb1",
             "ltm0", "ltm1",
             "lmq0", "lmq1",
             "lqs",
             "cr0", "cr1",
             "sx0", "sx1",
             "tm0", "tm1",
             "mq0", "mq1",
             "sq",
             "sc0", "sc1",
             "so0", "so1")
DMA_SEMS = set(SEM_NAMES[4:])


class Plan:
    def __init__(self):
        self.ops = {k: [] for k in ("tensor", "scalar", "vector", "gpsimd", "sync")}
        self.cnt = {k: 0 for k in SEM_NAMES}
        self.res = {}
        self.hiwater = {k: {} for k in self.ops}

    def _need_wait(self, eng, sem, val):
        hw = self.hiwater[eng]
        if hw.get(sem, 0) >= val:
            return False
        hw[sem] = val
        return True

    STRICT = {"lns", "rs", "pq", "pk", "ceps", "ones3",
              "bq", "bk", "bt", "bqa", "bka", "bstg"}

    def _op(self, eng, fn, reads, writes, sem, n):
        waits = {}
        my_val = self.cnt[sem] + n

        def add(dep_eng, dep_sem, dep_val, strict):
            if dep_eng == eng and dep_sem == "pe" and sem == "pe":
                return
            if dep_eng == eng and dep_sem in DMA_SEMS and sem == dep_sem:
                return
            waits[dep_sem] = max(waits.get(dep_sem, 0), dep_val)

        for k in reads:
            st = self.res.get(k)
            if st and st["w"]:
                add(*st["w"], k[0] in self.STRICT)
        for k in writes:
            st = self.res.get(k)
            if st:
                strict = k[0] in self.STRICT
                if st["w"]:
                    add(*st["w"], strict)
                for r in st["r"]:
                    add(*r, strict)
        for k in reads:
            st = self.res.get(k)
            if st is not None:
                st["r"].append((eng, sem, my_val))
            else:
                self.res[k] = {"w": None, "r": [(eng, sem, my_val)]}
        for k in writes:
            self.res[k] = {"w": (eng, sem, my_val), "r": []}
        self.cnt[sem] = my_val
        real_waits = [(s, v) for s, v in waits.items() if self._need_wait(eng, s, v)]
        self.ops[eng].append((real_waits, fn, sem, n))
        return my_val

    def comp(self, eng, fn, reads=(), writes=()):
        return self._op(eng, fn, reads, writes, ENG_SEM[eng], 1)

    def dma(self, queue, sem, fn, reads=(), writes=()):
        return self._op(queue, fn, reads, writes, sem, 16)

    def patch_total(self, keys, eng, sem):
        tot = (eng, sem, self.cnt[sem])
        for k in keys:
            if k in self.res:
                self.res[k]["w"] = tot

    def wait_all(self, eng, sems_):
        for s in sems_:
            v = self.cnt[s]
            if v and self._need_wait(eng, s, v):
                self.ops[eng].append(([(s, v)], None, None, 0))

    def barrier(self):
        for e in self.ops:
            self.wait_all(e, [s for s in self.cnt if s != ENG_SEM.get(e)])


def build_nc(dbg=False):
    nc = bass.Bass("TRN2", target_bir_lowering=False)

    x_d = nc.dram_tensor("x", [S, D], F32, kind="ExternalInput")
    Wq_d = nc.dram_tensor("Wq", [D, D], F32, kind="ExternalInput")
    Wk_d = nc.dram_tensor("Wk", [D, D], F32, kind="ExternalInput")
    WkT_d = nc.dram_tensor("WkT", [D, D], F32, kind="ExternalInput")
    Wt_d = nc.dram_tensor("Wt", [D, D], F32, kind="ExternalInput")
    Wqa_d = nc.dram_tensor("Wqa", [D, H], F32, kind="ExternalInput")
    Wka_d = nc.dram_tensor("Wka", [D, H], F32, kind="ExternalInput")
    WfcT_d = nc.dram_tensor("WfcT", [D, 4 * D], F32, kind="ExternalInput")
    WprojT_d = nc.dram_tensor("WprojT", [4 * D, D], F32, kind="ExternalInput")
    bq_d = nc.dram_tensor("bq", [1, D], F32, kind="ExternalInput")
    bk_d = nc.dram_tensor("bk", [1, D], F32, kind="ExternalInput")
    bt_d = nc.dram_tensor("bt", [1, D], F32, kind="ExternalInput")
    bqa_d = nc.dram_tensor("bqa", [1, H], F32, kind="ExternalInput")
    bka_d = nc.dram_tensor("bka", [1, H], F32, kind="ExternalInput")
    out_d = nc.dram_tensor("out", [S, D], F32, kind="ExternalOutput")

    xn_d = nc.dram_tensor("xn_scratch", [NB, 128, CH * T0], BF16, kind="Internal")
    xtm_d = nc.dram_tensor("xtm_scratch", [NT3, 128, D], BF16, kind="Internal")
    mq_d = nc.dram_tensor("mq_scratch", [NB, 128, CH * T0], BF16, kind="Internal")
    qs_d = nc.dram_tensor("qs_scratch", [H, S], BF16, kind="Internal")
    cnt_d = nc.dram_tensor("cnt_scratch", [NT3, 128, D], BF16, kind="Internal")

    I128_d = nc.inline_tensor(np.eye(128, dtype=ml_dtypes.bfloat16), "ident128")
    ONES_d = nc.inline_tensor(np.ones((16, 512), dtype=ml_dtypes.bfloat16), "ones16")
    mask_np = np.zeros((H, D), dtype=ml_dtypes.bfloat16)
    for f in range(D):
        mask_np[f // DH, f] = 1.0
    MASK_d = nc.inline_tensor(mask_np, "diagmask")

    with ExitStack() as ctx:
        e = ctx.enter_context
        AB1 = e(nc.sbuf_tensor("AB1", [128, 32768], BF16))
        AB2 = e(nc.sbuf_tensor("AB2", [128, 32768], BF16))
        AM = e(nc.sbuf_tensor("AM", [128, 12288], BF16))
        SCA = e(nc.sbuf_tensor("SCA", [128, 8192], BF16))
        XF = e(nc.sbuf_tensor("XF", [128, 3072], F32))
        SQ = e(nc.sbuf_tensor("SQ", [128, 1024], BF16))
        RB = e(nc.sbuf_tensor("RB", [128, 512], F32))
        YS = e(nc.sbuf_tensor("YS", [128, 2048], F32))
        OB = e(nc.sbuf_tensor("OB", [128, 2048], F32))
        LNS = e(nc.sbuf_tensor("LNS", [128, 16], F32))
        RS = e(nc.sbuf_tensor("RS", [16, 8], F32))
        PQ = e(nc.sbuf_tensor("PQ", [128, 8], F32))
        PK = e(nc.sbuf_tensor("PK", [128, 8], F32))
        CEPS = e(nc.sbuf_tensor("CEPS", [128, 1], F32))
        WA = e(nc.sbuf_tensor("WA", [128, 128], BF16))
        WB = e(nc.sbuf_tensor("WB", [128, 128], BF16))
        WKB = e(nc.sbuf_tensor("WKB", [128, 128], BF16))
        BIAS3 = e(nc.sbuf_tensor("BIAS3", [128, 1600], BF16))
        I128 = e(nc.sbuf_tensor("I128", [128, 128], BF16))
        ONES = e(nc.sbuf_tensor("ONES", [16, 512], BF16))
        MASK = e(nc.sbuf_tensor("MASK", [16, 1024], BF16))
        QSS = e(nc.sbuf_tensor("QSS", [16, 256], BF16))
        PS = [e(nc.psum_tensor(f"PS{i}", [128, 512], F32)) for i in range(8)]
        sems = {n: e(nc.semaphore(n)) for n in SEM_NAMES}
        GP_SEMS = ("pool", "lw0", "lw1", "lw4", "lw6", "so0", "so1",
                   "ln0", "ln2", "ln4", "ln6", "lmq0", "lmq1")
        SY_SEMS = ("lc", "lw2", "lw3", "lw5", "lw7", "l30", "l31", "l32",
                   "lb0", "lb1", "ltm0", "ltm1", "lqs",
                   "ln1", "ln3", "ln5", "ln7",
                   "cr0", "cr1")
        SC_SEMS = ("act", "sx0", "sx1", "tm0", "tm1", "mq0", "mq1", "sq",
                   "sc0", "sc1")
        for eng_obj, own in ((nc.tensor, ("pe",)), (nc.scalar, SC_SEMS),
                             (nc.vector, ("dve",)), (nc.gpsimd, GP_SEMS),
                             (nc.sync, SY_SEMS)):
            for s_ in own:
                eng_obj.sem_clear(sems[s_])
        nc.all_engine_barrier()
        block = e(nc.Block())

        pl = Plan()
        ONE5 = ONES[0:1, :]

        BQ = BIAS3[0:1, 0:1024]
        BK = BIAS3[32:33, 0:1024]
        BT = BIAS3[64:65, 0:1024]
        BQA = BIAS3[0:1, 1024:1040]
        BKA = BIAS3[32:33, 1024:1040]
        pl.dma("sync", "lc", lambda s: s.dma_start(I128[:], I128_d.ap()), writes=[("I128",)])
        pl.dma("sync", "lc", lambda s: s.dma_start(ONES[:], ONES_d.ap()), writes=[("ONES",)])
        pl.dma("sync", "lc", lambda s: s.dma_start(MASK[:], MASK_d.ap()), writes=[("MASK",)])
        bias_jobs = []
        for nm, dsrc, dst, w, stg, alias in (
                ("bq", bq_d, BQ, 1024, OB[0:1, 0:1024], ("stg", 0)),
                ("bk", bk_d, BK, 1024, OB[0:1, 1024:2048], ("stg", 1)),
                ("bt", bt_d, BT, 1024, YS[0:1, 0:1024], ("ys", 0)),
                ("bqa", bqa_d, BQA, 16, YS[0:1, 1024:1040], ("ys", 1)),
                ("bka", bka_d, BKA, 16, YS[0:1, 1056:1072], ("ys", 1))):
            pl.dma("sync", "lc", lambda s, dsrc=dsrc, stg=stg: s.dma_start(stg, dsrc.ap()),
                   writes=[("bstg", nm)])
            bias_jobs.append((nm, dst, w, stg, alias))
        lc_tot = ("sync", "lc", pl.cnt["lc"])
        for k in [("I128",), ("ONES",), ("MASK",)] + [("bstg", nm) for nm, *_ in bias_jobs]:
            pl.res[k]["w"] = lc_tot
        BIAS_FINE = {"bq": [("stgp", 0), ("stgp", 2)] + [("stgn", i) for i in range(4)],
                     "bk": [("stgp", 4), ("stgp", 6)] + [("stgn", i) for i in range(4, 8)],
                     "bt": [("ysp", 1), ("ysp", 3)], "bqa": [("ysp", 5)],
                     "bka": [("ysp", 5)]}
        for nm, dst, w, stg, alias in bias_jobs:
            pl.comp("vector", lambda v, dst=dst, stg=stg, w=w: v.tensor_copy(dst[:, 0:w], stg),
                    reads=[("bstg", nm)] + BIAS_FINE[nm], writes=[(nm,), alias])
        pl.comp("vector", lambda v: v.memset(BIAS3[:, 1088:1600], 1.0),
                writes=[("ones3",)])
        pl.comp("vector", lambda v: v.memset(CEPS[:], EPS), writes=[("ceps",)])

        ppc = [0]
        cast_pend = []
        CAST_SLOTS = [
            ("gpsimd", "lw0", OB[:, 0:512], ("stg", 0)),
            ("sync", "lw2", YS[:, 0:512], ("ys", 0)),
            ("sync", "lw5", OB[:, 512:1024], ("stg", 0)),
            ("gpsimd", "lw4", YS[:, 512:1024], ("ys", 0)),
            ("gpsimd", "lw1", OB[:, 1024:1536], ("stg", 1)),
            ("sync", "lw3", YS[:, 1024:1536], ("ys", 1)),
            ("sync", "lw7", OB[:, 1536:2048], ("stg", 1)),
            ("gpsimd", "lw6", YS[:, 1536:2048], ("ys", 1)),
        ]
        XFS = [XF[:, 2048:2560], XF[:, 2560:3072]]
        CAST_SLOTS_N = [
            ("gpsimd", "ln0", OB[:, 0:512], ("stg", 0), [("stgp", 0)]),
            ("sync", "ln1", OB[:, 512:1024], ("stg", 0), [("stgp", 2)]),
            ("gpsimd", "ln2", OB[:, 1024:1536], ("stg", 1), [("stgp", 4)]),
            ("sync", "ln3", OB[:, 1536:2048], ("stg", 1), [("stgp", 6)]),
            ("gpsimd", "ln4", XFS[0], ("xfs", 0), []),
            ("sync", "ln5", XFS[1], ("xfs", 1), []),
        ]
        cast_nslots = [4]

        def cast_piece(dsrc_ap, dst_ap, dst_keys):
            mode = cast_nslots[0]
            if mode == 8:
                sl = ppc[0] % 8
                queue, sem, stg_full, skey = CAST_SLOTS[sl]
                wkeys = [(skey[0] + "p", sl)]
            else:
                rot = (0, 1, 2, 3, 4, 5) if mode == 4 else (0, 2, 4)
                sl = rot[ppc[0] % len(rot)]
                queue, sem, stg_full, skey, okeys = CAST_SLOTS_N[sl]
                wkeys = [("stgn", sl)] + okeys
            stg = stg_full[:, 0:dsrc_ap.shape[1]]
            while len(cast_pend) >= min(4, 8 if mode == 8 else
                                        (6 if mode == 4 else 3)):
                cast_pend.pop(0)()
            pl.dma(queue, sem, lambda g, a=dsrc_ap, s=stg: g.dma_start(s, a),
                   reads=[], writes=wkeys)
            cast_pend.append(lambda s=stg, d=dst_ap, k=list(dst_keys),
                             rk=wkeys[0], sk=skey: pl.comp(
                "gpsimd", lambda eng, s=s, d=d: eng.tensor_copy(d, s),
                reads=[rk], writes=k + [sk]))
            ppc[0] += 1

        def cast(dsrc_ap, dst_ap, dst_keys):
            w = dsrc_ap.shape[1]
            if w > 512:
                for o in range(0, w, 512):
                    cast_piece(dsrc_ap[:, o:o + 512], dst_ap[:, o:o + 512], dst_keys)
            else:
                cast_piece(dsrc_ap, dst_ap, dst_keys)

        def cast_flush():
            while cast_pend:
                cast_pend.pop(0)()

        def a1(col, sz):
            return AB1[:, col: col + sz]

        def a1keys(col, sz):
            return [("A1", i) for i in range(col // 512, (col + sz + 511) // 512)]

        def a2keys(col, sz):
            return [("A2", i) for i in range(col // 512, (col + sz + 511) // 512)]

        def wsmkeys(col, sz):
            return [("WSM", i) for i in range(col // 1024, (col + sz + 1023) // 1024)]

        def pa_xload(gt):
            slot = XF[:, (gt % 2) * 1024:(gt % 2) * 1024 + 1024]
            pl.dma("sync", f"l3{gt % 2}",
                   lambda s, slot=slot, gt=gt: s.dma_start(
                       slot, x_d[gt * 128:(gt + 1) * 128, :]),
                   writes=[("x0", gt % 2)])

        def pa_ln_thunks(gt):
            th = []
            b, t = gt // 2, gt % 2
            xt = XF[:, (gt % 2) * 1024:(gt % 2) * 1024 + 1024]
            kx = ("x0", gt % 2)
            xtb = SCA[:, 4096 + (b % 2) * 2048: 4096 + (b % 2) * 2048 + 2048]
            kxtb = ("xtb", b % 2)
            par = t
            s1 = LNS[:, par * 8 + 0: par * 8 + 1]
            m_ = LNS[:, par * 8 + 1: par * 8 + 2]
            ssq = LNS[:, par * 8 + 2: par * 8 + 3]
            m2 = LNS[:, par * 8 + 3: par * 8 + 4]
            v_ = LNS[:, par * 8 + 4: par * 8 + 5]
            lv = LNS[:, par * 8 + 5: par * 8 + 6]
            r_ = LNS[:, par * 8 + 6: par * 8 + 7]
            kl = ("lns", par)
            th.append(lambda: pl.comp("vector", lambda v: v.reduce_sum(s1, xt, axis=AX.X),
                      reads=[kx], writes=[kl]))
            th.append(lambda: pl.comp("vector", lambda v: v.tensor_scalar(
                m_, s1, 1.0 / D, None, ALU.mult), reads=[kl], writes=[kl]))
            th.append(lambda: pl.comp("scalar", lambda sc: sc.activation(SQ[:], xt, AF.Square),
                      reads=[kx], writes=[("sq",)]))
            th.append(lambda: pl.comp("vector", lambda v: v.reduce_sum(ssq, SQ[:], axis=AX.X),
                      reads=[("sq",)], writes=[kl]))
            th.append(lambda: pl.comp("vector", lambda v: v.tensor_mul(m2, m_, m_),
                      reads=[kl], writes=[kl]))
            th.append(lambda: pl.comp("vector", lambda v: v.scalar_tensor_tensor(
                v_, ssq, 1.0 / D, m2, ALU.mult, ALU.subtract), reads=[kl], writes=[kl]))
            th.append(lambda: pl.comp("scalar", lambda sc: sc.activation(
                lv, v_, AF.Ln, bias=CEPS[:]), reads=[kl, ("ceps",)], writes=[kl]))
            th.append(lambda: pl.comp("scalar", lambda sc: sc.activation(
                r_, lv, AF.Exp, scale=-0.5), reads=[kl], writes=[kl]))
            xtt = xtb[:, t * 1024:(t + 1) * 1024]
            th.append(lambda: pl.comp("vector", lambda v: v.tensor_scalar(
                xtt, xt, m_, r_, ALU.subtract, ALU.mult),
                reads=[kx, kl], writes=[kxtb]))
            return th

        def pa_tr_thunks(b):
            th = []
            xnb = SCA[:, (b % 2) * 2048:(b % 2) * 2048 + 2048]
            xtb = SCA[:, 4096 + (b % 2) * 2048: 4096 + (b % 2) * 2048 + 2048]
            kxnb = ("xnb", b % 2)
            kxtb = ("xtb", b % 2)

            def mkpair(cp):
                def go():
                    bank = PS[6 + cp % 2]
                    kps = ("ps", 6 + cp % 2)
                    for ci in range(2):
                        c = 2 * cp + ci
                        for t in range(2):
                            lt = xtb[:, t * 1024 + c * 128: t * 1024 + (c + 1) * 128]
                            ot = bank[:, ci * 256 + t * 128: ci * 256 + (t + 1) * 128]
                            pl.comp("tensor", lambda te, ot=ot, lt=lt: te.matmul(
                                ot, lt, I128[:], start=True, stop=True),
                                reads=[kxtb, ("I128",)], writes=[kps])
                    dst = xnb[:, 2 * cp * 256: 2 * cp * 256 + 512]
                    pl.comp("vector", lambda v, dst=dst, cp=cp: v.tensor_copy(
                        dst, PS[6 + cp % 2][:]), reads=[("ps", 6 + cp % 2)],
                        writes=[kxnb])
                return go
            for cp in range(4):
                th.append(mkpair(cp))

            def stores():
                pl.dma("scalar", f"sx{b % 2}",
                       lambda g: g.dma_start(xn_d[b], xnb),
                       reads=[kxnb], writes=[("xnd", b)])
                for t in range(2):
                    src_ = xtb[:, t * 1024:(t + 1) * 1024]
                    gt = 2 * b + t
                    pl.dma("scalar", f"tm{gt % 2}",
                           lambda g, src_=src_, gt=gt: g.dma_start(xtm_d[gt], src_),
                           reads=[kxtb], writes=[("xtmd", gt)])
            th.append(stores)
            return th

        def pa_mq_group(b, eo):
            bank = PS[6 + eo % 2]
            kb = ("ps", 6 + eo % 2)
            xnb = SCA[:, (b % 2) * 2048:(b % 2) * 2048 + 2048]
            kxnb = ("xnb", b % 2)
            for d in range(8):
                lt = AM[:, d * 1024 + eo * 128: d * 1024 + (eo + 1) * 128]
                rh = xnb[:, d * 256:(d + 1) * 256]
                pl.comp("tensor", lambda te, bank=bank, lt=lt, rh=rh, d=d: te.matmul(
                    bank[:, 0:256], lt, rh, start=(d == 0), stop=False),
                    reads=wsmkeys(d * 1024 + eo * 128, 128) + [kxnb], writes=[kb])
            lt = BQ[:, eo * 128:(eo + 1) * 128]
            pl.comp("tensor", lambda te, bank=bank, lt=lt: te.matmul(
                bank[:, 0:256], lt, BIAS3[0:1, 1088:1344], start=False, stop=True),
                reads=[("bq",), ("ones3",)], writes=[kb])
            mqs = SCA[:, 4096 + (b % 2) * 2048 + eo * 256: 4096 + (b % 2) * 2048 + (eo + 1) * 256]
            pl.comp("vector", lambda v, mqs=mqs, bank=bank: v.tensor_copy(mqs, bank[:, 0:256]),
                    reads=[kb], writes=[("mqs", b % 2, eo), ("xtb", b % 2)])

        def pa_qs(b):
            xts = SCA[:, 4096 + (b % 2) * 2048: 4096 + (b % 2) * 2048 + 2048]
            for eo in range(CH):
                lt = WA[:, eo * 16:(eo + 1) * 16]
                rh = xts[:, eo * 256:(eo + 1) * 256]
                pl.comp("tensor", lambda te, lt=lt, rh=rh, eo=eo: te.matmul(
                    PS[6][0:16, 0:256], lt, rh, start=(eo == 0), stop=False),
                    reads=[("WA", eo), ("mqs", b % 2, eo), ("xtb", b % 2)],
                    writes=[("ps", 6)])
            pl.comp("tensor", lambda te: te.matmul(
                PS[6][0:16, 0:256], BQA, BIAS3[0:1, 1088:1344], start=False, stop=True),
                reads=[("bqa",), ("ones3",)], writes=[("ps", 6)])
            pl.comp("scalar", lambda sc: sc.activation(
                QSS[:], PS[6][0:16, 0:256], AF.Copy, scale=SCALE),
                reads=[("ps", 6)], writes=[("qss",)])
            pl.dma("scalar", "sq",
                   lambda g, b=b: g.dma_start(qs_d[:, b * 256:(b + 1) * 256], QSS[:]),
                   reads=[("qss",)], writes=[("qsd", b)])
            pl.dma("scalar", f"mq{b % 2}",
                   lambda g, b=b: g.dma_start(mq_d[b], xts),
                   reads=[("mqs", b % 2, eo_) for eo_ in range(CH)] + [("xtb", b % 2)],
                   writes=[("mqd", b)])

        pa_xload(0)
        pa_xload(1)
        for t_ in pa_ln_thunks(0):
            t_()
        pa_xload(2)
        for t_ in pa_ln_thunks(1):
            t_()
        pa_xload(3)
        for t_ in pa_tr_thunks(0):
            t_()

        def cast_startup_weights():
            for q in range(4):
                for c in range(CH):
                    cast(WfcT_d[c * 128:(c + 1) * 128, q * 1024:(q + 1) * 1024],
                         AB1[:, c * 4096 + q * 1024: c * 4096 + (q + 1) * 1024],
                         a1keys(c * 4096 + q * 1024, 1024))
                for j in range(q * 8, q * 8 + 8):
                    cast(WprojT_d[j * 128:(j + 1) * 128, :],
                         AB2[:, j * 1024:(j + 1) * 1024], a2keys(j * 1024, 1024))
                if q == 0:
                    cast_nslots[0] = 4
            cast_flush()

        cast_nslots[0] = 8
        for d in range(8):
            cast(Wq_d[d * 128:(d + 1) * 128, :], AM[:, d * 1024:(d + 1) * 1024],
                 [("WSM", d)])
        for wsrc, dst, nm in ((Wqa_d, WA, "WA"), (Wka_d, WB, "WB")):
            for d in range(8):
                cast(wsrc[d * 128:(d + 1) * 128, :], dst[:, d * 16:(d + 1) * 16],
                     [(nm, d)])
        cast_flush()
        cast_startup_weights()

        for eo in range(CH):
            pa_mq_group(0, eo)
        pa_qs(0)

        H2_COLS = (8192, 8448, 11776, 12032)

        def p0_xnload(b, extra=()):
            xnb = SCA[:, (b % 2) * 2048:(b % 2) * 2048 + 2048]
            pl.dma("sync", f"lb{b % 2}",
                   lambda s, b=b: s.dma_start(xnb, xn_d[b]),
                   reads=[("xnd", b)], writes=[("xnb", b % 2)] + list(extra))

        def p0_hmm(b, j):
            hb = PS[4 + j % 2]
            khb = ("ps", 4 + j % 2)
            kxnb = ("xnb", b % 2)
            xnb = SCA[:, (b % 2) * 2048:(b % 2) * 2048 + 2048]
            for c in range(CH):
                lt = AB1[:, c * 4096 + j * 128: c * 4096 + (j + 1) * 128]
                rh = xnb[:, c * 256:(c + 1) * 256]
                pl.comp("tensor", lambda te, hb=hb, lt=lt, rh=rh, c=c: te.matmul(
                    hb[:, 0:256], lt, rh, start=(c == 0), stop=(c == 7)),
                    reads=a1keys(c * 4096 + j * 128, 128) + [kxnb], writes=[khb])
            rr = RB[:, (j % 2) * 256:(j % 2) * 256 + 256]
            krr = ("rb", j % 2)
            pl.comp("scalar", lambda sc, rr=rr, hb=hb: sc.activation(
                rr, hb[:, 0:256], AF.Relu), reads=[khb], writes=[krr])
            g3 = (b * 32 + j) % 4
            h2 = AM[:, H2_COLS[g3]: H2_COLS[g3] + 256]
            pl.comp("vector", lambda v, h2=h2, rr=rr: v.tensor_mul(h2, rr, rr),
                    reads=[krr], writes=[("h2", g3)])

        def p0_ymm(b, j):
            g3 = (b * 32 + j) % 4
            h2 = AM[:, H2_COLS[g3]: H2_COLS[g3] + 256]
            kh2 = ("h2", g3)
            for t in range(2):
                lt = h2[:, t * 128:(t + 1) * 128]
                for nh in range(2):
                    yb = PS[2 * t + nh]
                    rh = AB2[:, j * 1024 + nh * 512: j * 1024 + (nh + 1) * 512]
                    pl.comp("tensor", lambda te, yb=yb, lt=lt, rh=rh, j=j: te.matmul(
                        yb[:], lt, rh, start=(j == 0), stop=(j == 31)),
                        reads=[kh2] + a2keys(j * 1024 + nh * 512, 512),
                        writes=[("ps", 2 * t + nh)])

        def p0_yevac(b):
            for t in range(2):
                yt = YS[:, t * 1024:(t + 1) * 1024]
                for nh in range(2):
                    pl.comp("scalar", lambda sc, yt=yt, t=t, nh=nh: sc.copy(
                        yt[:, nh * 512:(nh + 1) * 512], PS[2 * t + nh][:]),
                        reads=[("ps", 2 * t + nh)], writes=[("ys", t)])

        def p0_spike_thunks(b):
            th = []
            for t in range(2):
                yt = YS[:, t * 1024:(t + 1) * 1024]
                kyt = ("ys", t)
                tile = 2 * b + t
                cslot = AM[:, 8704 + (tile % 2) * 1024: 8704 + (tile % 2) * 1024 + 1024]
                kc = ("cnt", tile % 2)
                spk = AM[:, 10752:11776]
                ks_ = ("spk",)
                th.append(lambda yt=yt, cslot=cslot, kyt=kyt, kc=kc: pl.comp(
                    "vector", lambda v, cslot=cslot, yt=yt: v.tensor_scalar(
                        cslot, yt, TAUS[0], None, ALU.is_gt), reads=[kyt], writes=[kc]))
                for i in range(4):
                    o = spk if i % 2 == 0 else cslot
                    inn = cslot if i % 2 == 0 else spk
                    ki = kc if i % 2 == 0 else ks_
                    ko = ks_ if i % 2 == 0 else kc
                    th.append(lambda yt=yt, o=o, inn=inn, kyt=kyt, ki=ki, ko=ko,
                              tau=TAUS[i + 1]: pl.comp(
                        "vector", lambda v, o=o, yt=yt, inn=inn, tau=tau:
                        v.scalar_tensor_tensor(o, yt, tau, inn, ALU.is_gt, ALU.add),
                        reads=[kyt, ki], writes=[ko]))
                th.append(lambda cslot=cslot, tile=tile, kc=kc: pl.dma(
                    "scalar", f"sc{tile % 2}",
                    lambda g, cslot=cslot, tile=tile: g.dma_start(cnt_d[tile], cslot),
                    reads=[kc], writes=[("cntd", tile)]))
            return th

        def prep_thunks(m):
            th = []
            th += pa_ln_thunks(2 * m)
            if 2 * m + 2 < 2 * NB:
                th.append(lambda m=m: pa_xload(2 * m + 2))
            th += pa_ln_thunks(2 * m + 1)
            if 2 * m + 3 < 2 * NB:
                th.append(lambda m=m: pa_xload(2 * m + 3))
            th += pa_tr_thunks(m)
            th += [lambda m=m, eo=eo_: pa_mq_group(m, eo) for eo_ in range(CH)]
            th.append(lambda m=m: pa_qs(m))
            return th

        pa_q = []
        pending = []
        seq = [(b, j) for b in range(NB) for j in range(32)]
        for gi, (b, j) in enumerate(seq):
            if j == 0 and b + 1 < NB:
                pa_q.extend(prep_thunks(b + 1))
            p0_hmm(b, j)
            if pending and j % 2 == 1 and 2 < j < 29:
                pending.pop(0)()
            if pa_q:
                pa_q.pop(0)()
            if pa_q and j % 2 == 0 and j < 14:
                pa_q.pop(0)()
            if gi > 2:
                pb, pj = seq[gi - 3]
                p0_ymm(pb, pj)
                if pj == 31:
                    p0_yevac(pb)
                    pending = p0_spike_thunks(pb)
        while pa_q:
            pa_q.pop(0)()
        for (pb, pj) in seq[-3:]:
            p0_ymm(pb, pj)
            if pj == 31:
                p0_yevac(pb)
                pending = p0_spike_thunks(pb)
        for th in pending:
            th()
        pl.patch_total([("xnd", b) for b in range(0, NB, 2)], "scalar", "sx0")
        pl.patch_total([("xnd", b) for b in range(1, NB, 2)], "scalar", "sx1")
        pl.patch_total([("xtmd", g) for g in range(0, NT3, 2)], "scalar", "tm0")
        pl.patch_total([("xtmd", g) for g in range(1, NT3, 2)], "scalar", "tm1")
        pl.patch_total([("mqd", b) for b in range(0, NB, 2)], "scalar", "mq0")
        pl.patch_total([("mqd", b) for b in range(1, NB, 2)], "scalar", "mq1")
        pl.patch_total([("cntd", t) for t in range(0, NT3, 2)], "scalar", "sc0")
        pl.patch_total([("cntd", t) for t in range(1, NT3, 2)], "scalar", "sc1")

        SC1 = SCA[0:16, 0:4096]
        SC2 = SCA[0:16, 4096:8192]
        pl.dma("sync", "lqs", lambda s: s.dma_start(SC1, qs_d.ap()),
               reads=[("qsd", b) for b in range(NB)],
               writes=[("SC1",), ("xnb", 0), ("xnb", 1)])
        cast_nslots[0] = 2
        for d in range(8):
            cast(WkT_d[d * 128:(d + 1) * 128, :], AB2[:, d * 1024:(d + 1) * 1024],
                 a2keys(d * 1024, 1024))
        cast_flush()
        cast_nslots[0] = 4
        for g in range(NT3):
            dst = AB1[:, g * 1024:(g + 1) * 1024]
            pl.dma("sync", f"ltm{g % 2}",
                   lambda s, dst=dst, g=g: s.dma_start(dst, xtm_d[g]),
                   reads=[("xtmd", g)], writes=a1keys(g * 1024, 1024))
        for i in range(12):
            dst = AB2[:, 8192 + i * 2048: 8192 + (i + 1) * 2048]
            pl.dma("sync", f"lb{i % 2}",
                   lambda s, dst=dst, i=i: s.dma_start(dst, xn_d[i]),
                   reads=[("xnd", i)], writes=a2keys(8192 + i * 2048, 2048))

        pl.barrier()

        ENT = AM[:, 8192:8704]
        PBF = AM[0:16, 8704:9728]
        PTT = AM[:, 9728:9856]

        def pool_phase(tag, sc_src, sc_srckeys, sc_tmp, sc_tmpkeys, wrhs_ap,
                       wrhs_col, wrhs_keys):
            rmax, nmax, rsum, rrec = RS[:, 0:1], RS[:, 1:2], RS[:, 2:3], RS[:, 3:4]
            pl.comp("vector", lambda v: v.reduce_max(rmax, sc_src, axis=AX.X),
                    reads=sc_srckeys, writes=[("rs",)])
            pl.comp("vector", lambda v: v.tensor_scalar(nmax, rmax, -1.0, None, ALU.mult),
                    reads=[("rs",)], writes=[("rs",)])
            pl.comp("scalar", lambda sc: sc.activation(sc_tmp, sc_src, AF.Exp, bias=nmax),
                    reads=sc_srckeys + [("rs",)], writes=sc_tmpkeys)
            pl.comp("vector", lambda v: v.reduce_sum(rsum, sc_tmp, axis=AX.X),
                    reads=sc_tmpkeys, writes=[("rs",)])
            pl.comp("vector", lambda v: v.reciprocal(rrec, rsum),
                    reads=[("rs",)], writes=[("rs",)])
            pl.comp("vector", lambda v: v.tensor_scalar(sc_tmp, sc_tmp, rrec, None, ALU.mult),
                    reads=[("rs",)] + sc_tmpkeys, writes=sc_tmpkeys)
            for i in range(NT3):
                lt = sc_tmp[:, i * 128:(i + 1) * 128]
                ot = PS[6][:, i * 16:(i + 1) * 16]
                pl.comp("tensor", lambda te, ot=ot, lt=lt: te.matmul(
                    ot, lt, I128[0:16, 0:16], start=True, stop=True),
                    reads=sc_tmpkeys + [("I128",)], writes=[("ps", 6)])
            pl.comp("vector", lambda v: v.tensor_copy(ENT[:], PS[6][:]),
                    reads=[("ps", 6)], writes=[("ent",)])
            for i in range(NT3):
                lt = ENT[:, i * 16:(i + 1) * 16]
                for nh in range(2):
                    rh = AB1[:, i * 1024 + nh * 512: i * 1024 + (nh + 1) * 512]
                    pl.comp("tensor", lambda te, lt=lt, rh=rh, i=i, nh=nh: te.matmul(
                        PS[2 + nh][0:16, :], lt, rh,
                        start=(i == 0), stop=(i == NT3 - 1)),
                        reads=[("ent",)] + a1keys(i * 1024 + nh * 512, 512),
                        writes=[("ps", 2 + nh)])
            for nh in range(2):
                pl.comp("vector", lambda v, nh=nh: v.tensor_copy(
                    PBF[:, nh * 512:(nh + 1) * 512], PS[2 + nh][0:16, :]),
                    reads=[("ps", 2 + nh)], writes=[("pbf",)])
            for c in range(CH):
                lt = PBF[:, c * 128:(c + 1) * 128]
                ot = PS[7][:, c * 16:(c + 1) * 16]
                pl.comp("tensor", lambda te, ot=ot, lt=lt: te.matmul(
                    ot, lt, I128[0:16, 0:16], start=True, stop=True),
                    reads=[("pbf",), ("I128",)], writes=[("ps", 7)])
            pl.comp("vector", lambda v: v.tensor_copy(PTT[:], PS[7][:, 0:128]),
                    reads=[("ps", 7)], writes=[("ptt",)])
            for dc in range(CH):
                lt = PTT[:, dc * 16:(dc + 1) * 16]
                for nh in range(2):
                    rh = wrhs_ap[:, wrhs_col + dc * 1024 + nh * 512: wrhs_col + dc * 1024 + nh * 512 + 512]
                    pl.comp("tensor", lambda te, lt=lt, rh=rh, dc=dc, nh=nh: te.matmul(
                        PS[2 + nh][0:16, :], lt, rh, start=(dc == 0), stop=(dc == 7)),
                        reads=[("ptt",)] + wrhs_keys, writes=[("ps", 2 + nh)])
            for nh in range(2):
                pl.comp("vector", lambda v, nh=nh: v.tensor_tensor(
                    SQ[0:16, nh * 512:(nh + 1) * 512], PS[2 + nh][0:16, :],
                    MASK[:, nh * 512:(nh + 1) * 512], ALU.mult),
                    reads=[("ps", 2 + nh), ("MASK",)], writes=[("mskd",)])
            pl.comp("tensor", lambda te: te.matmul(
                PS[5][0:1, 0:512], ONES[:, 0:1], SQ[0:16, 0:512],
                start=True, stop=True),
                reads=[("mskd",), ("ONES",)], writes=[("ps", 5)])
            pl.comp("tensor", lambda te: te.matmul(
                PS[4][0:1, 0:512], ONES[:, 0:1], SQ[0:16, 512:1024],
                start=True, stop=True),
                reads=[("mskd",), ("ONES",)], writes=[("ps", 4)])

        def row_to_fm(srcrow_bf, srckeys, dst, dstname):
            for c in range(CH):
                lt = srcrow_bf[:, c * 128:(c + 1) * 128]
                ot = PS[7][:, c:c + 1]
                pl.comp("tensor", lambda te, ot=ot, lt=lt: te.matmul(
                    ot, lt, BIAS3[0:1, 1088:1089], start=True, stop=True),
                    reads=srckeys + [("ones3",)], writes=[("ps", 7)])
            pl.comp("vector", lambda v: v.tensor_copy(dst[:], PS[7][:, 0:8]),
                    reads=[("ps", 7)], writes=[(dstname, c_) for c_ in range(CH)])

        pool_phase("q", SC1, [("SC1",)], SC2, [("SC2",), ("xtb", 0), ("xtb", 1)],
                   AM, 0, [("WSM", d) for d in range(8)])
        PQROW = YS[0:1, 1024:2048]
        pl.comp("vector", lambda v: v.tensor_tensor(
            PQROW[:, 0:512], PS[5][0:1, 0:512], BQ[:, 0:512], ALU.add),
            reads=[("ps", 5), ("bq",)], writes=[("pqrow",)])
        pl.comp("vector", lambda v: v.tensor_tensor(
            PQROW[:, 512:1024], PS[4][0:1, 0:512], BQ[:, 512:1024], ALU.add),
            reads=[("ps", 4), ("bq",)], writes=[("pqrow",)])
        PQROWB = SQ[0:1, 0:1024]
        pl.comp("vector", lambda v: v.tensor_copy(PQROWB, PQROW),
                reads=[("pqrow",)], writes=[("pqrowb",), ("mskd",)])
        row_to_fm(PQROWB, [("pqrowb",)], PQ, "pq")

        for d in range(8):
            cast(Wt_d[d * 128:(d + 1) * 128, :], AM[:, d * 1024:(d + 1) * 1024],
                 [("WSM", d)])
        cast_flush()

        def ld_x3(t):
            slot = XF[:, (t % 2) * 1024:(t % 2 + 1) * 1024]
            pl.dma("sync", f"l3{t % 2}",
                   lambda s, slot=slot, t=t: s.dma_start(
                       slot, x_d[t * 128:(t + 1) * 128, :]),
                   writes=[("xf3", t % 2)])

        def ld_cnt3(t):
            slot = AM[:, 10240 + (t % 2) * 1024: 10240 + (t % 2) * 1024 + 1024]
            pl.dma("sync", f"cr{t % 2}",
                   lambda s, slot=slot, t=t: s.dma_start(slot, cnt_d[t]),
                   reads=[("cntd", t)], writes=[("cntr", t % 2)])

        for c in range(CH):
            wb = WB[:, c * 16:(c + 1) * 16]
            pl.comp("vector", lambda v, wb=wb, c=c: v.tensor_scalar(
                wb, wb, PQ[:, c:c + 1], None, ALU.mult),
                reads=[("pq", c), ("WB", c)], writes=[("WB", c)])
        for dc in range(CH):
            for fc in range(CH):
                lt = AB2[:, fc * 1024 + dc * 128: fc * 1024 + (dc + 1) * 128]
                rh = WB[:, fc * 16:(fc + 1) * 16]
                pl.comp("tensor", lambda te, lt=lt, rh=rh, dc=dc, fc=fc: te.matmul(
                    PS[4][:, dc * 16:(dc + 1) * 16], lt, rh,
                    start=(fc == 0), stop=(fc == 7)),
                    reads=a2keys(fc * 1024 + dc * 128, 128) + [("WB", fc)],
                    writes=[("ps", 4)])
        pl.comp("vector", lambda v: v.tensor_copy(WKB[:], PS[4][:, 0:128]),
                reads=[("ps", 4)], writes=[("WKB",)])
        BKFB = SQ[:, 16:24]
        for c in range(CH):
            lt = BK[:, c * 128:(c + 1) * 128]
            ot = PS[5][:, c:c + 1]
            pl.comp("tensor", lambda te, ot=ot, lt=lt: te.matmul(
                ot, lt, BIAS3[32:33, 1088:1089], start=True, stop=True),
                reads=[("bk",), ("ones3",)], writes=[("ps", 5)])
        pl.comp("vector", lambda v: v.tensor_copy(BKFB, PS[5][:, 0:8]),
                reads=[("ps", 5)], writes=[("bkfb",), ("pqrowb",)])
        for c in range(CH):
            pl.comp("tensor", lambda te, c=c: te.matmul(
                PS[5][0:16, 0:1], WB[:, c * 16:(c + 1) * 16], BKFB[:, c:c + 1],
                start=(c == 0), stop=False),
                reads=[("WB", c), ("bkfb",)], writes=[("ps", 5)])
        pl.comp("tensor", lambda te: te.matmul(
            PS[5][0:16, 0:1], BKA, BIAS3[32:33, 1088:1089], start=False, stop=True),
            reads=[("bka",), ("ones3",)], writes=[("ps", 5)])
        BIASK = RS[:, 4:5]
        pl.comp("vector", lambda v: v.tensor_scalar(
            BIASK, PS[5][0:16, 0:1], SCALE, None, ALU.mult),
            reads=[("ps", 5)], writes=[("rs",)])
        p0_xnload(12, extra=[("SC1",)])
        p0_xnload(13, extra=[("SC1",)])
        for b in range(NB):
            if b < 12:
                base = 8192 + b * 2048
                xnb = AB2[:, base: base + 2048]
                kxn = lambda dc, base=base: a2keys(base + dc * 256, 256)
            else:
                xnb = SCA[:, (b % 2) * 2048:(b % 2) * 2048 + 2048]
                kxn = lambda dc, b=b: [("xnb", b % 2)]
            for dc in range(CH):
                lt = WKB[:, dc * 16:(dc + 1) * 16]
                rh = xnb[:, dc * 256:(dc + 1) * 256]
                pl.comp("tensor", lambda te, lt=lt, rh=rh, dc=dc: te.matmul(
                    PS[4][0:16, 0:256], lt, rh, start=(dc == 0), stop=(dc == 7)),
                    reads=[("WKB",)] + kxn(dc), writes=[("ps", 4)])
            pl.comp("vector", lambda v, b=b: v.tensor_scalar(
                SC2[:, b * 256:(b + 1) * 256], PS[4][0:16, 0:256],
                SCALE, BIASK, ALU.mult, ALU.add),
                reads=[("ps", 4), ("rs",)], writes=[("SC2k", b)])
            if b in (12, 13):
                p0_xnload(b + 2, extra=[("SC1",)])
        for bb in range(8):
            base = 8192 + bb * 2048
            pl.dma("gpsimd", f"lmq{bb % 2}",
                   lambda g, bb=bb, base=base: g.dma_start(
                       AB2[:, base: base + 2048], mq_d[bb]),
                   reads=[("mqd", bb)], writes=a2keys(base, 2048))
        for t in range(2):
            ld_x3(t)
            ld_cnt3(t)

        for d in range(8):
            cast(Wk_d[d * 128:(d + 1) * 128, :],
                 AB2[:, 24576 + d * 1024: 24576 + (d + 1) * 1024],
                 a2keys(24576 + d * 1024, 1024))
        cast_flush()

        pool_phase("k", SC2, [("SC2k", b) for b in range(NB)], SC1,
                   [("SC1e",), ("SC1",), ("xnb", 0), ("xnb", 1)], AB2, 24576,
                   a2keys(24576, 8192))
        PKROW = YS[0:1, 0:1024]
        pl.comp("vector", lambda v: v.tensor_tensor(
            PKROW[:, 0:512], PS[5][0:1, 0:512], BK[:, 0:512], ALU.add),
            reads=[("ps", 5), ("bk",)], writes=[("pkrow",)])
        pl.comp("vector", lambda v: v.tensor_tensor(
            PKROW[:, 512:1024], PS[4][0:1, 0:512], BK[:, 512:1024], ALU.add),
            reads=[("ps", 4), ("bk",)], writes=[("pkrow",)])
        pl.comp("vector", lambda v: v.tensor_mul(PKROW, PKROW, PQROW),
                reads=[("pkrow",), ("pqrow",)], writes=[("pkrow",)])
        PKROWB = SQ[0:1, 0:1024]
        pl.comp("vector", lambda v: v.tensor_copy(PKROWB, PKROW),
                reads=[("pkrow",)], writes=[("pkrowb",), ("mskd",), ("bkfrow",)])
        row_to_fm(PKROWB, [("pkrowb",)], PK, "pk")
        MQ_SLOT_KEYS = {12: [("xnb", 0), ("SC1e",), ("SC1",)],
                        13: [("xnb", 1), ("SC1e",), ("SC1",)],
                        14: [("xtb", 0)] + [("SC2k", b_) for b_ in range(8)],
                        15: [("xtb", 1)] + [("SC2k", b_) for b_ in range(8, 16)]}
        for bb in range(8, NB):
            if bb < 12:
                base = 8192 + bb * 2048
                dst, wkeys = AB2[:, base: base + 2048], a2keys(base, 2048)
            else:
                dst = SCA[:, (bb - 12) * 2048:(bb - 11) * 2048]
                wkeys = MQ_SLOT_KEYS[bb]
            pl.dma("gpsimd", f"lmq{bb % 2}",
                   lambda g, bb=bb, dst=dst: g.dma_start(dst, mq_d[bb]),
                   reads=[("mqd", bb)], writes=wkeys)
        for d in range(8):
            cast(Wt_d[d * 128:(d + 1) * 128, :], AM[:, d * 1024:(d + 1) * 1024],
                 [("WSM", d)])
        cast_flush()
        for f in range(8):
            wt = AM[:, f * 1024:(f + 1) * 1024]
            pl.comp("vector", lambda v, wt=wt, f=f: v.tensor_scalar(
                wt, wt, PK[:, f:f + 1], None, ALU.mult),
                reads=[("pk", f), ("WSM", f)], writes=[("WSM", f)])

        pl.barrier()

        def mq_ap(bb):
            if bb < 12:
                return AB2[:, 8192 + bb * 2048: 8192 + (bb + 1) * 2048]
            return SCA[:, (bb - 12) * 2048:(bb - 11) * 2048]

        def mq_keys(bb):
            if bb < 12:
                return a2keys(8192 + bb * 2048, 2048)
            return [MQ_SLOT_KEYS[bb][0]]

        for t in range(NT3):
            bb = t // 2
            kmqb = mq_keys(bb)[0]

            def mqsl(c, t=t, bb=bb):
                return mq_ap(bb)[:, c * 256 + (t % 2) * 128: c * 256 + (t % 2) * 128 + 128]
            pset = 0 if t % 2 == 0 else 2
            b0, b1 = PS[pset], PS[pset + 1]
            kb0, kb1 = ("ps", pset), ("ps", pset + 1)
            for c in range(CH):
                lt = mqsl(c)
                for nh in range(2):
                    bank, kb = (b0, kb0) if nh == 0 else (b1, kb1)
                    rh = AM[:, c * 1024 + nh * 512: c * 1024 + nh * 512 + 512]
                    pl.comp("tensor", lambda te, bank=bank, lt=lt, rh=rh, c=c: te.matmul(
                        bank[:], lt, rh, start=(c == 0), stop=False),
                        reads=[kmqb, ("WSM", c)], writes=[kb])
            for c in range(CH):
                bank, kb = (b0, kb0) if c < 4 else (b1, kb1)
                reg = bank[:, (c % 4) * 128:(c % 4) * 128 + 128]
                lt = mqsl(c)
                pl.comp("tensor", lambda te, reg=reg, lt=lt: te.matmul(
                    reg, lt, I128[:], start=False, stop=False),
                    reads=[kmqb, ("I128",)], writes=[kb])
            for nh in range(2):
                bank, kb = (b0, kb0) if nh == 0 else (b1, kb1)
                rh = BT[:, nh * 512:(nh + 1) * 512]
                pl.comp("tensor", lambda te, bank=bank, rh=rh: te.matmul(
                    bank[:], BIAS3[64:65, 1088:1216], rh, start=False, stop=True),
                    reads=[("bt",), ("ones3",)], writes=[kb])
            cslot = AM[:, 10240 + (t % 2) * 1024: 10240 + (t % 2) * 1024 + 1024]
            gbuf = YS[:, (t % 2) * 1024:(t % 2) * 1024 + 1024]
            galias = ("pkrow",) if t % 2 == 0 else ("pqrow",)
            pl.comp("scalar", lambda sc, cslot=cslot, gbuf=gbuf: sc.activation(
                gbuf, cslot, AF.Silu, scale=0.2),
                reads=[("cntr", t % 2)], writes=[("g3", t % 2), galias])
            ot = OB[:, (t % 2) * 1024:(t % 2 + 1) * 1024]
            for nh in range(2):
                bank, kb = (b0, kb0) if nh == 0 else (b1, kb1)
                pl.comp("vector", lambda v, ot=ot, bank=bank, gbuf=gbuf, nh=nh:
                        v.tensor_mul(ot[:, nh * 512:(nh + 1) * 512], bank[:],
                                     gbuf[:, nh * 512:(nh + 1) * 512]),
                        reads=[kb, ("g3", t % 2)],
                        writes=[("ob3", t % 2), ("stg", t % 2)])
            xt = XF[:, (t % 2) * 1024:(t % 2 + 1) * 1024]
            pl.comp("vector", lambda v, ot=ot, xt=xt: v.tensor_add(ot, ot, xt),
                    reads=[("xf3", t % 2), ("ob3", t % 2)], writes=[("ob3", t % 2)])
            pl.dma("gpsimd", f"so{t % 2}",
                   lambda g, ot=ot, t=t: g.dma_start(out_d[t * 128:(t + 1) * 128, :], ot),
                   reads=[("ob3", t % 2)], writes=[("outd", t)])
            if t + 2 < NT3:
                ld_x3(t + 2)
                ld_cnt3(t + 2)

        pl.wait_all("sync", list(SY_SEMS))
        pl.wait_all("gpsimd", list(GP_SEMS))
        pl.wait_all("scalar", [s_ for s_ in SC_SEMS if s_ != "act"])

        def replay(eng_name):
            def body(eng):
                for waits, fn, sem, n in pl.ops[eng_name]:
                    for s_, v_ in waits:
                        eng.wait_ge(sems[s_], v_)
                    if fn is not None:
                        inst = fn(eng)
                        inst.then_inc(sems[sem], n)
            return body

        block.sync(replay("sync"))
        block.tensor(replay("tensor"))
        block.scalar(replay("scalar"))
        block.vector(replay("vector"))
        block.gpsimd(replay("gpsimd"))

    return nc


_NC_CACHE = None


def kernel(**inputs):
    global _NC_CACHE
    x = np.ascontiguousarray(inputs["L0_attn_hidden_states"], dtype=np.float32)
    Wk = np.ascontiguousarray(inputs["L0_attn_Wk"], dtype=np.float32)
    wmap = {
        "Wq": np.ascontiguousarray(inputs["L0_attn_Wq"], dtype=np.float32),
        "Wk": Wk,
        "WkT": np.ascontiguousarray(Wk.T),
        "Wt": np.ascontiguousarray(inputs["L0_attn_Wt"], dtype=np.float32),
        "Wqa": np.ascontiguousarray(inputs["L0_attn_Wqa"], dtype=np.float32),
        "Wka": np.ascontiguousarray(inputs["L0_attn_Wka"], dtype=np.float32),
        "WfcT": np.ascontiguousarray(np.asarray(inputs["L0_ffn_W_fc"], dtype=np.float32).T),
        "WprojT": np.ascontiguousarray(np.asarray(inputs["L0_ffn_W_proj"], dtype=np.float32).T),
        "bq": np.ascontiguousarray(np.asarray(inputs["L0_attn_bq"], dtype=np.float32).reshape(1, D)),
        "bk": np.ascontiguousarray(np.asarray(inputs["L0_attn_bk"], dtype=np.float32).reshape(1, D)),
        "bt": np.ascontiguousarray(np.asarray(inputs["L0_attn_bt"], dtype=np.float32).reshape(1, D)),
        "bqa": np.ascontiguousarray(np.asarray(inputs["L0_attn_bqa"], dtype=np.float32).reshape(1, H)),
        "bka": np.ascontiguousarray(np.asarray(inputs["L0_attn_bka"], dtype=np.float32).reshape(1, H)),
    }
    if _NC_CACHE is None:
        _NC_CACHE = build_nc()
    nc = _NC_CACHE
    in_maps = [dict(wmap, x=np.ascontiguousarray(x[b])) for b in range(8)]
    res = run_bass_kernel_spmd(nc, in_maps, core_ids=list(range(8)))
    return np.stack([np.asarray(r["out"], dtype=np.float32) for r in res.results])


# version 24
# speedup vs baseline: 1.0323x; 1.0323x over previous
import numpy as np
import ml_dtypes
from contextlib import ExitStack

import concourse.bass as bass
import concourse.mybir as mybir
from concourse.bass_utils import run_bass_kernel_spmd

F32 = mybir.dt.float32
BF16 = mybir.dt.bfloat16
AF = mybir.ActivationFunctionType
ALU = mybir.AluOpType
AX = mybir.AxisListType

S, D, H, DH, CH = 4096, 1024, 16, 64, 8
SCALE = 1.0 / 8.0
EPS = 1e-5

_a = 0.0
for _ in range(5):
    _a = _a * 0.9 + 1.0
TAUS = [c / _a for c in (1.0, 1.9, 2.81, 3.71, 4.439)]

NB, T0 = 16, 256
NT3 = 32

ENG_SEM = {"tensor": "pe", "scalar": "act", "vector": "dve", "gpsimd": "pool"}
SEM_NAMES = ("pe", "act", "dve", "pool", "lc",
             "lw0", "lw1", "lw2", "lw3", "lw4", "lw5", "lw6", "lw7",
             "ln0", "ln1", "ln2", "ln3", "ln4", "ln5", "ln6", "ln7",
             "l30", "l31", "l32",
             "lb0", "lb1",
             "ltm0", "ltm1",
             "lmq0", "lmq1",
             "lqs",
             "cr0", "cr1",
             "sx0", "sx1",
             "tm0", "tm1",
             "mq0", "mq1",
             "sq",
             "sc0", "sc1",
             "so0", "so1")
DMA_SEMS = set(SEM_NAMES[4:])


class Plan:
    def __init__(self):
        self.ops = {k: [] for k in ("tensor", "scalar", "vector", "gpsimd", "sync")}
        self.cnt = {k: 0 for k in SEM_NAMES}
        self.res = {}
        self.hiwater = {k: {} for k in self.ops}

    def _need_wait(self, eng, sem, val):
        hw = self.hiwater[eng]
        if hw.get(sem, 0) >= val:
            return False
        hw[sem] = val
        return True

    STRICT = {"lns", "rs", "pq", "pk", "ceps", "ones3",
              "bq", "bk", "bt", "bqa", "bka", "bstg"}

    def _op(self, eng, fn, reads, writes, sem, n):
        waits = {}
        my_val = self.cnt[sem] + n

        def add(dep_eng, dep_sem, dep_val, strict):
            if dep_eng == eng and dep_sem == "pe" and sem == "pe":
                return
            if dep_eng == eng and dep_sem in DMA_SEMS and sem == dep_sem:
                return
            waits[dep_sem] = max(waits.get(dep_sem, 0), dep_val)

        for k in reads:
            st = self.res.get(k)
            if st and st["w"]:
                add(*st["w"], k[0] in self.STRICT)
        for k in writes:
            st = self.res.get(k)
            if st:
                strict = k[0] in self.STRICT
                if st["w"]:
                    add(*st["w"], strict)
                for r in st["r"]:
                    add(*r, strict)
        for k in reads:
            st = self.res.get(k)
            if st is not None:
                st["r"].append((eng, sem, my_val))
            else:
                self.res[k] = {"w": None, "r": [(eng, sem, my_val)]}
        for k in writes:
            self.res[k] = {"w": (eng, sem, my_val), "r": []}
        self.cnt[sem] = my_val
        real_waits = [(s, v) for s, v in waits.items() if self._need_wait(eng, s, v)]
        self.ops[eng].append((real_waits, fn, sem, n))
        return my_val

    def comp(self, eng, fn, reads=(), writes=()):
        return self._op(eng, fn, reads, writes, ENG_SEM[eng], 1)

    def dma(self, queue, sem, fn, reads=(), writes=()):
        return self._op(queue, fn, reads, writes, sem, 16)

    def patch_total(self, keys, eng, sem):
        tot = (eng, sem, self.cnt[sem])
        for k in keys:
            if k in self.res:
                self.res[k]["w"] = tot

    def wait_all(self, eng, sems_):
        for s in sems_:
            v = self.cnt[s]
            if v and self._need_wait(eng, s, v):
                self.ops[eng].append(([(s, v)], None, None, 0))

    def barrier(self):
        for e in self.ops:
            self.wait_all(e, [s for s in self.cnt if s != ENG_SEM.get(e)])


def build_nc(dbg=False):
    nc = bass.Bass("TRN2", target_bir_lowering=False)

    x_d = nc.dram_tensor("x", [S, D], F32, kind="ExternalInput")
    Wq_d = nc.dram_tensor("Wq", [D, D], F32, kind="ExternalInput")
    Wk_d = nc.dram_tensor("Wk", [D, D], F32, kind="ExternalInput")
    WkT_d = nc.dram_tensor("WkT", [D, D], F32, kind="ExternalInput")
    Wt_d = nc.dram_tensor("Wt", [D, D], F32, kind="ExternalInput")
    Wqa_d = nc.dram_tensor("Wqa", [D, H], F32, kind="ExternalInput")
    Wka_d = nc.dram_tensor("Wka", [D, H], F32, kind="ExternalInput")
    WfcT_d = nc.dram_tensor("WfcT", [D, 4 * D], F32, kind="ExternalInput")
    WprojT_d = nc.dram_tensor("WprojT", [4 * D, D], F32, kind="ExternalInput")
    bq_d = nc.dram_tensor("bq", [1, D], F32, kind="ExternalInput")
    bk_d = nc.dram_tensor("bk", [1, D], F32, kind="ExternalInput")
    bt_d = nc.dram_tensor("bt", [1, D], F32, kind="ExternalInput")
    bqa_d = nc.dram_tensor("bqa", [1, H], F32, kind="ExternalInput")
    bka_d = nc.dram_tensor("bka", [1, H], F32, kind="ExternalInput")
    out_d = nc.dram_tensor("out", [S, D], F32, kind="ExternalOutput")

    xn_d = nc.dram_tensor("xn_scratch", [NB, 128, CH * T0], BF16, kind="Internal")
    xtm_d = nc.dram_tensor("xtm_scratch", [NT3, 128, D], BF16, kind="Internal")
    mq_d = nc.dram_tensor("mq_scratch", [NB, 128, CH * T0], BF16, kind="Internal")
    qs_d = nc.dram_tensor("qs_scratch", [H, S], BF16, kind="Internal")
    cnt_d = nc.dram_tensor("cnt_scratch", [NT3, 128, D], BF16, kind="Internal")

    I128_d = nc.inline_tensor(np.eye(128, dtype=ml_dtypes.bfloat16), "ident128")
    ONES_d = nc.inline_tensor(np.ones((16, 512), dtype=ml_dtypes.bfloat16), "ones16")
    mask_np = np.zeros((H, D), dtype=ml_dtypes.bfloat16)
    for f in range(D):
        mask_np[f // DH, f] = 1.0
    MASK_d = nc.inline_tensor(mask_np, "diagmask")

    with ExitStack() as ctx:
        e = ctx.enter_context
        AB1 = e(nc.sbuf_tensor("AB1", [128, 32768], BF16))
        AB2 = e(nc.sbuf_tensor("AB2", [128, 32768], BF16))
        AM = e(nc.sbuf_tensor("AM", [128, 12288], BF16))
        SCA = e(nc.sbuf_tensor("SCA", [128, 8192], BF16))
        XF = e(nc.sbuf_tensor("XF", [128, 3072], F32))
        SQ = e(nc.sbuf_tensor("SQ", [128, 1024], BF16))
        RB = e(nc.sbuf_tensor("RB", [128, 512], F32))
        YS = e(nc.sbuf_tensor("YS", [128, 2048], F32))
        OB = e(nc.sbuf_tensor("OB", [128, 2048], F32))
        LNS = e(nc.sbuf_tensor("LNS", [128, 16], F32))
        RS = e(nc.sbuf_tensor("RS", [16, 8], F32))
        PQ = e(nc.sbuf_tensor("PQ", [128, 8], F32))
        PK = e(nc.sbuf_tensor("PK", [128, 8], F32))
        CEPS = e(nc.sbuf_tensor("CEPS", [128, 1], F32))
        WA = e(nc.sbuf_tensor("WA", [128, 128], BF16))
        WB = e(nc.sbuf_tensor("WB", [128, 128], BF16))
        WKB = e(nc.sbuf_tensor("WKB", [128, 128], BF16))
        BIAS3 = e(nc.sbuf_tensor("BIAS3", [128, 1600], BF16))
        I128 = e(nc.sbuf_tensor("I128", [128, 128], BF16))
        ONES = e(nc.sbuf_tensor("ONES", [16, 512], BF16))
        MASK = e(nc.sbuf_tensor("MASK", [16, 1024], BF16))
        QSS = e(nc.sbuf_tensor("QSS", [16, 256], BF16))
        PS = [e(nc.psum_tensor(f"PS{i}", [128, 512], F32)) for i in range(8)]
        sems = {n: e(nc.semaphore(n)) for n in SEM_NAMES}
        GP_SEMS = ("pool", "lw0", "lw1", "lw4", "lw6", "so0", "so1",
                   "ln0", "ln2", "ln4", "ln6", "lmq0", "lmq1")
        SY_SEMS = ("lc", "lw2", "lw3", "lw5", "lw7", "l30", "l31", "l32",
                   "lb0", "lb1", "ltm0", "ltm1", "lqs",
                   "ln1", "ln3", "ln5", "ln7",
                   "cr0", "cr1")
        SC_SEMS = ("act", "sx0", "sx1", "tm0", "tm1", "mq0", "mq1", "sq",
                   "sc0", "sc1")
        for eng_obj, own in ((nc.tensor, ("pe",)), (nc.scalar, SC_SEMS),
                             (nc.vector, ("dve",)), (nc.gpsimd, GP_SEMS),
                             (nc.sync, SY_SEMS)):
            for s_ in own:
                eng_obj.sem_clear(sems[s_])
        nc.all_engine_barrier()
        block = e(nc.Block())

        pl = Plan()
        ONE5 = ONES[0:1, :]

        BQ = BIAS3[0:1, 0:1024]
        BK = BIAS3[32:33, 0:1024]
        BT = BIAS3[64:65, 0:1024]
        BQA = BIAS3[0:1, 1024:1040]
        BKA = BIAS3[32:33, 1024:1040]
        pl.dma("sync", "lc", lambda s: s.dma_start(I128[:], I128_d.ap()), writes=[("I128",)])
        pl.dma("sync", "lc", lambda s: s.dma_start(ONES[:], ONES_d.ap()), writes=[("ONES",)])
        pl.dma("sync", "lc", lambda s: s.dma_start(MASK[:], MASK_d.ap()), writes=[("MASK",)])
        bias_jobs = []
        for nm, dsrc, dst, w, stg, alias in (
                ("bq", bq_d, BQ, 1024, OB[0:1, 0:1024], ("stg", 0)),
                ("bk", bk_d, BK, 1024, OB[0:1, 1024:2048], ("stg", 1)),
                ("bt", bt_d, BT, 1024, YS[0:1, 0:1024], ("ys", 0)),
                ("bqa", bqa_d, BQA, 16, YS[0:1, 1024:1040], ("ys", 1)),
                ("bka", bka_d, BKA, 16, YS[0:1, 1056:1072], ("ys", 1))):
            pl.dma("sync", "lc", lambda s, dsrc=dsrc, stg=stg: s.dma_start(stg, dsrc.ap()),
                   writes=[("bstg", nm)])
            bias_jobs.append((nm, dst, w, stg, alias))
        lc_tot = ("sync", "lc", pl.cnt["lc"])
        for k in [("I128",), ("ONES",), ("MASK",)] + [("bstg", nm) for nm, *_ in bias_jobs]:
            pl.res[k]["w"] = lc_tot
        BIAS_FINE = {"bq": [("stgp", 0), ("stgp", 2)] + [("stgn", i) for i in range(4)],
                     "bk": [("stgp", 4), ("stgp", 6)] + [("stgn", i) for i in range(4, 8)],
                     "bt": [("ysp", 1), ("ysp", 3)], "bqa": [("ysp", 5)],
                     "bka": [("ysp", 5)]}
        for nm, dst, w, stg, alias in bias_jobs:
            pl.comp("vector", lambda v, dst=dst, stg=stg, w=w: v.tensor_copy(dst[:, 0:w], stg),
                    reads=[("bstg", nm)] + BIAS_FINE[nm], writes=[(nm,), alias])
        pl.comp("vector", lambda v: v.memset(BIAS3[:, 1088:1600], 1.0),
                writes=[("ones3",)])
        pl.comp("vector", lambda v: v.memset(CEPS[:], EPS), writes=[("ceps",)])

        ppc = [0]
        cast_pend = []
        CAST_SLOTS = [
            ("gpsimd", "lw0", OB[:, 0:512], ("stg", 0)),
            ("sync", "lw2", YS[:, 0:512], ("ys", 0)),
            ("sync", "lw5", OB[:, 512:1024], ("stg", 0)),
            ("gpsimd", "lw4", YS[:, 512:1024], ("ys", 0)),
            ("gpsimd", "lw1", OB[:, 1024:1536], ("stg", 1)),
            ("sync", "lw3", YS[:, 1024:1536], ("ys", 1)),
            ("sync", "lw7", OB[:, 1536:2048], ("stg", 1)),
            ("gpsimd", "lw6", YS[:, 1536:2048], ("ys", 1)),
        ]
        XFS = [XF[:, 2048:2560], XF[:, 2560:3072]]
        CAST_SLOTS_N = [
            ("gpsimd", "ln0", OB[:, 0:512], ("stg", 0), [("stgp", 0)]),
            ("sync", "ln1", OB[:, 512:1024], ("stg", 0), [("stgp", 2)]),
            ("gpsimd", "ln2", OB[:, 1024:1536], ("stg", 1), [("stgp", 4)]),
            ("sync", "ln3", OB[:, 1536:2048], ("stg", 1), [("stgp", 6)]),
            ("gpsimd", "ln4", XFS[0], ("xfs", 0), []),
            ("sync", "ln5", XFS[1], ("xfs", 1), []),
        ]
        cast_nslots = [4]
        cast_ceng = [True]

        def cast_piece(dsrc_ap, dst_ap, dst_keys):
            mode = cast_nslots[0]
            if mode == 8:
                sl = ppc[0] % 8
                queue, sem, stg_full, skey = CAST_SLOTS[sl]
                wkeys = [(skey[0] + "p", sl)]
            else:
                rot = (0, 1, 2, 3, 4, 5) if mode == 4 else (0, 2, 4)
                sl = rot[ppc[0] % len(rot)]
                queue, sem, stg_full, skey, okeys = CAST_SLOTS_N[sl]
                wkeys = [("stgn", sl)] + okeys
            stg = stg_full[:, 0:dsrc_ap.shape[1]]
            while len(cast_pend) >= min(4, 8 if mode == 8 else
                                        (6 if mode == 4 else 3)):
                cast_pend.pop(0)()
            pl.dma(queue, sem, lambda g, a=dsrc_ap, s=stg: g.dma_start(s, a),
                   reads=[], writes=wkeys)
            ceng = ("vector", "scalar")[ppc[0] % 2] if cast_ceng[0] else "gpsimd"
            meth = (lambda e_, s_, d_: e_.copy(d_, s_)) if ceng == "scalar" else \
                   (lambda e_, s_, d_: e_.tensor_copy(d_, s_))
            cast_pend.append(lambda s=stg, d=dst_ap, k=list(dst_keys),
                             rk=wkeys[0], en=ceng, m=meth, sk=skey: pl.comp(
                en, lambda eng, s=s, d=d, m=m: m(eng, s, d),
                reads=[rk], writes=k + [sk]))
            ppc[0] += 1

        def cast(dsrc_ap, dst_ap, dst_keys):
            w = dsrc_ap.shape[1]
            if w > 512:
                for o in range(0, w, 512):
                    cast_piece(dsrc_ap[:, o:o + 512], dst_ap[:, o:o + 512], dst_keys)
            else:
                cast_piece(dsrc_ap, dst_ap, dst_keys)

        def cast_flush():
            while cast_pend:
                cast_pend.pop(0)()

        def a1(col, sz):
            return AB1[:, col: col + sz]

        def a1keys(col, sz):
            return [("A1", i) for i in range(col // 512, (col + sz + 511) // 512)]

        def a2keys(col, sz):
            return [("A2", i) for i in range(col // 512, (col + sz + 511) // 512)]

        def wsmkeys(col, sz):
            return [("WSM", i) for i in range(col // 1024, (col + sz + 1023) // 1024)]

        def pa_xload(gt):
            slot = XF[:, (gt % 2) * 1024:(gt % 2) * 1024 + 1024]
            pl.dma("sync", f"l3{gt % 2}",
                   lambda s, slot=slot, gt=gt: s.dma_start(
                       slot, x_d[gt * 128:(gt + 1) * 128, :]),
                   writes=[("x0", gt % 2)])

        def pa_ln_thunks(gt):
            th = []
            b, t = gt // 2, gt % 2
            xt = XF[:, (gt % 2) * 1024:(gt % 2) * 1024 + 1024]
            kx = ("x0", gt % 2)
            xtb = SCA[:, 4096 + (b % 2) * 2048: 4096 + (b % 2) * 2048 + 2048]
            kxtb = ("xtb", b % 2)
            par = t
            s1 = LNS[:, par * 8 + 0: par * 8 + 1]
            m_ = LNS[:, par * 8 + 1: par * 8 + 2]
            ssq = LNS[:, par * 8 + 2: par * 8 + 3]
            m2 = LNS[:, par * 8 + 3: par * 8 + 4]
            v_ = LNS[:, par * 8 + 4: par * 8 + 5]
            lv = LNS[:, par * 8 + 5: par * 8 + 6]
            r_ = LNS[:, par * 8 + 6: par * 8 + 7]
            kl = ("lns", par)
            th.append(lambda: pl.comp("vector", lambda v: v.reduce_sum(s1, xt, axis=AX.X),
                      reads=[kx], writes=[kl]))
            th.append(lambda: pl.comp("vector", lambda v: v.tensor_scalar(
                m_, s1, 1.0 / D, None, ALU.mult), reads=[kl], writes=[kl]))
            th.append(lambda: pl.comp("scalar", lambda sc: sc.activation(SQ[:], xt, AF.Square),
                      reads=[kx], writes=[("sq",)]))
            th.append(lambda: pl.comp("vector", lambda v: v.reduce_sum(ssq, SQ[:], axis=AX.X),
                      reads=[("sq",)], writes=[kl]))
            th.append(lambda: pl.comp("vector", lambda v: v.tensor_mul(m2, m_, m_),
                      reads=[kl], writes=[kl]))
            th.append(lambda: pl.comp("vector", lambda v: v.scalar_tensor_tensor(
                v_, ssq, 1.0 / D, m2, ALU.mult, ALU.subtract), reads=[kl], writes=[kl]))
            th.append(lambda: pl.comp("scalar", lambda sc: sc.activation(
                lv, v_, AF.Ln, bias=CEPS[:]), reads=[kl, ("ceps",)], writes=[kl]))
            th.append(lambda: pl.comp("scalar", lambda sc: sc.activation(
                r_, lv, AF.Exp, scale=-0.5), reads=[kl], writes=[kl]))
            xtt = xtb[:, t * 1024:(t + 1) * 1024]
            th.append(lambda: pl.comp("vector", lambda v: v.tensor_scalar(
                xtt, xt, m_, r_, ALU.subtract, ALU.mult),
                reads=[kx, kl], writes=[kxtb]))
            return th

        def pa_tr_thunks(b):
            th = []
            xnb = SCA[:, (b % 2) * 2048:(b % 2) * 2048 + 2048]
            xtb = SCA[:, 4096 + (b % 2) * 2048: 4096 + (b % 2) * 2048 + 2048]
            kxnb = ("xnb", b % 2)
            kxtb = ("xtb", b % 2)

            def mkpair(cp):
                def go():
                    bank = PS[6 + cp % 2]
                    kps = ("ps", 6 + cp % 2)
                    for ci in range(2):
                        c = 2 * cp + ci
                        for t in range(2):
                            lt = xtb[:, t * 1024 + c * 128: t * 1024 + (c + 1) * 128]
                            ot = bank[:, ci * 256 + t * 128: ci * 256 + (t + 1) * 128]
                            pl.comp("tensor", lambda te, ot=ot, lt=lt: te.matmul(
                                ot, lt, I128[:], start=True, stop=True),
                                reads=[kxtb, ("I128",)], writes=[kps])
                    dst = xnb[:, 2 * cp * 256: 2 * cp * 256 + 512]
                    pl.comp("vector", lambda v, dst=dst, cp=cp: v.tensor_copy(
                        dst, PS[6 + cp % 2][:]), reads=[("ps", 6 + cp % 2)],
                        writes=[kxnb])
                return go
            for cp in range(4):
                th.append(mkpair(cp))

            def stores():
                pl.dma("scalar", f"sx{b % 2}",
                       lambda g: g.dma_start(xn_d[b], xnb),
                       reads=[kxnb], writes=[("xnd", b)])
                for t in range(2):
                    src_ = xtb[:, t * 1024:(t + 1) * 1024]
                    gt = 2 * b + t
                    pl.dma("scalar", f"tm{gt % 2}",
                           lambda g, src_=src_, gt=gt: g.dma_start(xtm_d[gt], src_),
                           reads=[kxtb], writes=[("xtmd", gt)])
            th.append(stores)
            return th

        def pa_mq_group(b, eo):
            bank = PS[6 + eo % 2]
            kb = ("ps", 6 + eo % 2)
            xnb = SCA[:, (b % 2) * 2048:(b % 2) * 2048 + 2048]
            kxnb = ("xnb", b % 2)
            for d in range(8):
                lt = AM[:, d * 1024 + eo * 128: d * 1024 + (eo + 1) * 128]
                rh = xnb[:, d * 256:(d + 1) * 256]
                pl.comp("tensor", lambda te, bank=bank, lt=lt, rh=rh, d=d: te.matmul(
                    bank[:, 0:256], lt, rh, start=(d == 0), stop=False),
                    reads=wsmkeys(d * 1024 + eo * 128, 128) + [kxnb], writes=[kb])
            lt = BQ[:, eo * 128:(eo + 1) * 128]
            pl.comp("tensor", lambda te, bank=bank, lt=lt: te.matmul(
                bank[:, 0:256], lt, BIAS3[0:1, 1088:1344], start=False, stop=True),
                reads=[("bq",), ("ones3",)], writes=[kb])
            mqs = SCA[:, 4096 + (b % 2) * 2048 + eo * 256: 4096 + (b % 2) * 2048 + (eo + 1) * 256]
            pl.comp("vector", lambda v, mqs=mqs, bank=bank: v.tensor_copy(mqs, bank[:, 0:256]),
                    reads=[kb], writes=[("mqs", b % 2, eo), ("xtb", b % 2)])

        def pa_qs(b):
            xts = SCA[:, 4096 + (b % 2) * 2048: 4096 + (b % 2) * 2048 + 2048]
            for eo in range(CH):
                lt = WA[:, eo * 16:(eo + 1) * 16]
                rh = xts[:, eo * 256:(eo + 1) * 256]
                pl.comp("tensor", lambda te, lt=lt, rh=rh, eo=eo: te.matmul(
                    PS[6][0:16, 0:256], lt, rh, start=(eo == 0), stop=False),
                    reads=[("WA", eo), ("mqs", b % 2, eo), ("xtb", b % 2)],
                    writes=[("ps", 6)])
            pl.comp("tensor", lambda te: te.matmul(
                PS[6][0:16, 0:256], BQA, BIAS3[0:1, 1088:1344], start=False, stop=True),
                reads=[("bqa",), ("ones3",)], writes=[("ps", 6)])
            pl.comp("scalar", lambda sc: sc.activation(
                QSS[:], PS[6][0:16, 0:256], AF.Copy, scale=SCALE),
                reads=[("ps", 6)], writes=[("qss",)])
            pl.dma("scalar", "sq",
                   lambda g, b=b: g.dma_start(qs_d[:, b * 256:(b + 1) * 256], QSS[:]),
                   reads=[("qss",)], writes=[("qsd", b)])
            pl.dma("scalar", f"mq{b % 2}",
                   lambda g, b=b: g.dma_start(mq_d[b], xts),
                   reads=[("mqs", b % 2, eo_) for eo_ in range(CH)] + [("xtb", b % 2)],
                   writes=[("mqd", b)])

        pa_xload(0)
        pa_xload(1)
        for t_ in pa_ln_thunks(0):
            t_()
        pa_xload(2)
        for t_ in pa_ln_thunks(1):
            t_()
        pa_xload(3)
        for t_ in pa_tr_thunks(0):
            t_()

        cast_q = []

        def cast_startup_weights():
            for q in range(4):
                emit = (lambda f: f()) if q == 0 else cast_q.append
                for c in range(CH):
                    emit(lambda c=c, q=q: cast(
                        WfcT_d[c * 128:(c + 1) * 128, q * 1024:(q + 1) * 1024],
                        AB1[:, c * 4096 + q * 1024: c * 4096 + (q + 1) * 1024],
                        a1keys(c * 4096 + q * 1024, 1024)))
                for j in range(q * 8, q * 8 + 8):
                    emit(lambda j=j: cast(
                        WprojT_d[j * 128:(j + 1) * 128, :],
                        AB2[:, j * 1024:(j + 1) * 1024], a2keys(j * 1024, 1024)))
                if q == 0:
                    cast_flush()
                    cast_nslots[0] = 4
                    cast_ceng[0] = False

        cast_nslots[0] = 8
        for d in range(8):
            cast(Wq_d[d * 128:(d + 1) * 128, :], AM[:, d * 1024:(d + 1) * 1024],
                 [("WSM", d)])
        for wsrc, dst, nm in ((Wqa_d, WA, "WA"), (Wka_d, WB, "WB")):
            for d in range(8):
                cast(wsrc[d * 128:(d + 1) * 128, :], dst[:, d * 16:(d + 1) * 16],
                     [(nm, d)])
        cast_flush()
        cast_startup_weights()

        for eo in range(CH):
            pa_mq_group(0, eo)
        pa_qs(0)

        H2_COLS = (8192, 8448, 11776, 12032)

        def p0_xnload(b, extra=()):
            xnb = SCA[:, (b % 2) * 2048:(b % 2) * 2048 + 2048]
            pl.dma("sync", f"lb{b % 2}",
                   lambda s, b=b: s.dma_start(xnb, xn_d[b]),
                   reads=[("xnd", b)], writes=[("xnb", b % 2)] + list(extra))

        def p0_hmm(b, j):
            hb = PS[4 + j % 2]
            khb = ("ps", 4 + j % 2)
            kxnb = ("xnb", b % 2)
            xnb = SCA[:, (b % 2) * 2048:(b % 2) * 2048 + 2048]
            for c in range(CH):
                lt = AB1[:, c * 4096 + j * 128: c * 4096 + (j + 1) * 128]
                rh = xnb[:, c * 256:(c + 1) * 256]
                pl.comp("tensor", lambda te, hb=hb, lt=lt, rh=rh, c=c: te.matmul(
                    hb[:, 0:256], lt, rh, start=(c == 0), stop=(c == 7)),
                    reads=a1keys(c * 4096 + j * 128, 128) + [kxnb], writes=[khb])
            rr = RB[:, (j % 2) * 256:(j % 2) * 256 + 256]
            krr = ("rb", j % 2)
            pl.comp("scalar", lambda sc, rr=rr, hb=hb: sc.activation(
                rr, hb[:, 0:256], AF.Relu), reads=[khb], writes=[krr])
            g3 = (b * 32 + j) % 4
            h2 = AM[:, H2_COLS[g3]: H2_COLS[g3] + 256]
            pl.comp("vector", lambda v, h2=h2, rr=rr: v.tensor_mul(h2, rr, rr),
                    reads=[krr], writes=[("h2", g3)])

        def p0_ymm(b, j):
            g3 = (b * 32 + j) % 4
            h2 = AM[:, H2_COLS[g3]: H2_COLS[g3] + 256]
            kh2 = ("h2", g3)
            for t in range(2):
                lt = h2[:, t * 128:(t + 1) * 128]
                for nh in range(2):
                    yb = PS[2 * t + nh]
                    rh = AB2[:, j * 1024 + nh * 512: j * 1024 + (nh + 1) * 512]
                    pl.comp("tensor", lambda te, yb=yb, lt=lt, rh=rh, j=j: te.matmul(
                        yb[:], lt, rh, start=(j == 0), stop=(j == 31)),
                        reads=[kh2] + a2keys(j * 1024 + nh * 512, 512),
                        writes=[("ps", 2 * t + nh)])

        def p0_yevac(b):
            for t in range(2):
                yt = YS[:, t * 1024:(t + 1) * 1024]
                for nh in range(2):
                    pl.comp("scalar", lambda sc, yt=yt, t=t, nh=nh: sc.copy(
                        yt[:, nh * 512:(nh + 1) * 512], PS[2 * t + nh][:]),
                        reads=[("ps", 2 * t + nh)], writes=[("ys", t)])

        def p0_spike_thunks(b):
            th = []
            for t in range(2):
                yt = YS[:, t * 1024:(t + 1) * 1024]
                kyt = ("ys", t)
                tile = 2 * b + t
                cslot = AM[:, 8704 + (tile % 2) * 1024: 8704 + (tile % 2) * 1024 + 1024]
                kc = ("cnt", tile % 2)
                spk = AM[:, 10752:11776]
                ks_ = ("spk",)
                th.append(lambda yt=yt, cslot=cslot, kyt=kyt, kc=kc: pl.comp(
                    "vector", lambda v, cslot=cslot, yt=yt: v.tensor_scalar(
                        cslot, yt, TAUS[0], None, ALU.is_gt), reads=[kyt], writes=[kc]))
                for i in range(4):
                    o = spk if i % 2 == 0 else cslot
                    inn = cslot if i % 2 == 0 else spk
                    ki = kc if i % 2 == 0 else ks_
                    ko = ks_ if i % 2 == 0 else kc
                    th.append(lambda yt=yt, o=o, inn=inn, kyt=kyt, ki=ki, ko=ko,
                              tau=TAUS[i + 1]: pl.comp(
                        "vector", lambda v, o=o, yt=yt, inn=inn, tau=tau:
                        v.scalar_tensor_tensor(o, yt, tau, inn, ALU.is_gt, ALU.add),
                        reads=[kyt, ki], writes=[ko]))
                th.append(lambda cslot=cslot, tile=tile, kc=kc: pl.dma(
                    "scalar", f"sc{tile % 2}",
                    lambda g, cslot=cslot, tile=tile: g.dma_start(cnt_d[tile], cslot),
                    reads=[kc], writes=[("cntd", tile)]))
            return th

        def prep_thunks(m):
            th = []
            th += pa_ln_thunks(2 * m)
            if 2 * m + 2 < 2 * NB:
                th.append(lambda m=m: pa_xload(2 * m + 2))
            th += pa_ln_thunks(2 * m + 1)
            if 2 * m + 3 < 2 * NB:
                th.append(lambda m=m: pa_xload(2 * m + 3))
            th += pa_tr_thunks(m)
            th += [lambda m=m, eo=eo_: pa_mq_group(m, eo) for eo_ in range(CH)]
            th.append(lambda m=m: pa_qs(m))
            return th

        pa_q = []
        pending = []
        seq = [(b, j) for b in range(NB) for j in range(32)]
        for gi, (b, j) in enumerate(seq):
            if j == 0 and b + 1 < NB:
                pa_q.extend(prep_thunks(b + 1))
            p0_hmm(b, j)
            for _ in range(3):
                if cast_q:
                    cast_q.pop(0)()
                elif cast_pend:
                    cast_pend.pop(0)()
            if pending and j % 2 == 1 and 2 < j < 29:
                pending.pop(0)()
            if pa_q:
                pa_q.pop(0)()
            if pa_q and j % 2 == 0 and j < 14:
                pa_q.pop(0)()
            if gi > 2:
                pb, pj = seq[gi - 3]
                p0_ymm(pb, pj)
                if pj == 31:
                    p0_yevac(pb)
                    pending = p0_spike_thunks(pb)
        while pa_q:
            pa_q.pop(0)()
        while cast_q:
            cast_q.pop(0)()
        cast_flush()
        for (pb, pj) in seq[-3:]:
            p0_ymm(pb, pj)
            if pj == 31:
                p0_yevac(pb)
                pending = p0_spike_thunks(pb)
        for th in pending:
            th()
        pl.patch_total([("xnd", b) for b in range(0, NB, 2)], "scalar", "sx0")
        pl.patch_total([("xnd", b) for b in range(1, NB, 2)], "scalar", "sx1")
        pl.patch_total([("xtmd", g) for g in range(0, NT3, 2)], "scalar", "tm0")
        pl.patch_total([("xtmd", g) for g in range(1, NT3, 2)], "scalar", "tm1")
        pl.patch_total([("mqd", b) for b in range(0, NB, 2)], "scalar", "mq0")
        pl.patch_total([("mqd", b) for b in range(1, NB, 2)], "scalar", "mq1")
        pl.patch_total([("cntd", t) for t in range(0, NT3, 2)], "scalar", "sc0")
        pl.patch_total([("cntd", t) for t in range(1, NT3, 2)], "scalar", "sc1")

        SC1 = SCA[0:16, 0:4096]
        SC2 = SCA[0:16, 4096:8192]
        pl.dma("sync", "lqs", lambda s: s.dma_start(SC1, qs_d.ap()),
               reads=[("qsd", b) for b in range(NB)],
               writes=[("SC1",), ("xnb", 0), ("xnb", 1)])
        cast_nslots[0] = 2
        for d in range(8):
            cast(WkT_d[d * 128:(d + 1) * 128, :], AB2[:, d * 1024:(d + 1) * 1024],
                 a2keys(d * 1024, 1024))
        cast_flush()
        cast_nslots[0] = 4
        for g in range(NT3):
            dst = AB1[:, g * 1024:(g + 1) * 1024]
            pl.dma("sync", f"ltm{g % 2}",
                   lambda s, dst=dst, g=g: s.dma_start(dst, xtm_d[g]),
                   reads=[("xtmd", g)], writes=a1keys(g * 1024, 1024))
        for i in range(12):
            dst = AB2[:, 8192 + i * 2048: 8192 + (i + 1) * 2048]
            pl.dma("sync", f"lb{i % 2}",
                   lambda s, dst=dst, i=i: s.dma_start(dst, xn_d[i]),
                   reads=[("xnd", i)], writes=a2keys(8192 + i * 2048, 2048))

        pl.barrier()

        ENT = AM[:, 8192:8704]
        PBF = AM[0:16, 8704:9728]
        PTT = AM[:, 9728:9856]

        def pool_phase(tag, sc_src, sc_srckeys, sc_tmp, sc_tmpkeys, wrhs_ap,
                       wrhs_col, wrhs_keys):
            rmax, nmax, rsum, rrec = RS[:, 0:1], RS[:, 1:2], RS[:, 2:3], RS[:, 3:4]
            pl.comp("vector", lambda v: v.reduce_max(rmax, sc_src, axis=AX.X),
                    reads=sc_srckeys, writes=[("rs",)])
            pl.comp("vector", lambda v: v.tensor_scalar(nmax, rmax, -1.0, None, ALU.mult),
                    reads=[("rs",)], writes=[("rs",)])
            pl.comp("scalar", lambda sc: sc.activation(sc_tmp, sc_src, AF.Exp, bias=nmax),
                    reads=sc_srckeys + [("rs",)], writes=sc_tmpkeys)
            pl.comp("vector", lambda v: v.reduce_sum(rsum, sc_tmp, axis=AX.X),
                    reads=sc_tmpkeys, writes=[("rs",)])
            pl.comp("vector", lambda v: v.reciprocal(rrec, rsum),
                    reads=[("rs",)], writes=[("rs",)])
            pl.comp("vector", lambda v: v.tensor_scalar(sc_tmp, sc_tmp, rrec, None, ALU.mult),
                    reads=[("rs",)] + sc_tmpkeys, writes=sc_tmpkeys)
            for i in range(NT3):
                lt = sc_tmp[:, i * 128:(i + 1) * 128]
                ot = PS[6][:, i * 16:(i + 1) * 16]
                pl.comp("tensor", lambda te, ot=ot, lt=lt: te.matmul(
                    ot, lt, I128[0:16, 0:16], start=True, stop=True),
                    reads=sc_tmpkeys + [("I128",)], writes=[("ps", 6)])
            pl.comp("vector", lambda v: v.tensor_copy(ENT[:], PS[6][:]),
                    reads=[("ps", 6)], writes=[("ent",)])
            for i in range(NT3):
                lt = ENT[:, i * 16:(i + 1) * 16]
                for nh in range(2):
                    rh = AB1[:, i * 1024 + nh * 512: i * 1024 + (nh + 1) * 512]
                    pl.comp("tensor", lambda te, lt=lt, rh=rh, i=i, nh=nh: te.matmul(
                        PS[2 + nh][0:16, :], lt, rh,
                        start=(i == 0), stop=(i == NT3 - 1)),
                        reads=[("ent",)] + a1keys(i * 1024 + nh * 512, 512),
                        writes=[("ps", 2 + nh)])
            for nh in range(2):
                pl.comp("vector", lambda v, nh=nh: v.tensor_copy(
                    PBF[:, nh * 512:(nh + 1) * 512], PS[2 + nh][0:16, :]),
                    reads=[("ps", 2 + nh)], writes=[("pbf",)])
            for c in range(CH):
                lt = PBF[:, c * 128:(c + 1) * 128]
                ot = PS[7][:, c * 16:(c + 1) * 16]
                pl.comp("tensor", lambda te, ot=ot, lt=lt: te.matmul(
                    ot, lt, I128[0:16, 0:16], start=True, stop=True),
                    reads=[("pbf",), ("I128",)], writes=[("ps", 7)])
            pl.comp("vector", lambda v: v.tensor_copy(PTT[:], PS[7][:, 0:128]),
                    reads=[("ps", 7)], writes=[("ptt",)])
            for dc in range(CH):
                lt = PTT[:, dc * 16:(dc + 1) * 16]
                for nh in range(2):
                    rh = wrhs_ap[:, wrhs_col + dc * 1024 + nh * 512: wrhs_col + dc * 1024 + nh * 512 + 512]
                    pl.comp("tensor", lambda te, lt=lt, rh=rh, dc=dc, nh=nh: te.matmul(
                        PS[2 + nh][0:16, :], lt, rh, start=(dc == 0), stop=(dc == 7)),
                        reads=[("ptt",)] + wrhs_keys, writes=[("ps", 2 + nh)])
            for nh in range(2):
                pl.comp("vector", lambda v, nh=nh: v.tensor_tensor(
                    SQ[0:16, nh * 512:(nh + 1) * 512], PS[2 + nh][0:16, :],
                    MASK[:, nh * 512:(nh + 1) * 512], ALU.mult),
                    reads=[("ps", 2 + nh), ("MASK",)], writes=[("mskd",)])
            pl.comp("tensor", lambda te: te.matmul(
                PS[5][0:1, 0:512], ONES[:, 0:1], SQ[0:16, 0:512],
                start=True, stop=True),
                reads=[("mskd",), ("ONES",)], writes=[("ps", 5)])
            pl.comp("tensor", lambda te: te.matmul(
                PS[4][0:1, 0:512], ONES[:, 0:1], SQ[0:16, 512:1024],
                start=True, stop=True),
                reads=[("mskd",), ("ONES",)], writes=[("ps", 4)])

        def row_to_fm(srcrow_bf, srckeys, dst, dstname):
            for c in range(CH):
                lt = srcrow_bf[:, c * 128:(c + 1) * 128]
                ot = PS[7][:, c:c + 1]
                pl.comp("tensor", lambda te, ot=ot, lt=lt: te.matmul(
                    ot, lt, BIAS3[0:1, 1088:1089], start=True, stop=True),
                    reads=srckeys + [("ones3",)], writes=[("ps", 7)])
            pl.comp("vector", lambda v: v.tensor_copy(dst[:], PS[7][:, 0:8]),
                    reads=[("ps", 7)], writes=[(dstname, c_) for c_ in range(CH)])

        pool_phase("q", SC1, [("SC1",)], SC2, [("SC2",), ("xtb", 0), ("xtb", 1)],
                   AM, 0, [("WSM", d) for d in range(8)])
        PQROW = YS[0:1, 1024:2048]
        pl.comp("vector", lambda v: v.tensor_tensor(
            PQROW[:, 0:512], PS[5][0:1, 0:512], BQ[:, 0:512], ALU.add),
            reads=[("ps", 5), ("bq",)], writes=[("pqrow",)])
        pl.comp("vector", lambda v: v.tensor_tensor(
            PQROW[:, 512:1024], PS[4][0:1, 0:512], BQ[:, 512:1024], ALU.add),
            reads=[("ps", 4), ("bq",)], writes=[("pqrow",)])
        PQROWB = SQ[0:1, 0:1024]
        pl.comp("vector", lambda v: v.tensor_copy(PQROWB, PQROW),
                reads=[("pqrow",)], writes=[("pqrowb",), ("mskd",)])
        row_to_fm(PQROWB, [("pqrowb",)], PQ, "pq")

        for d in range(8):
            cast(Wt_d[d * 128:(d + 1) * 128, :], AM[:, d * 1024:(d + 1) * 1024],
                 [("WSM", d)])
        cast_flush()

        def ld_x3(t):
            slot = XF[:, (t % 2) * 1024:(t % 2 + 1) * 1024]
            pl.dma("sync", f"l3{t % 2}",
                   lambda s, slot=slot, t=t: s.dma_start(
                       slot, x_d[t * 128:(t + 1) * 128, :]),
                   writes=[("xf3", t % 2)])

        def ld_cnt3(t):
            slot = AM[:, 10240 + (t % 2) * 1024: 10240 + (t % 2) * 1024 + 1024]
            pl.dma("sync", f"cr{t % 2}",
                   lambda s, slot=slot, t=t: s.dma_start(slot, cnt_d[t]),
                   reads=[("cntd", t)], writes=[("cntr", t % 2)])

        for c in range(CH):
            wb = WB[:, c * 16:(c + 1) * 16]
            pl.comp("vector", lambda v, wb=wb, c=c: v.tensor_scalar(
                wb, wb, PQ[:, c:c + 1], None, ALU.mult),
                reads=[("pq", c), ("WB", c)], writes=[("WB", c)])
        for dc in range(CH):
            for fc in range(CH):
                lt = AB2[:, fc * 1024 + dc * 128: fc * 1024 + (dc + 1) * 128]
                rh = WB[:, fc * 16:(fc + 1) * 16]
                pl.comp("tensor", lambda te, lt=lt, rh=rh, dc=dc, fc=fc: te.matmul(
                    PS[4][:, dc * 16:(dc + 1) * 16], lt, rh,
                    start=(fc == 0), stop=(fc == 7)),
                    reads=a2keys(fc * 1024 + dc * 128, 128) + [("WB", fc)],
                    writes=[("ps", 4)])
        pl.comp("vector", lambda v: v.tensor_copy(WKB[:], PS[4][:, 0:128]),
                reads=[("ps", 4)], writes=[("WKB",)])
        BKFB = SQ[:, 16:24]
        for c in range(CH):
            lt = BK[:, c * 128:(c + 1) * 128]
            ot = PS[5][:, c:c + 1]
            pl.comp("tensor", lambda te, ot=ot, lt=lt: te.matmul(
                ot, lt, BIAS3[32:33, 1088:1089], start=True, stop=True),
                reads=[("bk",), ("ones3",)], writes=[("ps", 5)])
        pl.comp("vector", lambda v: v.tensor_copy(BKFB, PS[5][:, 0:8]),
                reads=[("ps", 5)], writes=[("bkfb",), ("pqrowb",)])
        for c in range(CH):
            pl.comp("tensor", lambda te, c=c: te.matmul(
                PS[5][0:16, 0:1], WB[:, c * 16:(c + 1) * 16], BKFB[:, c:c + 1],
                start=(c == 0), stop=False),
                reads=[("WB", c), ("bkfb",)], writes=[("ps", 5)])
        pl.comp("tensor", lambda te: te.matmul(
            PS[5][0:16, 0:1], BKA, BIAS3[32:33, 1088:1089], start=False, stop=True),
            reads=[("bka",), ("ones3",)], writes=[("ps", 5)])
        BIASK = RS[:, 4:5]
        pl.comp("vector", lambda v: v.tensor_scalar(
            BIASK, PS[5][0:16, 0:1], SCALE, None, ALU.mult),
            reads=[("ps", 5)], writes=[("rs",)])
        p0_xnload(12, extra=[("SC1",)])
        p0_xnload(13, extra=[("SC1",)])
        for b in range(NB):
            if b < 12:
                base = 8192 + b * 2048
                xnb = AB2[:, base: base + 2048]
                kxn = lambda dc, base=base: a2keys(base + dc * 256, 256)
            else:
                xnb = SCA[:, (b % 2) * 2048:(b % 2) * 2048 + 2048]
                kxn = lambda dc, b=b: [("xnb", b % 2)]
            for dc in range(CH):
                lt = WKB[:, dc * 16:(dc + 1) * 16]
                rh = xnb[:, dc * 256:(dc + 1) * 256]
                pl.comp("tensor", lambda te, lt=lt, rh=rh, dc=dc: te.matmul(
                    PS[4][0:16, 0:256], lt, rh, start=(dc == 0), stop=(dc == 7)),
                    reads=[("WKB",)] + kxn(dc), writes=[("ps", 4)])
            pl.comp("vector", lambda v, b=b: v.tensor_scalar(
                SC2[:, b * 256:(b + 1) * 256], PS[4][0:16, 0:256],
                SCALE, BIASK, ALU.mult, ALU.add),
                reads=[("ps", 4), ("rs",)], writes=[("SC2k", b)])
            if b in (12, 13):
                p0_xnload(b + 2, extra=[("SC1",)])
        for bb in range(8):
            base = 8192 + bb * 2048
            pl.dma("gpsimd", f"lmq{bb % 2}",
                   lambda g, bb=bb, base=base: g.dma_start(
                       AB2[:, base: base + 2048], mq_d[bb]),
                   reads=[("mqd", bb)], writes=a2keys(base, 2048))
        for t in range(2):
            ld_x3(t)
            ld_cnt3(t)

        for d in range(8):
            cast(Wk_d[d * 128:(d + 1) * 128, :],
                 AB2[:, 24576 + d * 1024: 24576 + (d + 1) * 1024],
                 a2keys(24576 + d * 1024, 1024))
        cast_flush()

        pool_phase("k", SC2, [("SC2k", b) for b in range(NB)], SC1,
                   [("SC1e",), ("SC1",), ("xnb", 0), ("xnb", 1)], AB2, 24576,
                   a2keys(24576, 8192))
        PKROW = YS[0:1, 0:1024]
        pl.comp("vector", lambda v: v.tensor_tensor(
            PKROW[:, 0:512], PS[5][0:1, 0:512], BK[:, 0:512], ALU.add),
            reads=[("ps", 5), ("bk",)], writes=[("pkrow",)])
        pl.comp("vector", lambda v: v.tensor_tensor(
            PKROW[:, 512:1024], PS[4][0:1, 0:512], BK[:, 512:1024], ALU.add),
            reads=[("ps", 4), ("bk",)], writes=[("pkrow",)])
        pl.comp("vector", lambda v: v.tensor_mul(PKROW, PKROW, PQROW),
                reads=[("pkrow",), ("pqrow",)], writes=[("pkrow",)])
        PKROWB = SQ[0:1, 0:1024]
        pl.comp("vector", lambda v: v.tensor_copy(PKROWB, PKROW),
                reads=[("pkrow",)], writes=[("pkrowb",), ("mskd",), ("bkfrow",)])
        row_to_fm(PKROWB, [("pkrowb",)], PK, "pk")
        MQ_SLOT_KEYS = {12: [("xnb", 0), ("SC1e",), ("SC1",)],
                        13: [("xnb", 1), ("SC1e",), ("SC1",)],
                        14: [("xtb", 0)] + [("SC2k", b_) for b_ in range(8)],
                        15: [("xtb", 1)] + [("SC2k", b_) for b_ in range(8, 16)]}
        for bb in range(8, NB):
            if bb < 12:
                base = 8192 + bb * 2048
                dst, wkeys = AB2[:, base: base + 2048], a2keys(base, 2048)
            else:
                dst = SCA[:, (bb - 12) * 2048:(bb - 11) * 2048]
                wkeys = MQ_SLOT_KEYS[bb]
            pl.dma("gpsimd", f"lmq{bb % 2}",
                   lambda g, bb=bb, dst=dst: g.dma_start(dst, mq_d[bb]),
                   reads=[("mqd", bb)], writes=wkeys)
        for d in range(8):
            cast(Wt_d[d * 128:(d + 1) * 128, :], AM[:, d * 1024:(d + 1) * 1024],
                 [("WSM", d)])
        cast_flush()
        for f in range(8):
            wt = AM[:, f * 1024:(f + 1) * 1024]
            pl.comp("vector", lambda v, wt=wt, f=f: v.tensor_scalar(
                wt, wt, PK[:, f:f + 1], None, ALU.mult),
                reads=[("pk", f), ("WSM", f)], writes=[("WSM", f)])

        pl.barrier()

        def mq_ap(bb):
            if bb < 12:
                return AB2[:, 8192 + bb * 2048: 8192 + (bb + 1) * 2048]
            return SCA[:, (bb - 12) * 2048:(bb - 11) * 2048]

        def mq_keys(bb):
            if bb < 12:
                return a2keys(8192 + bb * 2048, 2048)
            return [MQ_SLOT_KEYS[bb][0]]

        for t in range(NT3):
            bb = t // 2
            kmqb = mq_keys(bb)[0]

            def mqsl(c, t=t, bb=bb):
                return mq_ap(bb)[:, c * 256 + (t % 2) * 128: c * 256 + (t % 2) * 128 + 128]
            pset = 0 if t % 2 == 0 else 2
            b0, b1 = PS[pset], PS[pset + 1]
            kb0, kb1 = ("ps", pset), ("ps", pset + 1)
            for c in range(CH):
                lt = mqsl(c)
                for nh in range(2):
                    bank, kb = (b0, kb0) if nh == 0 else (b1, kb1)
                    rh = AM[:, c * 1024 + nh * 512: c * 1024 + nh * 512 + 512]
                    pl.comp("tensor", lambda te, bank=bank, lt=lt, rh=rh, c=c: te.matmul(
                        bank[:], lt, rh, start=(c == 0), stop=False),
                        reads=[kmqb, ("WSM", c)], writes=[kb])
            for c in range(CH):
                bank, kb = (b0, kb0) if c < 4 else (b1, kb1)
                reg = bank[:, (c % 4) * 128:(c % 4) * 128 + 128]
                lt = mqsl(c)
                pl.comp("tensor", lambda te, reg=reg, lt=lt: te.matmul(
                    reg, lt, I128[:], start=False, stop=False),
                    reads=[kmqb, ("I128",)], writes=[kb])
            for nh in range(2):
                bank, kb = (b0, kb0) if nh == 0 else (b1, kb1)
                rh = BT[:, nh * 512:(nh + 1) * 512]
                pl.comp("tensor", lambda te, bank=bank, rh=rh: te.matmul(
                    bank[:], BIAS3[64:65, 1088:1216], rh, start=False, stop=True),
                    reads=[("bt",), ("ones3",)], writes=[kb])
            cslot = AM[:, 10240 + (t % 2) * 1024: 10240 + (t % 2) * 1024 + 1024]
            gbuf = YS[:, (t % 2) * 1024:(t % 2) * 1024 + 1024]
            galias = ("pkrow",) if t % 2 == 0 else ("pqrow",)
            pl.comp("scalar", lambda sc, cslot=cslot, gbuf=gbuf: sc.activation(
                gbuf, cslot, AF.Silu, scale=0.2),
                reads=[("cntr", t % 2)], writes=[("g3", t % 2), galias])
            ot = OB[:, (t % 2) * 1024:(t % 2 + 1) * 1024]
            for nh in range(2):
                bank, kb = (b0, kb0) if nh == 0 else (b1, kb1)
                pl.comp("vector", lambda v, ot=ot, bank=bank, gbuf=gbuf, nh=nh:
                        v.tensor_mul(ot[:, nh * 512:(nh + 1) * 512], bank[:],
                                     gbuf[:, nh * 512:(nh + 1) * 512]),
                        reads=[kb, ("g3", t % 2)],
                        writes=[("ob3", t % 2), ("stg", t % 2)])
            xt = XF[:, (t % 2) * 1024:(t % 2 + 1) * 1024]
            pl.comp("vector", lambda v, ot=ot, xt=xt: v.tensor_add(ot, ot, xt),
                    reads=[("xf3", t % 2), ("ob3", t % 2)], writes=[("ob3", t % 2)])
            pl.dma("gpsimd", f"so{t % 2}",
                   lambda g, ot=ot, t=t: g.dma_start(out_d[t * 128:(t + 1) * 128, :], ot),
                   reads=[("ob3", t % 2)], writes=[("outd", t)])
            if t + 2 < NT3:
                ld_x3(t + 2)
                ld_cnt3(t + 2)

        pl.wait_all("sync", list(SY_SEMS))
        pl.wait_all("gpsimd", list(GP_SEMS))
        pl.wait_all("scalar", [s_ for s_ in SC_SEMS if s_ != "act"])

        def replay(eng_name):
            def body(eng):
                for waits, fn, sem, n in pl.ops[eng_name]:
                    for s_, v_ in waits:
                        eng.wait_ge(sems[s_], v_)
                    if fn is not None:
                        inst = fn(eng)
                        inst.then_inc(sems[sem], n)
            return body

        block.sync(replay("sync"))
        block.tensor(replay("tensor"))
        block.scalar(replay("scalar"))
        block.vector(replay("vector"))
        block.gpsimd(replay("gpsimd"))

    return nc


_NC_CACHE = None


def kernel(**inputs):
    global _NC_CACHE
    x = np.ascontiguousarray(inputs["L0_attn_hidden_states"], dtype=np.float32)
    Wk = np.ascontiguousarray(inputs["L0_attn_Wk"], dtype=np.float32)
    wmap = {
        "Wq": np.ascontiguousarray(inputs["L0_attn_Wq"], dtype=np.float32),
        "Wk": Wk,
        "WkT": np.ascontiguousarray(Wk.T),
        "Wt": np.ascontiguousarray(inputs["L0_attn_Wt"], dtype=np.float32),
        "Wqa": np.ascontiguousarray(inputs["L0_attn_Wqa"], dtype=np.float32),
        "Wka": np.ascontiguousarray(inputs["L0_attn_Wka"], dtype=np.float32),
        "WfcT": np.ascontiguousarray(np.asarray(inputs["L0_ffn_W_fc"], dtype=np.float32).T),
        "WprojT": np.ascontiguousarray(np.asarray(inputs["L0_ffn_W_proj"], dtype=np.float32).T),
        "bq": np.ascontiguousarray(np.asarray(inputs["L0_attn_bq"], dtype=np.float32).reshape(1, D)),
        "bk": np.ascontiguousarray(np.asarray(inputs["L0_attn_bk"], dtype=np.float32).reshape(1, D)),
        "bt": np.ascontiguousarray(np.asarray(inputs["L0_attn_bt"], dtype=np.float32).reshape(1, D)),
        "bqa": np.ascontiguousarray(np.asarray(inputs["L0_attn_bqa"], dtype=np.float32).reshape(1, H)),
        "bka": np.ascontiguousarray(np.asarray(inputs["L0_attn_bka"], dtype=np.float32).reshape(1, H)),
    }
    if _NC_CACHE is None:
        _NC_CACHE = build_nc()
    nc = _NC_CACHE
    in_maps = [dict(wmap, x=np.ascontiguousarray(x[b])) for b in range(8)]
    res = run_bass_kernel_spmd(nc, in_maps, core_ids=list(range(8)))
    return np.stack([np.asarray(r["out"], dtype=np.float32) for r in res.results])


# version 25
# speedup vs baseline: 1.0429x; 1.0103x over previous
import numpy as np
import ml_dtypes
from contextlib import ExitStack

import concourse.bass as bass
import concourse.mybir as mybir
from concourse.bass_utils import run_bass_kernel_spmd

F32 = mybir.dt.float32
BF16 = mybir.dt.bfloat16
AF = mybir.ActivationFunctionType
ALU = mybir.AluOpType
AX = mybir.AxisListType

S, D, H, DH, CH = 4096, 1024, 16, 64, 8
SCALE = 1.0 / 8.0
EPS = 1e-5

_a = 0.0
for _ in range(5):
    _a = _a * 0.9 + 1.0
TAUS = [c / _a for c in (1.0, 1.9, 2.81, 3.71, 4.439)]

NB, T0 = 16, 256
NT3 = 32

ENG_SEM = {"tensor": "pe", "scalar": "act", "vector": "dve", "gpsimd": "pool"}
SEM_NAMES = ("pe", "act", "dve", "pool", "lc",
             "lw0", "lw1", "lw2", "lw3", "lw4", "lw5", "lw6", "lw7",
             "ln0", "ln1", "ln2", "ln3", "ln4", "ln5", "ln6", "ln7",
             "l30", "l31", "l32",
             "lb0", "lb1",
             "ltm0", "ltm1",
             "lmq0", "lmq1",
             "lqs",
             "cr0", "cr1",
             "sx0", "sx1",
             "tm0", "tm1",
             "mq0", "mq1",
             "sq",
             "sc0", "sc1",
             "so0", "so1")
DMA_SEMS = set(SEM_NAMES[4:])


class Plan:
    def __init__(self):
        self.ops = {k: [] for k in ("tensor", "scalar", "vector", "gpsimd", "sync")}
        self.cnt = {k: 0 for k in SEM_NAMES}
        self.res = {}
        self.hiwater = {k: {} for k in self.ops}

    def _need_wait(self, eng, sem, val):
        hw = self.hiwater[eng]
        if hw.get(sem, 0) >= val:
            return False
        hw[sem] = val
        return True

    STRICT = {"lns", "rs", "pq", "pk", "ceps", "ones3",
              "bq", "bk", "bt", "bqa", "bka", "bstg"}

    def _op(self, eng, fn, reads, writes, sem, n):
        waits = {}
        my_val = self.cnt[sem] + n

        def add(dep_eng, dep_sem, dep_val, strict):
            if dep_eng == eng and dep_sem == "pe" and sem == "pe":
                return
            if dep_eng == eng and dep_sem in DMA_SEMS and sem == dep_sem:
                return
            waits[dep_sem] = max(waits.get(dep_sem, 0), dep_val)

        for k in reads:
            st = self.res.get(k)
            if st and st["w"]:
                add(*st["w"], k[0] in self.STRICT)
        for k in writes:
            st = self.res.get(k)
            if st:
                strict = k[0] in self.STRICT
                if st["w"]:
                    add(*st["w"], strict)
                for r in st["r"]:
                    add(*r, strict)
        for k in reads:
            st = self.res.get(k)
            if st is not None:
                st["r"].append((eng, sem, my_val))
            else:
                self.res[k] = {"w": None, "r": [(eng, sem, my_val)]}
        for k in writes:
            self.res[k] = {"w": (eng, sem, my_val), "r": []}
        self.cnt[sem] = my_val
        real_waits = [(s, v) for s, v in waits.items() if self._need_wait(eng, s, v)]
        self.ops[eng].append((real_waits, fn, sem, n))
        return my_val

    def comp(self, eng, fn, reads=(), writes=()):
        return self._op(eng, fn, reads, writes, ENG_SEM[eng], 1)

    def dma(self, queue, sem, fn, reads=(), writes=()):
        return self._op(queue, fn, reads, writes, sem, 16)

    def patch_total(self, keys, eng, sem):
        tot = (eng, sem, self.cnt[sem])
        for k in keys:
            if k in self.res:
                self.res[k]["w"] = tot

    def wait_all(self, eng, sems_):
        for s in sems_:
            v = self.cnt[s]
            if v and self._need_wait(eng, s, v):
                self.ops[eng].append(([(s, v)], None, None, 0))

    def barrier(self):
        for e in self.ops:
            self.wait_all(e, [s for s in self.cnt if s != ENG_SEM.get(e)])


def build_nc(dbg=False):
    nc = bass.Bass("TRN2", target_bir_lowering=False)

    x_d = nc.dram_tensor("x", [S, D], F32, kind="ExternalInput")
    Wq_d = nc.dram_tensor("Wq", [D, D], F32, kind="ExternalInput")
    Wk_d = nc.dram_tensor("Wk", [D, D], F32, kind="ExternalInput")
    WkT_d = nc.dram_tensor("WkT", [D, D], F32, kind="ExternalInput")
    Wt_d = nc.dram_tensor("Wt", [D, D], F32, kind="ExternalInput")
    Wqa_d = nc.dram_tensor("Wqa", [D, H], F32, kind="ExternalInput")
    Wka_d = nc.dram_tensor("Wka", [D, H], F32, kind="ExternalInput")
    WfcT_d = nc.dram_tensor("WfcT", [D, 4 * D], F32, kind="ExternalInput")
    WprojT_d = nc.dram_tensor("WprojT", [4 * D, D], F32, kind="ExternalInput")
    bq_d = nc.dram_tensor("bq", [1, D], F32, kind="ExternalInput")
    bk_d = nc.dram_tensor("bk", [1, D], F32, kind="ExternalInput")
    bt_d = nc.dram_tensor("bt", [1, D], F32, kind="ExternalInput")
    bqa_d = nc.dram_tensor("bqa", [1, H], F32, kind="ExternalInput")
    bka_d = nc.dram_tensor("bka", [1, H], F32, kind="ExternalInput")
    out_d = nc.dram_tensor("out", [S, D], F32, kind="ExternalOutput")

    xn_d = nc.dram_tensor("xn_scratch", [NB, 128, CH * T0], BF16, kind="Internal")
    xtm_d = nc.dram_tensor("xtm_scratch", [NT3, 128, D], BF16, kind="Internal")
    mq_d = nc.dram_tensor("mq_scratch", [NB, 128, CH * T0], BF16, kind="Internal")
    qs_d = nc.dram_tensor("qs_scratch", [H, S], BF16, kind="Internal")
    cnt_d = nc.dram_tensor("cnt_scratch", [NT3, 128, D], BF16, kind="Internal")

    I128_d = nc.inline_tensor(np.eye(128, dtype=ml_dtypes.bfloat16), "ident128")
    ONES_d = nc.inline_tensor(np.ones((16, 512), dtype=ml_dtypes.bfloat16), "ones16")
    mask_np = np.zeros((H, D), dtype=ml_dtypes.bfloat16)
    for f in range(D):
        mask_np[f // DH, f] = 1.0
    MASK_d = nc.inline_tensor(mask_np, "diagmask")

    with ExitStack() as ctx:
        e = ctx.enter_context
        AB1 = e(nc.sbuf_tensor("AB1", [128, 32768], BF16))
        AB2 = e(nc.sbuf_tensor("AB2", [128, 32768], BF16))
        AM = e(nc.sbuf_tensor("AM", [128, 12288], BF16))
        SCA = e(nc.sbuf_tensor("SCA", [128, 8192], BF16))
        XF = e(nc.sbuf_tensor("XF", [128, 3072], F32))
        SQ = e(nc.sbuf_tensor("SQ", [128, 1024], BF16))
        RB = e(nc.sbuf_tensor("RB", [128, 512], F32))
        YS = e(nc.sbuf_tensor("YS", [128, 2048], F32))
        OB = e(nc.sbuf_tensor("OB", [128, 2048], F32))
        LNS = e(nc.sbuf_tensor("LNS", [128, 16], F32))
        RS = e(nc.sbuf_tensor("RS", [16, 8], F32))
        PQ = e(nc.sbuf_tensor("PQ", [128, 8], F32))
        PK = e(nc.sbuf_tensor("PK", [128, 8], F32))
        CEPS = e(nc.sbuf_tensor("CEPS", [128, 1], F32))
        WA = e(nc.sbuf_tensor("WA", [128, 128], BF16))
        WB = e(nc.sbuf_tensor("WB", [128, 128], BF16))
        WKB = e(nc.sbuf_tensor("WKB", [128, 128], BF16))
        BIAS3 = e(nc.sbuf_tensor("BIAS3", [128, 1600], BF16))
        I128 = e(nc.sbuf_tensor("I128", [128, 128], BF16))
        ONES = e(nc.sbuf_tensor("ONES", [16, 512], BF16))
        MASK = e(nc.sbuf_tensor("MASK", [16, 1024], BF16))
        QSS = e(nc.sbuf_tensor("QSS", [16, 256], BF16))
        PS = [e(nc.psum_tensor(f"PS{i}", [128, 512], F32)) for i in range(8)]
        sems = {n: e(nc.semaphore(n)) for n in SEM_NAMES}
        GP_SEMS = ("pool", "lw0", "lw1", "lw4", "lw6", "so0", "so1",
                   "ln0", "ln2", "ln4", "ln6", "lmq0", "lmq1")
        SY_SEMS = ("lc", "lw2", "lw3", "lw5", "lw7", "l30", "l31", "l32",
                   "lb0", "lb1", "ltm0", "ltm1", "lqs",
                   "ln1", "ln3", "ln5", "ln7",
                   "cr0", "cr1")
        SC_SEMS = ("act", "sx0", "sx1", "tm0", "tm1", "mq0", "mq1", "sq",
                   "sc0", "sc1")
        for eng_obj, own in ((nc.tensor, ("pe",)), (nc.scalar, SC_SEMS),
                             (nc.vector, ("dve",)), (nc.gpsimd, GP_SEMS),
                             (nc.sync, SY_SEMS)):
            for s_ in own:
                eng_obj.sem_clear(sems[s_])
        nc.all_engine_barrier()
        block = e(nc.Block())

        pl = Plan()
        ONE5 = ONES[0:1, :]

        BQ = BIAS3[0:1, 0:1024]
        BK = BIAS3[32:33, 0:1024]
        BT = BIAS3[64:65, 0:1024]
        BQA = BIAS3[0:1, 1024:1040]
        BKA = BIAS3[32:33, 1024:1040]
        pl.dma("sync", "lc", lambda s: s.dma_start(I128[:], I128_d.ap()), writes=[("I128",)])
        pl.dma("sync", "lc", lambda s: s.dma_start(ONES[:], ONES_d.ap()), writes=[("ONES",)])
        pl.dma("sync", "lc", lambda s: s.dma_start(MASK[:], MASK_d.ap()), writes=[("MASK",)])
        bias_jobs = []
        for nm, dsrc, dst, w, stg, alias in (
                ("bq", bq_d, BQ, 1024, OB[0:1, 0:1024], ("stg", 0)),
                ("bk", bk_d, BK, 1024, OB[0:1, 1024:2048], ("stg", 1)),
                ("bt", bt_d, BT, 1024, YS[0:1, 0:1024], ("ys", 0)),
                ("bqa", bqa_d, BQA, 16, YS[0:1, 1024:1040], ("ys", 1)),
                ("bka", bka_d, BKA, 16, YS[0:1, 1056:1072], ("ys", 1))):
            pl.dma("sync", "lc", lambda s, dsrc=dsrc, stg=stg: s.dma_start(stg, dsrc.ap()),
                   writes=[("bstg", nm)])
            bias_jobs.append((nm, dst, w, stg, alias))
        lc_tot = ("sync", "lc", pl.cnt["lc"])
        for k in [("I128",), ("ONES",), ("MASK",)] + [("bstg", nm) for nm, *_ in bias_jobs]:
            pl.res[k]["w"] = lc_tot
        BIAS_FINE = {"bq": [("stgp", 0), ("stgp", 2)] + [("stgn", i) for i in range(4)],
                     "bk": [("stgp", 4), ("stgp", 6)] + [("stgn", i) for i in range(4, 8)],
                     "bt": [("ysp", 1), ("ysp", 3)], "bqa": [("ysp", 5)],
                     "bka": [("ysp", 5)]}
        for nm, dst, w, stg, alias in bias_jobs:
            pl.comp("vector", lambda v, dst=dst, stg=stg, w=w: v.tensor_copy(dst[:, 0:w], stg),
                    reads=[("bstg", nm)] + BIAS_FINE[nm], writes=[(nm,), alias])
        pl.comp("vector", lambda v: v.memset(BIAS3[:, 1088:1600], 1.0),
                writes=[("ones3",)])
        pl.comp("vector", lambda v: v.memset(CEPS[:], EPS), writes=[("ceps",)])

        ppc = [0]
        cast_pend = []
        CAST_SLOTS = [
            ("gpsimd", "lw0", OB[:, 0:512], ("stg", 0)),
            ("sync", "lw2", YS[:, 0:512], ("ys", 0)),
            ("sync", "lw5", OB[:, 512:1024], ("stg", 0)),
            ("gpsimd", "lw4", YS[:, 512:1024], ("ys", 0)),
            ("gpsimd", "lw1", OB[:, 1024:1536], ("stg", 1)),
            ("sync", "lw3", YS[:, 1024:1536], ("ys", 1)),
            ("sync", "lw7", OB[:, 1536:2048], ("stg", 1)),
            ("gpsimd", "lw6", YS[:, 1536:2048], ("ys", 1)),
        ]
        XFS = [XF[:, 2048:2560], XF[:, 2560:3072]]
        CAST_SLOTS_N = [
            ("gpsimd", "ln0", OB[:, 0:512], ("stg", 0), [("stgp", 0)]),
            ("sync", "ln1", OB[:, 512:1024], ("stg", 0), [("stgp", 2)]),
            ("gpsimd", "ln2", OB[:, 1024:1536], ("stg", 1), [("stgp", 4)]),
            ("sync", "ln3", OB[:, 1536:2048], ("stg", 1), [("stgp", 6)]),
            ("gpsimd", "ln4", XFS[0], ("xfs", 0), []),
            ("sync", "ln5", XFS[1], ("xfs", 1), []),
        ]
        cast_nslots = [4]
        cast_ceng = [True]

        def cast_piece(dsrc_ap, dst_ap, dst_keys):
            mode = cast_nslots[0]
            if mode == 8:
                sl = ppc[0] % 8
                queue, sem, stg_full, skey = CAST_SLOTS[sl]
                wkeys = [(skey[0] + "p", sl)]
            else:
                rot = (0, 1, 2, 3, 4, 5) if mode == 4 else (0, 2, 4)
                sl = rot[ppc[0] % len(rot)]
                queue, sem, stg_full, skey, okeys = CAST_SLOTS_N[sl]
                wkeys = [("stgn", sl)] + okeys
            stg = stg_full[:, 0:dsrc_ap.shape[1]]
            while len(cast_pend) >= min(4, 8 if mode == 8 else
                                        (6 if mode == 4 else 3)):
                cast_pend.pop(0)()
            pl.dma(queue, sem, lambda g, a=dsrc_ap, s=stg: g.dma_start(s, a),
                   reads=[], writes=wkeys)
            ceng = ("vector", "scalar")[ppc[0] % 2] if cast_ceng[0] else \
                   ("scalar", "gpsimd")[ppc[0] % 2]
            meth = (lambda e_, s_, d_: e_.copy(d_, s_)) if ceng == "scalar" else \
                   (lambda e_, s_, d_: e_.tensor_copy(d_, s_))
            cast_pend.append(lambda s=stg, d=dst_ap, k=list(dst_keys),
                             rk=wkeys[0], en=ceng, m=meth, sk=skey: pl.comp(
                en, lambda eng, s=s, d=d, m=m: m(eng, s, d),
                reads=[rk], writes=k + [sk]))
            ppc[0] += 1

        def cast(dsrc_ap, dst_ap, dst_keys):
            w = dsrc_ap.shape[1]
            if w > 512:
                for o in range(0, w, 512):
                    cast_piece(dsrc_ap[:, o:o + 512], dst_ap[:, o:o + 512], dst_keys)
            else:
                cast_piece(dsrc_ap, dst_ap, dst_keys)

        def cast_flush():
            while cast_pend:
                cast_pend.pop(0)()

        def a1(col, sz):
            return AB1[:, col: col + sz]

        def a1keys(col, sz):
            return [("A1", i) for i in range(col // 512, (col + sz + 511) // 512)]

        def a2keys(col, sz):
            return [("A2", i) for i in range(col // 512, (col + sz + 511) // 512)]

        def wsmkeys(col, sz):
            return [("WSM", i) for i in range(col // 1024, (col + sz + 1023) // 1024)]

        def pa_xload(gt):
            slot = XF[:, (gt % 2) * 1024:(gt % 2) * 1024 + 1024]
            pl.dma("sync", f"l3{gt % 2}",
                   lambda s, slot=slot, gt=gt: s.dma_start(
                       slot, x_d[gt * 128:(gt + 1) * 128, :]),
                   writes=[("x0", gt % 2)])

        def pa_ln_thunks(gt):
            th = []
            b, t = gt // 2, gt % 2
            xt = XF[:, (gt % 2) * 1024:(gt % 2) * 1024 + 1024]
            kx = ("x0", gt % 2)
            xtb = SCA[:, 4096 + (b % 2) * 2048: 4096 + (b % 2) * 2048 + 2048]
            kxtb = ("xtb", b % 2)
            par = t
            s1 = LNS[:, par * 8 + 0: par * 8 + 1]
            m_ = LNS[:, par * 8 + 1: par * 8 + 2]
            ssq = LNS[:, par * 8 + 2: par * 8 + 3]
            m2 = LNS[:, par * 8 + 3: par * 8 + 4]
            v_ = LNS[:, par * 8 + 4: par * 8 + 5]
            lv = LNS[:, par * 8 + 5: par * 8 + 6]
            r_ = LNS[:, par * 8 + 6: par * 8 + 7]
            kl = ("lns", par)
            th.append(lambda: pl.comp("vector", lambda v: v.reduce_sum(s1, xt, axis=AX.X),
                      reads=[kx], writes=[kl]))
            th.append(lambda: pl.comp("vector", lambda v: v.tensor_scalar(
                m_, s1, 1.0 / D, None, ALU.mult), reads=[kl], writes=[kl]))
            th.append(lambda: pl.comp("scalar", lambda sc: sc.activation(SQ[:], xt, AF.Square),
                      reads=[kx], writes=[("sq",)]))
            th.append(lambda: pl.comp("vector", lambda v: v.reduce_sum(ssq, SQ[:], axis=AX.X),
                      reads=[("sq",)], writes=[kl]))
            th.append(lambda: pl.comp("vector", lambda v: v.tensor_mul(m2, m_, m_),
                      reads=[kl], writes=[kl]))
            th.append(lambda: pl.comp("vector", lambda v: v.scalar_tensor_tensor(
                v_, ssq, 1.0 / D, m2, ALU.mult, ALU.subtract), reads=[kl], writes=[kl]))
            th.append(lambda: pl.comp("scalar", lambda sc: sc.activation(
                lv, v_, AF.Ln, bias=CEPS[:]), reads=[kl, ("ceps",)], writes=[kl]))
            th.append(lambda: pl.comp("scalar", lambda sc: sc.activation(
                r_, lv, AF.Exp, scale=-0.5), reads=[kl], writes=[kl]))
            xtt = xtb[:, t * 1024:(t + 1) * 1024]
            th.append(lambda: pl.comp("vector", lambda v: v.tensor_scalar(
                xtt, xt, m_, r_, ALU.subtract, ALU.mult),
                reads=[kx, kl], writes=[kxtb]))
            return th

        def pa_tr_thunks(b):
            th = []
            xnb = SCA[:, (b % 2) * 2048:(b % 2) * 2048 + 2048]
            xtb = SCA[:, 4096 + (b % 2) * 2048: 4096 + (b % 2) * 2048 + 2048]
            kxnb = ("xnb", b % 2)
            kxtb = ("xtb", b % 2)

            def mkpair(cp):
                def go():
                    bank = PS[6 + cp % 2]
                    kps = ("ps", 6 + cp % 2)
                    for ci in range(2):
                        c = 2 * cp + ci
                        for t in range(2):
                            lt = xtb[:, t * 1024 + c * 128: t * 1024 + (c + 1) * 128]
                            ot = bank[:, ci * 256 + t * 128: ci * 256 + (t + 1) * 128]
                            pl.comp("tensor", lambda te, ot=ot, lt=lt: te.matmul(
                                ot, lt, I128[:], start=True, stop=True),
                                reads=[kxtb, ("I128",)], writes=[kps])
                    dst = xnb[:, 2 * cp * 256: 2 * cp * 256 + 512]
                    pl.comp("vector", lambda v, dst=dst, cp=cp: v.tensor_copy(
                        dst, PS[6 + cp % 2][:]), reads=[("ps", 6 + cp % 2)],
                        writes=[kxnb])
                return go
            for cp in range(4):
                th.append(mkpair(cp))

            def stores():
                pl.dma("scalar", f"sx{b % 2}",
                       lambda g: g.dma_start(xn_d[b], xnb),
                       reads=[kxnb], writes=[("xnd", b)])
                for t in range(2):
                    src_ = xtb[:, t * 1024:(t + 1) * 1024]
                    gt = 2 * b + t
                    pl.dma("scalar", f"tm{gt % 2}",
                           lambda g, src_=src_, gt=gt: g.dma_start(xtm_d[gt], src_),
                           reads=[kxtb], writes=[("xtmd", gt)])
            th.append(stores)
            return th

        def pa_mq_group(b, eo):
            bank = PS[6 + eo % 2]
            kb = ("ps", 6 + eo % 2)
            xnb = SCA[:, (b % 2) * 2048:(b % 2) * 2048 + 2048]
            kxnb = ("xnb", b % 2)
            for d in range(8):
                lt = AM[:, d * 1024 + eo * 128: d * 1024 + (eo + 1) * 128]
                rh = xnb[:, d * 256:(d + 1) * 256]
                pl.comp("tensor", lambda te, bank=bank, lt=lt, rh=rh, d=d: te.matmul(
                    bank[:, 0:256], lt, rh, start=(d == 0), stop=False),
                    reads=wsmkeys(d * 1024 + eo * 128, 128) + [kxnb], writes=[kb])
            lt = BQ[:, eo * 128:(eo + 1) * 128]
            pl.comp("tensor", lambda te, bank=bank, lt=lt: te.matmul(
                bank[:, 0:256], lt, BIAS3[0:1, 1088:1344], start=False, stop=True),
                reads=[("bq",), ("ones3",)], writes=[kb])
            mqs = SCA[:, 4096 + (b % 2) * 2048 + eo * 256: 4096 + (b % 2) * 2048 + (eo + 1) * 256]
            pl.comp("vector", lambda v, mqs=mqs, bank=bank: v.tensor_copy(mqs, bank[:, 0:256]),
                    reads=[kb], writes=[("mqs", b % 2, eo), ("xtb", b % 2)])

        def pa_qs(b):
            xts = SCA[:, 4096 + (b % 2) * 2048: 4096 + (b % 2) * 2048 + 2048]
            for eo in range(CH):
                lt = WA[:, eo * 16:(eo + 1) * 16]
                rh = xts[:, eo * 256:(eo + 1) * 256]
                pl.comp("tensor", lambda te, lt=lt, rh=rh, eo=eo: te.matmul(
                    PS[6][0:16, 0:256], lt, rh, start=(eo == 0), stop=False),
                    reads=[("WA", eo), ("mqs", b % 2, eo), ("xtb", b % 2)],
                    writes=[("ps", 6)])
            pl.comp("tensor", lambda te: te.matmul(
                PS[6][0:16, 0:256], BQA, BIAS3[0:1, 1088:1344], start=False, stop=True),
                reads=[("bqa",), ("ones3",)], writes=[("ps", 6)])
            pl.comp("scalar", lambda sc: sc.activation(
                QSS[:], PS[6][0:16, 0:256], AF.Copy, scale=SCALE),
                reads=[("ps", 6)], writes=[("qss",)])
            pl.dma("scalar", "sq",
                   lambda g, b=b: g.dma_start(qs_d[:, b * 256:(b + 1) * 256], QSS[:]),
                   reads=[("qss",)], writes=[("qsd", b)])
            pl.dma("scalar", f"mq{b % 2}",
                   lambda g, b=b: g.dma_start(mq_d[b], xts),
                   reads=[("mqs", b % 2, eo_) for eo_ in range(CH)] + [("xtb", b % 2)],
                   writes=[("mqd", b)])

        pa_xload(0)
        pa_xload(1)
        for t_ in pa_ln_thunks(0):
            t_()
        pa_xload(2)
        for t_ in pa_ln_thunks(1):
            t_()
        pa_xload(3)
        for t_ in pa_tr_thunks(0):
            t_()

        cast_q = []

        def cast_startup_weights():
            for q in range(4):
                emit = (lambda f: f()) if q == 0 else cast_q.append
                for c in range(CH):
                    emit(lambda c=c, q=q: cast(
                        WfcT_d[c * 128:(c + 1) * 128, q * 1024:(q + 1) * 1024],
                        AB1[:, c * 4096 + q * 1024: c * 4096 + (q + 1) * 1024],
                        a1keys(c * 4096 + q * 1024, 1024)))
                for j in range(q * 8, q * 8 + 8):
                    emit(lambda j=j: cast(
                        WprojT_d[j * 128:(j + 1) * 128, :],
                        AB2[:, j * 1024:(j + 1) * 1024], a2keys(j * 1024, 1024)))
                if q == 0:
                    cast_flush()
                    cast_nslots[0] = 4
                    cast_ceng[0] = False

        cast_nslots[0] = 8
        for d in range(8):
            cast(Wq_d[d * 128:(d + 1) * 128, :], AM[:, d * 1024:(d + 1) * 1024],
                 [("WSM", d)])
        for wsrc, dst, nm in ((Wqa_d, WA, "WA"), (Wka_d, WB, "WB")):
            for d in range(8):
                cast(wsrc[d * 128:(d + 1) * 128, :], dst[:, d * 16:(d + 1) * 16],
                     [(nm, d)])
        cast_flush()
        cast_startup_weights()

        for eo in range(CH):
            pa_mq_group(0, eo)
        pa_qs(0)

        H2_COLS = (8192, 8448, 11776, 12032)

        def p0_xnload(b, extra=()):
            xnb = SCA[:, (b % 2) * 2048:(b % 2) * 2048 + 2048]
            pl.dma("sync", f"lb{b % 2}",
                   lambda s, b=b: s.dma_start(xnb, xn_d[b]),
                   reads=[("xnd", b)], writes=[("xnb", b % 2)] + list(extra))

        def p0_hmm(b, j):
            hb = PS[4 + j % 2]
            khb = ("ps", 4 + j % 2)
            kxnb = ("xnb", b % 2)
            xnb = SCA[:, (b % 2) * 2048:(b % 2) * 2048 + 2048]
            for c in range(CH):
                lt = AB1[:, c * 4096 + j * 128: c * 4096 + (j + 1) * 128]
                rh = xnb[:, c * 256:(c + 1) * 256]
                pl.comp("tensor", lambda te, hb=hb, lt=lt, rh=rh, c=c: te.matmul(
                    hb[:, 0:256], lt, rh, start=(c == 0), stop=(c == 7)),
                    reads=a1keys(c * 4096 + j * 128, 128) + [kxnb], writes=[khb])
            rr = RB[:, (j % 2) * 256:(j % 2) * 256 + 256]
            krr = ("rb", j % 2)
            pl.comp("scalar", lambda sc, rr=rr, hb=hb: sc.activation(
                rr, hb[:, 0:256], AF.Relu), reads=[khb], writes=[krr])
            g3 = (b * 32 + j) % 4
            h2 = AM[:, H2_COLS[g3]: H2_COLS[g3] + 256]
            pl.comp("vector", lambda v, h2=h2, rr=rr: v.tensor_mul(h2, rr, rr),
                    reads=[krr], writes=[("h2", g3)])

        def p0_ymm(b, j):
            g3 = (b * 32 + j) % 4
            h2 = AM[:, H2_COLS[g3]: H2_COLS[g3] + 256]
            kh2 = ("h2", g3)
            for t in range(2):
                lt = h2[:, t * 128:(t + 1) * 128]
                for nh in range(2):
                    yb = PS[2 * t + nh]
                    rh = AB2[:, j * 1024 + nh * 512: j * 1024 + (nh + 1) * 512]
                    pl.comp("tensor", lambda te, yb=yb, lt=lt, rh=rh, j=j: te.matmul(
                        yb[:], lt, rh, start=(j == 0), stop=(j == 31)),
                        reads=[kh2] + a2keys(j * 1024 + nh * 512, 512),
                        writes=[("ps", 2 * t + nh)])

        def p0_yevac(b):
            for t in range(2):
                yt = YS[:, t * 1024:(t + 1) * 1024]
                for nh in range(2):
                    pl.comp("scalar", lambda sc, yt=yt, t=t, nh=nh: sc.copy(
                        yt[:, nh * 512:(nh + 1) * 512], PS[2 * t + nh][:]),
                        reads=[("ps", 2 * t + nh)], writes=[("ys", t)])

        def p0_spike_thunks(b):
            th = []
            for t in range(2):
                yt = YS[:, t * 1024:(t + 1) * 1024]
                kyt = ("ys", t)
                tile = 2 * b + t
                cslot = AM[:, 8704 + (tile % 2) * 1024: 8704 + (tile % 2) * 1024 + 1024]
                kc = ("cnt", tile % 2)
                spk = AM[:, 10752:11776]
                ks_ = ("spk",)
                th.append(lambda yt=yt, cslot=cslot, kyt=kyt, kc=kc: pl.comp(
                    "vector", lambda v, cslot=cslot, yt=yt: v.tensor_scalar(
                        cslot, yt, TAUS[0], None, ALU.is_gt), reads=[kyt], writes=[kc]))
                for i in range(4):
                    o = spk if i % 2 == 0 else cslot
                    inn = cslot if i % 2 == 0 else spk
                    ki = kc if i % 2 == 0 else ks_
                    ko = ks_ if i % 2 == 0 else kc
                    th.append(lambda yt=yt, o=o, inn=inn, kyt=kyt, ki=ki, ko=ko,
                              tau=TAUS[i + 1]: pl.comp(
                        "vector", lambda v, o=o, yt=yt, inn=inn, tau=tau:
                        v.scalar_tensor_tensor(o, yt, tau, inn, ALU.is_gt, ALU.add),
                        reads=[kyt, ki], writes=[ko]))
                th.append(lambda cslot=cslot, tile=tile, kc=kc: pl.dma(
                    "scalar", f"sc{tile % 2}",
                    lambda g, cslot=cslot, tile=tile: g.dma_start(cnt_d[tile], cslot),
                    reads=[kc], writes=[("cntd", tile)]))
            return th

        def prep_thunks(m):
            th = []
            th += pa_ln_thunks(2 * m)
            if 2 * m + 2 < 2 * NB:
                th.append(lambda m=m: pa_xload(2 * m + 2))
            th += pa_ln_thunks(2 * m + 1)
            if 2 * m + 3 < 2 * NB:
                th.append(lambda m=m: pa_xload(2 * m + 3))
            th += pa_tr_thunks(m)
            th += [lambda m=m, eo=eo_: pa_mq_group(m, eo) for eo_ in range(CH)]
            th.append(lambda m=m: pa_qs(m))
            return th

        pa_q = []
        pending = []
        seq = [(b, j) for b in range(NB) for j in range(32)]
        for gi, (b, j) in enumerate(seq):
            if j == 0 and b + 1 < NB:
                pa_q.extend(prep_thunks(b + 1))
            p0_hmm(b, j)
            for _ in range(3):
                if cast_q:
                    cast_q.pop(0)()
                elif cast_pend:
                    cast_pend.pop(0)()
            if pending and j % 2 == 1 and 2 < j < 29:
                pending.pop(0)()
            if pa_q:
                pa_q.pop(0)()
            if pa_q and j % 2 == 0 and j < 14:
                pa_q.pop(0)()
            if gi > 2:
                pb, pj = seq[gi - 3]
                p0_ymm(pb, pj)
                if pj == 31:
                    p0_yevac(pb)
                    pending = p0_spike_thunks(pb)
        while pa_q:
            pa_q.pop(0)()
        while cast_q:
            cast_q.pop(0)()
        cast_flush()
        for (pb, pj) in seq[-3:]:
            p0_ymm(pb, pj)
            if pj == 31:
                p0_yevac(pb)
                pending = p0_spike_thunks(pb)
        for th in pending:
            th()
        pl.patch_total([("xnd", b) for b in range(0, NB, 2)], "scalar", "sx0")
        pl.patch_total([("xnd", b) for b in range(1, NB, 2)], "scalar", "sx1")
        pl.patch_total([("xtmd", g) for g in range(0, NT3, 2)], "scalar", "tm0")
        pl.patch_total([("xtmd", g) for g in range(1, NT3, 2)], "scalar", "tm1")
        pl.patch_total([("mqd", b) for b in range(0, NB, 2)], "scalar", "mq0")
        pl.patch_total([("mqd", b) for b in range(1, NB, 2)], "scalar", "mq1")
        pl.patch_total([("cntd", t) for t in range(0, NT3, 2)], "scalar", "sc0")
        pl.patch_total([("cntd", t) for t in range(1, NT3, 2)], "scalar", "sc1")

        SC1 = SCA[0:16, 0:4096]
        SC2 = SCA[0:16, 4096:8192]
        pl.dma("sync", "lqs", lambda s: s.dma_start(SC1, qs_d.ap()),
               reads=[("qsd", b) for b in range(NB)],
               writes=[("SC1",), ("xnb", 0), ("xnb", 1)])
        cast_nslots[0] = 2
        for d in range(8):
            cast(WkT_d[d * 128:(d + 1) * 128, :], AB2[:, d * 1024:(d + 1) * 1024],
                 a2keys(d * 1024, 1024))
        cast_flush()
        cast_nslots[0] = 4
        for g in range(NT3):
            dst = AB1[:, g * 1024:(g + 1) * 1024]
            pl.dma("sync", f"ltm{g % 2}",
                   lambda s, dst=dst, g=g: s.dma_start(dst, xtm_d[g]),
                   reads=[("xtmd", g)], writes=a1keys(g * 1024, 1024))
        for i in range(12):
            dst = AB2[:, 8192 + i * 2048: 8192 + (i + 1) * 2048]
            pl.dma("sync", f"lb{i % 2}",
                   lambda s, dst=dst, i=i: s.dma_start(dst, xn_d[i]),
                   reads=[("xnd", i)], writes=a2keys(8192 + i * 2048, 2048))

        pl.barrier()

        ENT = AM[:, 8192:8704]
        PBF = AM[0:16, 8704:9728]
        PTT = AM[:, 9728:9856]

        def pool_phase(tag, sc_src, sc_srckeys, sc_tmp, sc_tmpkeys, wrhs_ap,
                       wrhs_col, wrhs_keys):
            rmax, nmax, rsum, rrec = RS[:, 0:1], RS[:, 1:2], RS[:, 2:3], RS[:, 3:4]
            pl.comp("vector", lambda v: v.reduce_max(rmax, sc_src, axis=AX.X),
                    reads=sc_srckeys, writes=[("rs",)])
            pl.comp("vector", lambda v: v.tensor_scalar(nmax, rmax, -1.0, None, ALU.mult),
                    reads=[("rs",)], writes=[("rs",)])
            pl.comp("scalar", lambda sc: sc.activation(sc_tmp, sc_src, AF.Exp, bias=nmax),
                    reads=sc_srckeys + [("rs",)], writes=sc_tmpkeys)
            pl.comp("vector", lambda v: v.reduce_sum(rsum, sc_tmp, axis=AX.X),
                    reads=sc_tmpkeys, writes=[("rs",)])
            pl.comp("vector", lambda v: v.reciprocal(rrec, rsum),
                    reads=[("rs",)], writes=[("rs",)])
            pl.comp("vector", lambda v: v.tensor_scalar(sc_tmp, sc_tmp, rrec, None, ALU.mult),
                    reads=[("rs",)] + sc_tmpkeys, writes=sc_tmpkeys)
            for i in range(NT3):
                lt = sc_tmp[:, i * 128:(i + 1) * 128]
                ot = PS[6][:, i * 16:(i + 1) * 16]
                pl.comp("tensor", lambda te, ot=ot, lt=lt: te.matmul(
                    ot, lt, I128[0:16, 0:16], start=True, stop=True),
                    reads=sc_tmpkeys + [("I128",)], writes=[("ps", 6)])
            pl.comp("vector", lambda v: v.tensor_copy(ENT[:], PS[6][:]),
                    reads=[("ps", 6)], writes=[("ent",)])
            for i in range(NT3):
                lt = ENT[:, i * 16:(i + 1) * 16]
                for nh in range(2):
                    rh = AB1[:, i * 1024 + nh * 512: i * 1024 + (nh + 1) * 512]
                    pl.comp("tensor", lambda te, lt=lt, rh=rh, i=i, nh=nh: te.matmul(
                        PS[2 + nh][0:16, :], lt, rh,
                        start=(i == 0), stop=(i == NT3 - 1)),
                        reads=[("ent",)] + a1keys(i * 1024 + nh * 512, 512),
                        writes=[("ps", 2 + nh)])
            for nh in range(2):
                pl.comp("vector", lambda v, nh=nh: v.tensor_copy(
                    PBF[:, nh * 512:(nh + 1) * 512], PS[2 + nh][0:16, :]),
                    reads=[("ps", 2 + nh)], writes=[("pbf",)])
            for c in range(CH):
                lt = PBF[:, c * 128:(c + 1) * 128]
                ot = PS[7][:, c * 16:(c + 1) * 16]
                pl.comp("tensor", lambda te, ot=ot, lt=lt: te.matmul(
                    ot, lt, I128[0:16, 0:16], start=True, stop=True),
                    reads=[("pbf",), ("I128",)], writes=[("ps", 7)])
            pl.comp("vector", lambda v: v.tensor_copy(PTT[:], PS[7][:, 0:128]),
                    reads=[("ps", 7)], writes=[("ptt",)])
            for dc in range(CH):
                lt = PTT[:, dc * 16:(dc + 1) * 16]
                for nh in range(2):
                    rh = wrhs_ap[:, wrhs_col + dc * 1024 + nh * 512: wrhs_col + dc * 1024 + nh * 512 + 512]
                    pl.comp("tensor", lambda te, lt=lt, rh=rh, dc=dc, nh=nh: te.matmul(
                        PS[2 + nh][0:16, :], lt, rh, start=(dc == 0), stop=(dc == 7)),
                        reads=[("ptt",)] + wrhs_keys, writes=[("ps", 2 + nh)])
            for nh in range(2):
                pl.comp("vector", lambda v, nh=nh: v.tensor_tensor(
                    SQ[0:16, nh * 512:(nh + 1) * 512], PS[2 + nh][0:16, :],
                    MASK[:, nh * 512:(nh + 1) * 512], ALU.mult),
                    reads=[("ps", 2 + nh), ("MASK",)], writes=[("mskd",)])
            pl.comp("tensor", lambda te: te.matmul(
                PS[5][0:1, 0:512], ONES[:, 0:1], SQ[0:16, 0:512],
                start=True, stop=True),
                reads=[("mskd",), ("ONES",)], writes=[("ps", 5)])
            pl.comp("tensor", lambda te: te.matmul(
                PS[4][0:1, 0:512], ONES[:, 0:1], SQ[0:16, 512:1024],
                start=True, stop=True),
                reads=[("mskd",), ("ONES",)], writes=[("ps", 4)])

        def row_to_fm(srcrow_bf, srckeys, dst, dstname):
            for c in range(CH):
                lt = srcrow_bf[:, c * 128:(c + 1) * 128]
                ot = PS[7][:, c:c + 1]
                pl.comp("tensor", lambda te, ot=ot, lt=lt: te.matmul(
                    ot, lt, BIAS3[0:1, 1088:1089], start=True, stop=True),
                    reads=srckeys + [("ones3",)], writes=[("ps", 7)])
            pl.comp("vector", lambda v: v.tensor_copy(dst[:], PS[7][:, 0:8]),
                    reads=[("ps", 7)], writes=[(dstname, c_) for c_ in range(CH)])

        pool_phase("q", SC1, [("SC1",)], SC2, [("SC2",), ("xtb", 0), ("xtb", 1)],
                   AM, 0, [("WSM", d) for d in range(8)])
        PQROW = YS[0:1, 1024:2048]
        pl.comp("vector", lambda v: v.tensor_tensor(
            PQROW[:, 0:512], PS[5][0:1, 0:512], BQ[:, 0:512], ALU.add),
            reads=[("ps", 5), ("bq",)], writes=[("pqrow",)])
        pl.comp("vector", lambda v: v.tensor_tensor(
            PQROW[:, 512:1024], PS[4][0:1, 0:512], BQ[:, 512:1024], ALU.add),
            reads=[("ps", 4), ("bq",)], writes=[("pqrow",)])
        PQROWB = SQ[0:1, 0:1024]
        pl.comp("vector", lambda v: v.tensor_copy(PQROWB, PQROW),
                reads=[("pqrow",)], writes=[("pqrowb",), ("mskd",)])
        row_to_fm(PQROWB, [("pqrowb",)], PQ, "pq")

        for d in range(8):
            cast(Wt_d[d * 128:(d + 1) * 128, :], AM[:, d * 1024:(d + 1) * 1024],
                 [("WSM", d)])
        cast_flush()

        def ld_x3(t):
            slot = XF[:, (t % 2) * 1024:(t % 2 + 1) * 1024]
            pl.dma("sync", f"l3{t % 2}",
                   lambda s, slot=slot, t=t: s.dma_start(
                       slot, x_d[t * 128:(t + 1) * 128, :]),
                   writes=[("xf3", t % 2)])

        def ld_cnt3(t):
            slot = AM[:, 10240 + (t % 2) * 1024: 10240 + (t % 2) * 1024 + 1024]
            pl.dma("sync", f"cr{t % 2}",
                   lambda s, slot=slot, t=t: s.dma_start(slot, cnt_d[t]),
                   reads=[("cntd", t)], writes=[("cntr", t % 2)])

        for c in range(CH):
            wb = WB[:, c * 16:(c + 1) * 16]
            pl.comp("vector", lambda v, wb=wb, c=c: v.tensor_scalar(
                wb, wb, PQ[:, c:c + 1], None, ALU.mult),
                reads=[("pq", c), ("WB", c)], writes=[("WB", c)])
        for dc in range(CH):
            for fc in range(CH):
                lt = AB2[:, fc * 1024 + dc * 128: fc * 1024 + (dc + 1) * 128]
                rh = WB[:, fc * 16:(fc + 1) * 16]
                pl.comp("tensor", lambda te, lt=lt, rh=rh, dc=dc, fc=fc: te.matmul(
                    PS[4][:, dc * 16:(dc + 1) * 16], lt, rh,
                    start=(fc == 0), stop=(fc == 7)),
                    reads=a2keys(fc * 1024 + dc * 128, 128) + [("WB", fc)],
                    writes=[("ps", 4)])
        pl.comp("vector", lambda v: v.tensor_copy(WKB[:], PS[4][:, 0:128]),
                reads=[("ps", 4)], writes=[("WKB",)])
        BKFB = SQ[:, 16:24]
        for c in range(CH):
            lt = BK[:, c * 128:(c + 1) * 128]
            ot = PS[5][:, c:c + 1]
            pl.comp("tensor", lambda te, ot=ot, lt=lt: te.matmul(
                ot, lt, BIAS3[32:33, 1088:1089], start=True, stop=True),
                reads=[("bk",), ("ones3",)], writes=[("ps", 5)])
        pl.comp("vector", lambda v: v.tensor_copy(BKFB, PS[5][:, 0:8]),
                reads=[("ps", 5)], writes=[("bkfb",), ("pqrowb",)])
        for c in range(CH):
            pl.comp("tensor", lambda te, c=c: te.matmul(
                PS[5][0:16, 0:1], WB[:, c * 16:(c + 1) * 16], BKFB[:, c:c + 1],
                start=(c == 0), stop=False),
                reads=[("WB", c), ("bkfb",)], writes=[("ps", 5)])
        pl.comp("tensor", lambda te: te.matmul(
            PS[5][0:16, 0:1], BKA, BIAS3[32:33, 1088:1089], start=False, stop=True),
            reads=[("bka",), ("ones3",)], writes=[("ps", 5)])
        BIASK = RS[:, 4:5]
        pl.comp("vector", lambda v: v.tensor_scalar(
            BIASK, PS[5][0:16, 0:1], SCALE, None, ALU.mult),
            reads=[("ps", 5)], writes=[("rs",)])
        p0_xnload(12, extra=[("SC1",)])
        p0_xnload(13, extra=[("SC1",)])
        for b in range(NB):
            if b < 12:
                base = 8192 + b * 2048
                xnb = AB2[:, base: base + 2048]
                kxn = lambda dc, base=base: a2keys(base + dc * 256, 256)
            else:
                xnb = SCA[:, (b % 2) * 2048:(b % 2) * 2048 + 2048]
                kxn = lambda dc, b=b: [("xnb", b % 2)]
            for dc in range(CH):
                lt = WKB[:, dc * 16:(dc + 1) * 16]
                rh = xnb[:, dc * 256:(dc + 1) * 256]
                pl.comp("tensor", lambda te, lt=lt, rh=rh, dc=dc: te.matmul(
                    PS[4][0:16, 0:256], lt, rh, start=(dc == 0), stop=(dc == 7)),
                    reads=[("WKB",)] + kxn(dc), writes=[("ps", 4)])
            pl.comp("vector", lambda v, b=b: v.tensor_scalar(
                SC2[:, b * 256:(b + 1) * 256], PS[4][0:16, 0:256],
                SCALE, BIASK, ALU.mult, ALU.add),
                reads=[("ps", 4), ("rs",)], writes=[("SC2k", b)])
            if b in (12, 13):
                p0_xnload(b + 2, extra=[("SC1",)])
        for bb in range(8):
            base = 8192 + bb * 2048
            pl.dma("gpsimd", f"lmq{bb % 2}",
                   lambda g, bb=bb, base=base: g.dma_start(
                       AB2[:, base: base + 2048], mq_d[bb]),
                   reads=[("mqd", bb)], writes=a2keys(base, 2048))
        for t in range(2):
            ld_x3(t)
            ld_cnt3(t)

        for d in range(8):
            cast(Wk_d[d * 128:(d + 1) * 128, :],
                 AB2[:, 24576 + d * 1024: 24576 + (d + 1) * 1024],
                 a2keys(24576 + d * 1024, 1024))
        cast_flush()

        pool_phase("k", SC2, [("SC2k", b) for b in range(NB)], SC1,
                   [("SC1e",), ("SC1",), ("xnb", 0), ("xnb", 1)], AB2, 24576,
                   a2keys(24576, 8192))
        PKROW = YS[0:1, 0:1024]
        pl.comp("vector", lambda v: v.tensor_tensor(
            PKROW[:, 0:512], PS[5][0:1, 0:512], BK[:, 0:512], ALU.add),
            reads=[("ps", 5), ("bk",)], writes=[("pkrow",)])
        pl.comp("vector", lambda v: v.tensor_tensor(
            PKROW[:, 512:1024], PS[4][0:1, 0:512], BK[:, 512:1024], ALU.add),
            reads=[("ps", 4), ("bk",)], writes=[("pkrow",)])
        pl.comp("vector", lambda v: v.tensor_mul(PKROW, PKROW, PQROW),
                reads=[("pkrow",), ("pqrow",)], writes=[("pkrow",)])
        PKROWB = SQ[0:1, 0:1024]
        pl.comp("vector", lambda v: v.tensor_copy(PKROWB, PKROW),
                reads=[("pkrow",)], writes=[("pkrowb",), ("mskd",), ("bkfrow",)])
        row_to_fm(PKROWB, [("pkrowb",)], PK, "pk")
        MQ_SLOT_KEYS = {12: [("xnb", 0), ("SC1e",), ("SC1",)],
                        13: [("xnb", 1), ("SC1e",), ("SC1",)],
                        14: [("xtb", 0)] + [("SC2k", b_) for b_ in range(8)],
                        15: [("xtb", 1)] + [("SC2k", b_) for b_ in range(8, 16)]}
        for bb in range(8, NB):
            if bb < 12:
                base = 8192 + bb * 2048
                dst, wkeys = AB2[:, base: base + 2048], a2keys(base, 2048)
            else:
                dst = SCA[:, (bb - 12) * 2048:(bb - 11) * 2048]
                wkeys = MQ_SLOT_KEYS[bb]
            pl.dma("gpsimd", f"lmq{bb % 2}",
                   lambda g, bb=bb, dst=dst: g.dma_start(dst, mq_d[bb]),
                   reads=[("mqd", bb)], writes=wkeys)
        for d in range(8):
            cast(Wt_d[d * 128:(d + 1) * 128, :], AM[:, d * 1024:(d + 1) * 1024],
                 [("WSM", d)])
        cast_flush()
        for f in range(8):
            wt = AM[:, f * 1024:(f + 1) * 1024]
            pl.comp("vector", lambda v, wt=wt, f=f: v.tensor_scalar(
                wt, wt, PK[:, f:f + 1], None, ALU.mult),
                reads=[("pk", f), ("WSM", f)], writes=[("WSM", f)])

        pl.barrier()

        def mq_ap(bb):
            if bb < 12:
                return AB2[:, 8192 + bb * 2048: 8192 + (bb + 1) * 2048]
            return SCA[:, (bb - 12) * 2048:(bb - 11) * 2048]

        def mq_keys(bb):
            if bb < 12:
                return a2keys(8192 + bb * 2048, 2048)
            return [MQ_SLOT_KEYS[bb][0]]

        for t in range(NT3):
            bb = t // 2
            kmqb = mq_keys(bb)[0]

            def mqsl(c, t=t, bb=bb):
                return mq_ap(bb)[:, c * 256 + (t % 2) * 128: c * 256 + (t % 2) * 128 + 128]
            pset = 0 if t % 2 == 0 else 2
            b0, b1 = PS[pset], PS[pset + 1]
            kb0, kb1 = ("ps", pset), ("ps", pset + 1)
            for c in range(CH):
                lt = mqsl(c)
                for nh in range(2):
                    bank, kb = (b0, kb0) if nh == 0 else (b1, kb1)
                    rh = AM[:, c * 1024 + nh * 512: c * 1024 + nh * 512 + 512]
                    pl.comp("tensor", lambda te, bank=bank, lt=lt, rh=rh, c=c: te.matmul(
                        bank[:], lt, rh, start=(c == 0), stop=False),
                        reads=[kmqb, ("WSM", c)], writes=[kb])
            for c in range(CH):
                bank, kb = (b0, kb0) if c < 4 else (b1, kb1)
                reg = bank[:, (c % 4) * 128:(c % 4) * 128 + 128]
                lt = mqsl(c)
                pl.comp("tensor", lambda te, reg=reg, lt=lt: te.matmul(
                    reg, lt, I128[:], start=False, stop=False),
                    reads=[kmqb, ("I128",)], writes=[kb])
            for nh in range(2):
                bank, kb = (b0, kb0) if nh == 0 else (b1, kb1)
                rh = BT[:, nh * 512:(nh + 1) * 512]
                pl.comp("tensor", lambda te, bank=bank, rh=rh: te.matmul(
                    bank[:], BIAS3[64:65, 1088:1216], rh, start=False, stop=True),
                    reads=[("bt",), ("ones3",)], writes=[kb])
            cslot = AM[:, 10240 + (t % 2) * 1024: 10240 + (t % 2) * 1024 + 1024]
            gbuf = YS[:, (t % 2) * 1024:(t % 2) * 1024 + 1024]
            galias = ("pkrow",) if t % 2 == 0 else ("pqrow",)
            pl.comp("scalar", lambda sc, cslot=cslot, gbuf=gbuf: sc.activation(
                gbuf, cslot, AF.Silu, scale=0.2),
                reads=[("cntr", t % 2)], writes=[("g3", t % 2), galias])
            ot = OB[:, (t % 2) * 1024:(t % 2 + 1) * 1024]
            for nh in range(2):
                bank, kb = (b0, kb0) if nh == 0 else (b1, kb1)
                pl.comp("vector", lambda v, ot=ot, bank=bank, gbuf=gbuf, nh=nh:
                        v.tensor_mul(ot[:, nh * 512:(nh + 1) * 512], bank[:],
                                     gbuf[:, nh * 512:(nh + 1) * 512]),
                        reads=[kb, ("g3", t % 2)],
                        writes=[("ob3", t % 2), ("stg", t % 2)])
            xt = XF[:, (t % 2) * 1024:(t % 2 + 1) * 1024]
            pl.comp("vector", lambda v, ot=ot, xt=xt: v.tensor_add(ot, ot, xt),
                    reads=[("xf3", t % 2), ("ob3", t % 2)], writes=[("ob3", t % 2)])
            pl.dma("gpsimd", f"so{t % 2}",
                   lambda g, ot=ot, t=t: g.dma_start(out_d[t * 128:(t + 1) * 128, :], ot),
                   reads=[("ob3", t % 2)], writes=[("outd", t)])
            if t + 2 < NT3:
                ld_x3(t + 2)
                ld_cnt3(t + 2)

        pl.wait_all("sync", list(SY_SEMS))
        pl.wait_all("gpsimd", list(GP_SEMS))
        pl.wait_all("scalar", [s_ for s_ in SC_SEMS if s_ != "act"])

        def replay(eng_name):
            def body(eng):
                for waits, fn, sem, n in pl.ops[eng_name]:
                    for s_, v_ in waits:
                        eng.wait_ge(sems[s_], v_)
                    if fn is not None:
                        inst = fn(eng)
                        inst.then_inc(sems[sem], n)
            return body

        block.sync(replay("sync"))
        block.tensor(replay("tensor"))
        block.scalar(replay("scalar"))
        block.vector(replay("vector"))
        block.gpsimd(replay("gpsimd"))

    return nc


_NC_CACHE = None


def kernel(**inputs):
    global _NC_CACHE
    x = np.ascontiguousarray(inputs["L0_attn_hidden_states"], dtype=np.float32)
    Wk = np.ascontiguousarray(inputs["L0_attn_Wk"], dtype=np.float32)
    wmap = {
        "Wq": np.ascontiguousarray(inputs["L0_attn_Wq"], dtype=np.float32),
        "Wk": Wk,
        "WkT": np.ascontiguousarray(Wk.T),
        "Wt": np.ascontiguousarray(inputs["L0_attn_Wt"], dtype=np.float32),
        "Wqa": np.ascontiguousarray(inputs["L0_attn_Wqa"], dtype=np.float32),
        "Wka": np.ascontiguousarray(inputs["L0_attn_Wka"], dtype=np.float32),
        "WfcT": np.ascontiguousarray(np.asarray(inputs["L0_ffn_W_fc"], dtype=np.float32).T),
        "WprojT": np.ascontiguousarray(np.asarray(inputs["L0_ffn_W_proj"], dtype=np.float32).T),
        "bq": np.ascontiguousarray(np.asarray(inputs["L0_attn_bq"], dtype=np.float32).reshape(1, D)),
        "bk": np.ascontiguousarray(np.asarray(inputs["L0_attn_bk"], dtype=np.float32).reshape(1, D)),
        "bt": np.ascontiguousarray(np.asarray(inputs["L0_attn_bt"], dtype=np.float32).reshape(1, D)),
        "bqa": np.ascontiguousarray(np.asarray(inputs["L0_attn_bqa"], dtype=np.float32).reshape(1, H)),
        "bka": np.ascontiguousarray(np.asarray(inputs["L0_attn_bka"], dtype=np.float32).reshape(1, H)),
    }
    if _NC_CACHE is None:
        _NC_CACHE = build_nc()
    nc = _NC_CACHE
    in_maps = [dict(wmap, x=np.ascontiguousarray(x[b])) for b in range(8)]
    res = run_bass_kernel_spmd(nc, in_maps, core_ids=list(range(8)))
    return np.stack([np.asarray(r["out"], dtype=np.float32) for r in res.results])


# version 26
# speedup vs baseline: 1.0432x; 1.0003x over previous
import numpy as np
import ml_dtypes
from contextlib import ExitStack

import concourse.bass as bass
import concourse.mybir as mybir
from concourse.bass_utils import run_bass_kernel_spmd

F32 = mybir.dt.float32
BF16 = mybir.dt.bfloat16
AF = mybir.ActivationFunctionType
ALU = mybir.AluOpType
AX = mybir.AxisListType

S, D, H, DH, CH = 4096, 1024, 16, 64, 8
SCALE = 1.0 / 8.0
EPS = 1e-5

_a = 0.0
for _ in range(5):
    _a = _a * 0.9 + 1.0
TAUS = [c / _a for c in (1.0, 1.9, 2.81, 3.71, 4.439)]

NB, T0 = 16, 256
NT3 = 32

ENG_SEM = {"tensor": "pe", "scalar": "act", "vector": "dve", "gpsimd": "pool"}
SEM_NAMES = ("pe", "act", "dve", "pool", "lc",
             "lw0", "lw1", "lw2", "lw3", "lw4", "lw5", "lw6", "lw7",
             "ln0", "ln1", "ln2", "ln3", "ln4", "ln5", "ln6", "ln7",
             "l30", "l31", "l32",
             "lb0", "lb1",
             "ltm0", "ltm1",
             "lmq0", "lmq1",
             "lm0", "lm1", "lm2", "lm3",
             "lqs",
             "cr0", "cr1",
             "sx0", "sx1",
             "tm0", "tm1",
             "mq0", "mq1",
             "sq",
             "sc0", "sc1",
             "so0", "so1")
DMA_SEMS = set(SEM_NAMES[4:])


class Plan:
    def __init__(self):
        self.ops = {k: [] for k in ("tensor", "scalar", "vector", "gpsimd", "sync")}
        self.cnt = {k: 0 for k in SEM_NAMES}
        self.res = {}
        self.hiwater = {k: {} for k in self.ops}

    def _need_wait(self, eng, sem, val):
        hw = self.hiwater[eng]
        if hw.get(sem, 0) >= val:
            return False
        hw[sem] = val
        return True

    STRICT = {"lns", "rs", "pq", "pk", "ceps", "ones3",
              "bq", "bk", "bt", "bqa", "bka", "bstg"}

    def _op(self, eng, fn, reads, writes, sem, n):
        waits = {}
        my_val = self.cnt[sem] + n

        def add(dep_eng, dep_sem, dep_val, strict):
            if dep_eng == eng and dep_sem == "pe" and sem == "pe":
                return
            if dep_eng == eng and dep_sem in DMA_SEMS and sem == dep_sem:
                return
            waits[dep_sem] = max(waits.get(dep_sem, 0), dep_val)

        for k in reads:
            st = self.res.get(k)
            if st and st["w"]:
                add(*st["w"], k[0] in self.STRICT)
        for k in writes:
            st = self.res.get(k)
            if st:
                strict = k[0] in self.STRICT
                if st["w"]:
                    add(*st["w"], strict)
                for r in st["r"]:
                    add(*r, strict)
        for k in reads:
            st = self.res.get(k)
            if st is not None:
                st["r"].append((eng, sem, my_val))
            else:
                self.res[k] = {"w": None, "r": [(eng, sem, my_val)]}
        for k in writes:
            self.res[k] = {"w": (eng, sem, my_val), "r": []}
        self.cnt[sem] = my_val
        real_waits = [(s, v) for s, v in waits.items() if self._need_wait(eng, s, v)]
        self.ops[eng].append((real_waits, fn, sem, n))
        return my_val

    def comp(self, eng, fn, reads=(), writes=()):
        return self._op(eng, fn, reads, writes, ENG_SEM[eng], 1)

    def dma(self, queue, sem, fn, reads=(), writes=()):
        return self._op(queue, fn, reads, writes, sem, 16)

    def patch_total(self, keys, eng, sem):
        tot = (eng, sem, self.cnt[sem])
        for k in keys:
            if k in self.res:
                self.res[k]["w"] = tot

    def wait_all(self, eng, sems_):
        for s in sems_:
            v = self.cnt[s]
            if v and self._need_wait(eng, s, v):
                self.ops[eng].append(([(s, v)], None, None, 0))

    def barrier(self):
        for e in self.ops:
            self.wait_all(e, [s for s in self.cnt if s != ENG_SEM.get(e)])


def build_nc(dbg=False):
    nc = bass.Bass("TRN2", target_bir_lowering=False)

    x_d = nc.dram_tensor("x", [S, D], F32, kind="ExternalInput")
    Wq_d = nc.dram_tensor("Wq", [D, D], F32, kind="ExternalInput")
    Wk_d = nc.dram_tensor("Wk", [D, D], F32, kind="ExternalInput")
    WkT_d = nc.dram_tensor("WkT", [D, D], F32, kind="ExternalInput")
    Wt_d = nc.dram_tensor("Wt", [D, D], F32, kind="ExternalInput")
    Wqa_d = nc.dram_tensor("Wqa", [D, H], F32, kind="ExternalInput")
    Wka_d = nc.dram_tensor("Wka", [D, H], F32, kind="ExternalInput")
    WfcT_d = nc.dram_tensor("WfcT", [D, 4 * D], F32, kind="ExternalInput")
    WprojT_d = nc.dram_tensor("WprojT", [4 * D, D], F32, kind="ExternalInput")
    bq_d = nc.dram_tensor("bq", [1, D], F32, kind="ExternalInput")
    bk_d = nc.dram_tensor("bk", [1, D], F32, kind="ExternalInput")
    bt_d = nc.dram_tensor("bt", [1, D], F32, kind="ExternalInput")
    bqa_d = nc.dram_tensor("bqa", [1, H], F32, kind="ExternalInput")
    bka_d = nc.dram_tensor("bka", [1, H], F32, kind="ExternalInput")
    out_d = nc.dram_tensor("out", [S, D], F32, kind="ExternalOutput")

    xn_d = nc.dram_tensor("xn_scratch", [NB, 128, CH * T0], BF16, kind="Internal")
    xtm_d = nc.dram_tensor("xtm_scratch", [NT3, 128, D], BF16, kind="Internal")
    mq_d = nc.dram_tensor("mq_scratch", [NB, 128, CH * T0], BF16, kind="Internal")
    qs_d = nc.dram_tensor("qs_scratch", [H, S], BF16, kind="Internal")
    cnt_d = nc.dram_tensor("cnt_scratch", [NT3, 128, D], BF16, kind="Internal")

    I128_d = nc.inline_tensor(np.eye(128, dtype=ml_dtypes.bfloat16), "ident128")
    ONES_d = nc.inline_tensor(np.ones((16, 512), dtype=ml_dtypes.bfloat16), "ones16")
    mask_np = np.zeros((H, D), dtype=ml_dtypes.bfloat16)
    for f in range(D):
        mask_np[f // DH, f] = 1.0
    MASK_d = nc.inline_tensor(mask_np, "diagmask")

    with ExitStack() as ctx:
        e = ctx.enter_context
        AB1 = e(nc.sbuf_tensor("AB1", [128, 32768], BF16))
        AB2 = e(nc.sbuf_tensor("AB2", [128, 32768], BF16))
        AM = e(nc.sbuf_tensor("AM", [128, 12288], BF16))
        SCA = e(nc.sbuf_tensor("SCA", [128, 8192], BF16))
        XF = e(nc.sbuf_tensor("XF", [128, 3072], F32))
        SQ = e(nc.sbuf_tensor("SQ", [128, 1024], BF16))
        RB = e(nc.sbuf_tensor("RB", [128, 512], F32))
        YS = e(nc.sbuf_tensor("YS", [128, 2048], F32))
        OB = e(nc.sbuf_tensor("OB", [128, 2048], F32))
        LNS = e(nc.sbuf_tensor("LNS", [128, 16], F32))
        RS = e(nc.sbuf_tensor("RS", [16, 8], F32))
        PQ = e(nc.sbuf_tensor("PQ", [128, 8], F32))
        PK = e(nc.sbuf_tensor("PK", [128, 8], F32))
        CEPS = e(nc.sbuf_tensor("CEPS", [128, 1], F32))
        WA = e(nc.sbuf_tensor("WA", [128, 128], BF16))
        WB = e(nc.sbuf_tensor("WB", [128, 128], BF16))
        WKB = e(nc.sbuf_tensor("WKB", [128, 128], BF16))
        BIAS3 = e(nc.sbuf_tensor("BIAS3", [128, 1600], BF16))
        I128 = e(nc.sbuf_tensor("I128", [128, 128], BF16))
        ONES = e(nc.sbuf_tensor("ONES", [16, 512], BF16))
        MASK = e(nc.sbuf_tensor("MASK", [16, 1024], BF16))
        QSS = e(nc.sbuf_tensor("QSS", [16, 256], BF16))
        PS = [e(nc.psum_tensor(f"PS{i}", [128, 512], F32)) for i in range(8)]
        sems = {n: e(nc.semaphore(n)) for n in SEM_NAMES}
        GP_SEMS = ("pool", "lw0", "lw1", "lw4", "lw6", "so0", "so1",
                   "ln0", "ln2", "ln4", "ln6", "lmq0", "lmq1",
                   "lm0", "lm1", "lm2", "lm3")
        SY_SEMS = ("lc", "lw2", "lw3", "lw5", "lw7", "l30", "l31", "l32",
                   "lb0", "lb1", "ltm0", "ltm1", "lqs",
                   "ln1", "ln3", "ln5", "ln7",
                   "cr0", "cr1")
        SC_SEMS = ("act", "sx0", "sx1", "tm0", "tm1", "mq0", "mq1", "sq",
                   "sc0", "sc1")
        for eng_obj, own in ((nc.tensor, ("pe",)), (nc.scalar, SC_SEMS),
                             (nc.vector, ("dve",)), (nc.gpsimd, GP_SEMS),
                             (nc.sync, SY_SEMS)):
            for s_ in own:
                eng_obj.sem_clear(sems[s_])
        nc.all_engine_barrier()
        block = e(nc.Block())

        pl = Plan()
        ONE5 = ONES[0:1, :]

        BQ = BIAS3[0:1, 0:1024]
        BK = BIAS3[32:33, 0:1024]
        BT = BIAS3[64:65, 0:1024]
        BQA = BIAS3[0:1, 1024:1040]
        BKA = BIAS3[32:33, 1024:1040]
        pl.dma("sync", "lc", lambda s: s.dma_start(I128[:], I128_d.ap()), writes=[("I128",)])
        pl.dma("sync", "lc", lambda s: s.dma_start(ONES[:], ONES_d.ap()), writes=[("ONES",)])
        pl.dma("sync", "lc", lambda s: s.dma_start(MASK[:], MASK_d.ap()), writes=[("MASK",)])
        bias_jobs = []
        for nm, dsrc, dst, w, stg, alias in (
                ("bq", bq_d, BQ, 1024, OB[0:1, 0:1024], ("stg", 0)),
                ("bk", bk_d, BK, 1024, OB[0:1, 1024:2048], ("stg", 1)),
                ("bt", bt_d, BT, 1024, YS[0:1, 0:1024], ("ys", 0)),
                ("bqa", bqa_d, BQA, 16, YS[0:1, 1024:1040], ("ys", 1)),
                ("bka", bka_d, BKA, 16, YS[0:1, 1056:1072], ("ys", 1))):
            pl.dma("sync", "lc", lambda s, dsrc=dsrc, stg=stg: s.dma_start(stg, dsrc.ap()),
                   writes=[("bstg", nm)])
            bias_jobs.append((nm, dst, w, stg, alias))
        lc_tot = ("sync", "lc", pl.cnt["lc"])
        for k in [("I128",), ("ONES",), ("MASK",)] + [("bstg", nm) for nm, *_ in bias_jobs]:
            pl.res[k]["w"] = lc_tot
        BIAS_FINE = {"bq": [("stgp", 0), ("stgp", 2)] + [("stgn", i) for i in range(4)],
                     "bk": [("stgp", 4), ("stgp", 6)] + [("stgn", i) for i in range(4, 8)],
                     "bt": [("ysp", 1), ("ysp", 3)], "bqa": [("ysp", 5)],
                     "bka": [("ysp", 5)]}
        for nm, dst, w, stg, alias in bias_jobs:
            pl.comp("vector", lambda v, dst=dst, stg=stg, w=w: v.tensor_copy(dst[:, 0:w], stg),
                    reads=[("bstg", nm)] + BIAS_FINE[nm], writes=[(nm,), alias])
        pl.comp("vector", lambda v: v.memset(BIAS3[:, 1088:1600], 1.0),
                writes=[("ones3",)])
        pl.comp("vector", lambda v: v.memset(CEPS[:], EPS), writes=[("ceps",)])

        ppc = [0]
        cast_pend = []
        CAST_SLOTS = [
            ("gpsimd", "lw0", OB[:, 0:512], ("stg", 0)),
            ("sync", "lw2", YS[:, 0:512], ("ys", 0)),
            ("sync", "lw5", OB[:, 512:1024], ("stg", 0)),
            ("gpsimd", "lw4", YS[:, 512:1024], ("ys", 0)),
            ("gpsimd", "lw1", OB[:, 1024:1536], ("stg", 1)),
            ("sync", "lw3", YS[:, 1024:1536], ("ys", 1)),
            ("sync", "lw7", OB[:, 1536:2048], ("stg", 1)),
            ("gpsimd", "lw6", YS[:, 1536:2048], ("ys", 1)),
        ]
        XFS = [XF[:, 2048:2560], XF[:, 2560:3072]]
        CAST_SLOTS_N = [
            ("gpsimd", "ln0", OB[:, 0:512], ("stg", 0), [("stgp", 0)]),
            ("sync", "ln1", OB[:, 512:1024], ("stg", 0), [("stgp", 2)]),
            ("gpsimd", "ln2", OB[:, 1024:1536], ("stg", 1), [("stgp", 4)]),
            ("sync", "ln3", OB[:, 1536:2048], ("stg", 1), [("stgp", 6)]),
            ("gpsimd", "ln4", XFS[0], ("xfs", 0), []),
            ("sync", "ln5", XFS[1], ("xfs", 1), []),
        ]
        cast_nslots = [4]
        cast_ceng = [True]

        def cast_piece(dsrc_ap, dst_ap, dst_keys):
            mode = cast_nslots[0]
            if mode == 8:
                sl = ppc[0] % 8
                queue, sem, stg_full, skey = CAST_SLOTS[sl]
                wkeys = [(skey[0] + "p", sl)]
            else:
                rot = (0, 1, 2, 3, 4, 5) if mode == 4 else (0, 2, 4)
                sl = rot[ppc[0] % len(rot)]
                queue, sem, stg_full, skey, okeys = CAST_SLOTS_N[sl]
                wkeys = [("stgn", sl)] + okeys
            stg = stg_full[:, 0:dsrc_ap.shape[1]]
            while len(cast_pend) >= min(4, 8 if mode == 8 else
                                        (6 if mode == 4 else 3)):
                cast_pend.pop(0)()
            pl.dma(queue, sem, lambda g, a=dsrc_ap, s=stg: g.dma_start(s, a),
                   reads=[], writes=wkeys)
            ceng = ("vector", "scalar")[ppc[0] % 2] if cast_ceng[0] else \
                   ("scalar", "gpsimd")[ppc[0] % 2]
            meth = (lambda e_, s_, d_: e_.copy(d_, s_)) if ceng == "scalar" else \
                   (lambda e_, s_, d_: e_.tensor_copy(d_, s_))
            cast_pend.append(lambda s=stg, d=dst_ap, k=list(dst_keys),
                             rk=wkeys[0], en=ceng, m=meth, sk=skey: pl.comp(
                en, lambda eng, s=s, d=d, m=m: m(eng, s, d),
                reads=[rk], writes=k + [sk]))
            ppc[0] += 1

        def cast(dsrc_ap, dst_ap, dst_keys):
            w = dsrc_ap.shape[1]
            if w > 512:
                for o in range(0, w, 512):
                    cast_piece(dsrc_ap[:, o:o + 512], dst_ap[:, o:o + 512], dst_keys)
            else:
                cast_piece(dsrc_ap, dst_ap, dst_keys)

        def cast_flush():
            while cast_pend:
                cast_pend.pop(0)()

        def a1(col, sz):
            return AB1[:, col: col + sz]

        def a1keys(col, sz):
            return [("A1", i) for i in range(col // 512, (col + sz + 511) // 512)]

        def a2keys(col, sz):
            return [("A2", i) for i in range(col // 512, (col + sz + 511) // 512)]

        def wsmkeys(col, sz):
            return [("WSM", i) for i in range(col // 1024, (col + sz + 1023) // 1024)]

        def pa_xload(gt):
            slot = XF[:, (gt % 2) * 1024:(gt % 2) * 1024 + 1024]
            pl.dma("sync", f"l3{gt % 2}",
                   lambda s, slot=slot, gt=gt: s.dma_start(
                       slot, x_d[gt * 128:(gt + 1) * 128, :]),
                   writes=[("x0", gt % 2)])

        def pa_ln_thunks(gt):
            th = []
            b, t = gt // 2, gt % 2
            xt = XF[:, (gt % 2) * 1024:(gt % 2) * 1024 + 1024]
            kx = ("x0", gt % 2)
            xtb = SCA[:, 4096 + (b % 2) * 2048: 4096 + (b % 2) * 2048 + 2048]
            kxtb = ("xtb", b % 2)
            par = t
            s1 = LNS[:, par * 8 + 0: par * 8 + 1]
            m_ = LNS[:, par * 8 + 1: par * 8 + 2]
            ssq = LNS[:, par * 8 + 2: par * 8 + 3]
            m2 = LNS[:, par * 8 + 3: par * 8 + 4]
            v_ = LNS[:, par * 8 + 4: par * 8 + 5]
            lv = LNS[:, par * 8 + 5: par * 8 + 6]
            r_ = LNS[:, par * 8 + 6: par * 8 + 7]
            kl = ("lns", par)
            th.append(lambda: pl.comp("vector", lambda v: v.reduce_sum(s1, xt, axis=AX.X),
                      reads=[kx], writes=[kl]))
            th.append(lambda: pl.comp("vector", lambda v: v.tensor_scalar(
                m_, s1, 1.0 / D, None, ALU.mult), reads=[kl], writes=[kl]))
            th.append(lambda: pl.comp("scalar", lambda sc: sc.activation(SQ[:], xt, AF.Square),
                      reads=[kx], writes=[("sq",)]))
            th.append(lambda: pl.comp("vector", lambda v: v.reduce_sum(ssq, SQ[:], axis=AX.X),
                      reads=[("sq",)], writes=[kl]))
            th.append(lambda: pl.comp("vector", lambda v: v.tensor_mul(m2, m_, m_),
                      reads=[kl], writes=[kl]))
            th.append(lambda: pl.comp("vector", lambda v: v.scalar_tensor_tensor(
                v_, ssq, 1.0 / D, m2, ALU.mult, ALU.subtract), reads=[kl], writes=[kl]))
            th.append(lambda: pl.comp("scalar", lambda sc: sc.activation(
                lv, v_, AF.Ln, bias=CEPS[:]), reads=[kl, ("ceps",)], writes=[kl]))
            th.append(lambda: pl.comp("scalar", lambda sc: sc.activation(
                r_, lv, AF.Exp, scale=-0.5), reads=[kl], writes=[kl]))
            xtt = xtb[:, t * 1024:(t + 1) * 1024]
            th.append(lambda: pl.comp("vector", lambda v: v.tensor_scalar(
                xtt, xt, m_, r_, ALU.subtract, ALU.mult),
                reads=[kx, kl], writes=[kxtb]))
            return th

        def pa_tr_thunks(b):
            th = []
            xnb = SCA[:, (b % 2) * 2048:(b % 2) * 2048 + 2048]
            xtb = SCA[:, 4096 + (b % 2) * 2048: 4096 + (b % 2) * 2048 + 2048]
            kxnb = ("xnb", b % 2)
            kxtb = ("xtb", b % 2)

            def mkpair(cp):
                def go():
                    bank = PS[6 + cp % 2]
                    kps = ("ps", 6 + cp % 2)
                    for ci in range(2):
                        c = 2 * cp + ci
                        for t in range(2):
                            lt = xtb[:, t * 1024 + c * 128: t * 1024 + (c + 1) * 128]
                            ot = bank[:, ci * 256 + t * 128: ci * 256 + (t + 1) * 128]
                            pl.comp("tensor", lambda te, ot=ot, lt=lt: te.matmul(
                                ot, lt, I128[:], start=True, stop=True),
                                reads=[kxtb, ("I128",)], writes=[kps])
                    dst = xnb[:, 2 * cp * 256: 2 * cp * 256 + 512]
                    pl.comp("vector", lambda v, dst=dst, cp=cp: v.tensor_copy(
                        dst, PS[6 + cp % 2][:]), reads=[("ps", 6 + cp % 2)],
                        writes=[kxnb])
                return go
            for cp in range(4):
                th.append(mkpair(cp))

            def stores():
                pl.dma("scalar", f"sx{b % 2}",
                       lambda g: g.dma_start(xn_d[b], xnb),
                       reads=[kxnb], writes=[("xnd", b)])
                for t in range(2):
                    src_ = xtb[:, t * 1024:(t + 1) * 1024]
                    gt = 2 * b + t
                    pl.dma("scalar", f"tm{gt % 2}",
                           lambda g, src_=src_, gt=gt: g.dma_start(xtm_d[gt], src_),
                           reads=[kxtb], writes=[("xtmd", gt)])
            th.append(stores)
            return th

        def pa_mq_group(b, eo):
            bank = PS[6 + eo % 2]
            kb = ("ps", 6 + eo % 2)
            xnb = SCA[:, (b % 2) * 2048:(b % 2) * 2048 + 2048]
            kxnb = ("xnb", b % 2)
            for d in range(8):
                lt = AM[:, d * 1024 + eo * 128: d * 1024 + (eo + 1) * 128]
                rh = xnb[:, d * 256:(d + 1) * 256]
                pl.comp("tensor", lambda te, bank=bank, lt=lt, rh=rh, d=d: te.matmul(
                    bank[:, 0:256], lt, rh, start=(d == 0), stop=False),
                    reads=wsmkeys(d * 1024 + eo * 128, 128) + [kxnb], writes=[kb])
            lt = BQ[:, eo * 128:(eo + 1) * 128]
            pl.comp("tensor", lambda te, bank=bank, lt=lt: te.matmul(
                bank[:, 0:256], lt, BIAS3[0:1, 1088:1344], start=False, stop=True),
                reads=[("bq",), ("ones3",)], writes=[kb])
            mqs = SCA[:, 4096 + (b % 2) * 2048 + eo * 256: 4096 + (b % 2) * 2048 + (eo + 1) * 256]
            pl.comp("vector", lambda v, mqs=mqs, bank=bank: v.tensor_copy(mqs, bank[:, 0:256]),
                    reads=[kb], writes=[("mqs", b % 2, eo), ("xtb", b % 2)])

        def pa_qs(b):
            xts = SCA[:, 4096 + (b % 2) * 2048: 4096 + (b % 2) * 2048 + 2048]
            for eo in range(CH):
                lt = WA[:, eo * 16:(eo + 1) * 16]
                rh = xts[:, eo * 256:(eo + 1) * 256]
                pl.comp("tensor", lambda te, lt=lt, rh=rh, eo=eo: te.matmul(
                    PS[6][0:16, 0:256], lt, rh, start=(eo == 0), stop=False),
                    reads=[("WA", eo), ("mqs", b % 2, eo), ("xtb", b % 2)],
                    writes=[("ps", 6)])
            pl.comp("tensor", lambda te: te.matmul(
                PS[6][0:16, 0:256], BQA, BIAS3[0:1, 1088:1344], start=False, stop=True),
                reads=[("bqa",), ("ones3",)], writes=[("ps", 6)])
            pl.comp("scalar", lambda sc: sc.activation(
                QSS[:], PS[6][0:16, 0:256], AF.Copy, scale=SCALE),
                reads=[("ps", 6)], writes=[("qss",)])
            pl.dma("scalar", "sq",
                   lambda g, b=b: g.dma_start(qs_d[:, b * 256:(b + 1) * 256], QSS[:]),
                   reads=[("qss",)], writes=[("qsd", b)])
            pl.dma("scalar", f"mq{b % 2}",
                   lambda g, b=b: g.dma_start(mq_d[b], xts),
                   reads=[("mqs", b % 2, eo_) for eo_ in range(CH)] + [("xtb", b % 2)],
                   writes=[("mqd", b)])

        pa_xload(0)
        pa_xload(1)
        for t_ in pa_ln_thunks(0):
            t_()
        pa_xload(2)
        for t_ in pa_ln_thunks(1):
            t_()
        pa_xload(3)
        for t_ in pa_tr_thunks(0):
            t_()

        cast_q = []

        def cast_startup_weights():
            for q in range(4):
                emit = (lambda f: f()) if q == 0 else cast_q.append
                for c in range(CH):
                    emit(lambda c=c, q=q: cast(
                        WfcT_d[c * 128:(c + 1) * 128, q * 1024:(q + 1) * 1024],
                        AB1[:, c * 4096 + q * 1024: c * 4096 + (q + 1) * 1024],
                        a1keys(c * 4096 + q * 1024, 1024)))
                for j in range(q * 8, q * 8 + 8):
                    emit(lambda j=j: cast(
                        WprojT_d[j * 128:(j + 1) * 128, :],
                        AB2[:, j * 1024:(j + 1) * 1024], a2keys(j * 1024, 1024)))
                if q == 0:
                    cast_flush()
                    cast_nslots[0] = 4
                    cast_ceng[0] = False

        cast_nslots[0] = 8
        for d in range(8):
            cast(Wq_d[d * 128:(d + 1) * 128, :], AM[:, d * 1024:(d + 1) * 1024],
                 [("WSM", d)])
        for wsrc, dst, nm in ((Wqa_d, WA, "WA"), (Wka_d, WB, "WB")):
            for d in range(8):
                cast(wsrc[d * 128:(d + 1) * 128, :], dst[:, d * 16:(d + 1) * 16],
                     [(nm, d)])
        cast_flush()
        cast_startup_weights()

        for eo in range(CH):
            pa_mq_group(0, eo)
        pa_qs(0)

        H2_COLS = (8192, 8448, 11776, 12032)

        def p0_xnload(b, extra=()):
            xnb = SCA[:, (b % 2) * 2048:(b % 2) * 2048 + 2048]
            pl.dma("sync", f"ltm{b % 2}",
                   lambda s, b=b: s.dma_start(xnb, xn_d[b]),
                   reads=[("xnd", b)], writes=[("xnb", b % 2)] + list(extra))

        def p0_hmm(b, j):
            hb = PS[4 + j % 2]
            khb = ("ps", 4 + j % 2)
            kxnb = ("xnb", b % 2)
            xnb = SCA[:, (b % 2) * 2048:(b % 2) * 2048 + 2048]
            for c in range(CH):
                lt = AB1[:, c * 4096 + j * 128: c * 4096 + (j + 1) * 128]
                rh = xnb[:, c * 256:(c + 1) * 256]
                pl.comp("tensor", lambda te, hb=hb, lt=lt, rh=rh, c=c: te.matmul(
                    hb[:, 0:256], lt, rh, start=(c == 0), stop=(c == 7)),
                    reads=a1keys(c * 4096 + j * 128, 128) + [kxnb], writes=[khb])
            rr = RB[:, (j % 2) * 256:(j % 2) * 256 + 256]
            krr = ("rb", j % 2)
            pl.comp("scalar", lambda sc, rr=rr, hb=hb: sc.activation(
                rr, hb[:, 0:256], AF.Relu), reads=[khb], writes=[krr])
            g3 = (b * 32 + j) % 4
            h2 = AM[:, H2_COLS[g3]: H2_COLS[g3] + 256]
            pl.comp("vector", lambda v, h2=h2, rr=rr: v.tensor_mul(h2, rr, rr),
                    reads=[krr], writes=[("h2", g3)])

        def p0_ymm(b, j):
            g3 = (b * 32 + j) % 4
            h2 = AM[:, H2_COLS[g3]: H2_COLS[g3] + 256]
            kh2 = ("h2", g3)
            for t in range(2):
                lt = h2[:, t * 128:(t + 1) * 128]
                for nh in range(2):
                    yb = PS[2 * t + nh]
                    rh = AB2[:, j * 1024 + nh * 512: j * 1024 + (nh + 1) * 512]
                    pl.comp("tensor", lambda te, yb=yb, lt=lt, rh=rh, j=j: te.matmul(
                        yb[:], lt, rh, start=(j == 0), stop=(j == 31)),
                        reads=[kh2] + a2keys(j * 1024 + nh * 512, 512),
                        writes=[("ps", 2 * t + nh)])

        def p0_yevac(b):
            for t in range(2):
                yt = YS[:, t * 1024:(t + 1) * 1024]
                for nh in range(2):
                    pl.comp("scalar", lambda sc, yt=yt, t=t, nh=nh: sc.copy(
                        yt[:, nh * 512:(nh + 1) * 512], PS[2 * t + nh][:]),
                        reads=[("ps", 2 * t + nh)], writes=[("ys", t)])

        def p0_spike_thunks(b):
            th = []
            for t in range(2):
                yt = YS[:, t * 1024:(t + 1) * 1024]
                kyt = ("ys", t)
                tile = 2 * b + t
                cslot = AM[:, 8704 + (tile % 2) * 1024: 8704 + (tile % 2) * 1024 + 1024]
                kc = ("cnt", tile % 2)
                spk = AM[:, 10752:11776]
                ks_ = ("spk",)
                th.append(lambda yt=yt, cslot=cslot, kyt=kyt, kc=kc: pl.comp(
                    "vector", lambda v, cslot=cslot, yt=yt: v.tensor_scalar(
                        cslot, yt, TAUS[0], None, ALU.is_gt), reads=[kyt], writes=[kc]))
                for i in range(4):
                    o = spk if i % 2 == 0 else cslot
                    inn = cslot if i % 2 == 0 else spk
                    ki = kc if i % 2 == 0 else ks_
                    ko = ks_ if i % 2 == 0 else kc
                    th.append(lambda yt=yt, o=o, inn=inn, kyt=kyt, ki=ki, ko=ko,
                              tau=TAUS[i + 1]: pl.comp(
                        "vector", lambda v, o=o, yt=yt, inn=inn, tau=tau:
                        v.scalar_tensor_tensor(o, yt, tau, inn, ALU.is_gt, ALU.add),
                        reads=[kyt, ki], writes=[ko]))
                th.append(lambda cslot=cslot, tile=tile, kc=kc: pl.dma(
                    "scalar", f"sc{tile % 2}",
                    lambda g, cslot=cslot, tile=tile: g.dma_start(cnt_d[tile], cslot),
                    reads=[kc], writes=[("cntd", tile)]))
            return th

        def prep_thunks(m):
            th = []
            th += pa_ln_thunks(2 * m)
            if 2 * m + 2 < 2 * NB:
                th.append(lambda m=m: pa_xload(2 * m + 2))
            th += pa_ln_thunks(2 * m + 1)
            if 2 * m + 3 < 2 * NB:
                th.append(lambda m=m: pa_xload(2 * m + 3))
            th += pa_tr_thunks(m)
            th += [lambda m=m, eo=eo_: pa_mq_group(m, eo) for eo_ in range(CH)]
            th.append(lambda m=m: pa_qs(m))
            return th

        pa_q = []
        pending = []
        seq = [(b, j) for b in range(NB) for j in range(32)]
        for gi, (b, j) in enumerate(seq):
            if j == 0 and b + 1 < NB:
                pa_q.extend(prep_thunks(b + 1))
            p0_hmm(b, j)
            for _ in range(3):
                if cast_q:
                    cast_q.pop(0)()
                elif cast_pend:
                    cast_pend.pop(0)()
            if pending and j % 2 == 1 and 2 < j < 29:
                pending.pop(0)()
            if pa_q:
                pa_q.pop(0)()
            if pa_q and j % 2 == 0 and j < 14:
                pa_q.pop(0)()
            if gi > 2:
                pb, pj = seq[gi - 3]
                p0_ymm(pb, pj)
                if pj == 31:
                    p0_yevac(pb)
                    pending = p0_spike_thunks(pb)
        while pa_q:
            pa_q.pop(0)()
        while cast_q:
            cast_q.pop(0)()
        cast_flush()
        for (pb, pj) in seq[-3:]:
            p0_ymm(pb, pj)
            if pj == 31:
                p0_yevac(pb)
                pending = p0_spike_thunks(pb)
        for th in pending:
            th()
        pl.patch_total([("xnd", b) for b in range(0, NB, 2)], "scalar", "sx0")
        pl.patch_total([("xnd", b) for b in range(1, NB, 2)], "scalar", "sx1")
        pl.patch_total([("xtmd", g) for g in range(0, NT3, 2)], "scalar", "tm0")
        pl.patch_total([("xtmd", g) for g in range(1, NT3, 2)], "scalar", "tm1")
        pl.patch_total([("mqd", b) for b in range(0, NB, 2)], "scalar", "mq0")
        pl.patch_total([("mqd", b) for b in range(1, NB, 2)], "scalar", "mq1")
        pl.patch_total([("cntd", t) for t in range(0, NT3, 2)], "scalar", "sc0")
        pl.patch_total([("cntd", t) for t in range(1, NT3, 2)], "scalar", "sc1")

        SC1 = SCA[0:16, 0:4096]
        SC2 = SCA[0:16, 4096:8192]
        pl.dma("sync", "lqs", lambda s: s.dma_start(SC1, qs_d.ap()),
               reads=[("qsd", b) for b in range(NB)],
               writes=[("SC1",), ("xnb", 0), ("xnb", 1)])
        cast_nslots[0] = 2
        for d in range(8):
            cast(WkT_d[d * 128:(d + 1) * 128, :], AB2[:, d * 1024:(d + 1) * 1024],
                 a2keys(d * 1024, 1024))
        cast_flush()
        cast_nslots[0] = 4
        for g in range(NT3):
            dst = AB1[:, g * 1024:(g + 1) * 1024]
            pl.dma("sync", f"ltm{g % 2}",
                   lambda s, dst=dst, g=g: s.dma_start(dst, xtm_d[g]),
                   reads=[("xtmd", g)], writes=a1keys(g * 1024, 1024))
        pl.patch_total([("A1", 2 * g + o) for g in range(0, NT3, 2) for o in (0, 1)],
                       "sync", "ltm0")
        pl.patch_total([("A1", 2 * g + o) for g in range(1, NT3, 2) for o in (0, 1)],
                       "sync", "ltm1")
        for i in range(12):
            dst = AB2[:, 8192 + i * 2048: 8192 + (i + 1) * 2048]
            pl.dma("scalar", f"lb{i % 2}",
                   lambda s, dst=dst, i=i: s.dma_start(dst, xn_d[i]),
                   reads=[("xnd", i)], writes=a2keys(8192 + i * 2048, 2048))
        pl.patch_total([("A2", 16 + 4 * i + o) for i in range(0, 12, 2)
                        for o in range(4)], "scalar", "lb0")
        pl.patch_total([("A2", 16 + 4 * i + o) for i in range(1, 12, 2)
                        for o in range(4)], "scalar", "lb1")

        ENT = AM[:, 8192:8704]
        PBF = AM[0:16, 8704:9728]
        PTT = AM[:, 9728:9856]

        def softmax_en(sc_src, sc_srckeys, sc_tmp, sc_tmpkeys, ent_extra=()):
            rmax, nmax, rsum, rrec = RS[:, 0:1], RS[:, 1:2], RS[:, 2:3], RS[:, 3:4]
            pl.comp("vector", lambda v: v.reduce_max(rmax, sc_src, axis=AX.X),
                    reads=sc_srckeys, writes=[("rs",)])
            pl.comp("vector", lambda v: v.tensor_scalar(nmax, rmax, -1.0, None, ALU.mult),
                    reads=[("rs",)], writes=[("rs",)])
            pl.comp("scalar", lambda sc: sc.activation(sc_tmp, sc_src, AF.Exp, bias=nmax),
                    reads=sc_srckeys + [("rs",)], writes=sc_tmpkeys)
            pl.comp("vector", lambda v: v.reduce_sum(rsum, sc_tmp, axis=AX.X),
                    reads=sc_tmpkeys, writes=[("rs",)])
            pl.comp("vector", lambda v: v.reciprocal(rrec, rsum),
                    reads=[("rs",)], writes=[("rs",)])
            pl.comp("vector", lambda v: v.tensor_scalar(sc_tmp, sc_tmp, rrec, None, ALU.mult),
                    reads=[("rs",)] + sc_tmpkeys, writes=sc_tmpkeys)
            for i in range(NT3):
                lt = sc_tmp[:, i * 128:(i + 1) * 128]
                ot = PS[6][:, i * 16:(i + 1) * 16]
                pl.comp("tensor", lambda te, ot=ot, lt=lt: te.matmul(
                    ot, lt, I128[0:16, 0:16], start=True, stop=True),
                    reads=sc_tmpkeys + [("I128",)], writes=[("ps", 6)])
            pl.comp("vector", lambda v: v.tensor_copy(ENT[:], PS[6][:]),
                    reads=[("ps", 6)], writes=[("ent",)] + list(ent_extra))

        def pool_phase(tag, wrhs_ap, wrhs_col, wrhs_keys):
            for i in range(NT3):
                lt = ENT[:, i * 16:(i + 1) * 16]
                for nh in range(2):
                    rh = AB1[:, i * 1024 + nh * 512: i * 1024 + (nh + 1) * 512]
                    pl.comp("tensor", lambda te, lt=lt, rh=rh, i=i, nh=nh: te.matmul(
                        PS[2 + nh][0:16, :], lt, rh,
                        start=(i == 0), stop=(i == NT3 - 1)),
                        reads=[("ent",)] + a1keys(i * 1024 + nh * 512, 512),
                        writes=[("ps", 2 + nh)])
            for nh in range(2):
                pl.comp("vector", lambda v, nh=nh: v.tensor_copy(
                    PBF[:, nh * 512:(nh + 1) * 512], PS[2 + nh][0:16, :]),
                    reads=[("ps", 2 + nh)], writes=[("pbf",), ("cnt", 0)])
            for c in range(CH):
                lt = PBF[:, c * 128:(c + 1) * 128]
                ot = PS[7][:, c * 16:(c + 1) * 16]
                pl.comp("tensor", lambda te, ot=ot, lt=lt: te.matmul(
                    ot, lt, I128[0:16, 0:16], start=True, stop=True),
                    reads=[("pbf",), ("I128",)], writes=[("ps", 7)])
            pl.comp("vector", lambda v: v.tensor_copy(PTT[:], PS[7][:, 0:128]),
                    reads=[("ps", 7)], writes=[("ptt",), ("cnt", 1)])
            for dc in range(CH):
                lt = PTT[:, dc * 16:(dc + 1) * 16]
                for nh in range(2):
                    rh = wrhs_ap[:, wrhs_col + dc * 1024 + nh * 512: wrhs_col + dc * 1024 + nh * 512 + 512]
                    pl.comp("tensor", lambda te, lt=lt, rh=rh, dc=dc, nh=nh: te.matmul(
                        PS[2 + nh][0:16, :], lt, rh, start=(dc == 0), stop=(dc == 7)),
                        reads=[("ptt",)] + wrhs_keys, writes=[("ps", 2 + nh)])
            for nh in range(2):
                pl.comp("vector", lambda v, nh=nh: v.tensor_tensor(
                    SQ[0:16, nh * 512:(nh + 1) * 512], PS[2 + nh][0:16, :],
                    MASK[:, nh * 512:(nh + 1) * 512], ALU.mult),
                    reads=[("ps", 2 + nh), ("MASK",)], writes=[("mskd",)])
            pl.comp("tensor", lambda te: te.matmul(
                PS[5][0:1, 0:512], ONES[:, 0:1], SQ[0:16, 0:512],
                start=True, stop=True),
                reads=[("mskd",), ("ONES",)], writes=[("ps", 5)])
            pl.comp("tensor", lambda te: te.matmul(
                PS[4][0:1, 0:512], ONES[:, 0:1], SQ[0:16, 512:1024],
                start=True, stop=True),
                reads=[("mskd",), ("ONES",)], writes=[("ps", 4)])

        def row_to_fm(srcrow_bf, srckeys, dst, dstname):
            for c in range(CH):
                lt = srcrow_bf[:, c * 128:(c + 1) * 128]
                ot = PS[7][:, c:c + 1]
                pl.comp("tensor", lambda te, ot=ot, lt=lt: te.matmul(
                    ot, lt, BIAS3[0:1, 1088:1089], start=True, stop=True),
                    reads=srckeys + [("ones3",)], writes=[("ps", 7)])
            pl.comp("vector", lambda v: v.tensor_copy(dst[:], PS[7][:, 0:8]),
                    reads=[("ps", 7)], writes=[(dstname, c_) for c_ in range(CH)])

        softmax_en(SC1, [("SC1",)], SC2, [("SC2",), ("xtb", 0), ("xtb", 1)])
        pool_phase("q", AM, 0, [("WSM", d) for d in range(8)])
        PQROW = YS[0:1, 1024:2048]
        pl.comp("vector", lambda v: v.tensor_tensor(
            PQROW[:, 0:512], PS[5][0:1, 0:512], BQ[:, 0:512], ALU.add),
            reads=[("ps", 5), ("bq",)], writes=[("pqrow",)])
        pl.comp("vector", lambda v: v.tensor_tensor(
            PQROW[:, 512:1024], PS[4][0:1, 0:512], BQ[:, 512:1024], ALU.add),
            reads=[("ps", 4), ("bq",)], writes=[("pqrow",)])
        PQROWB = SQ[0:1, 0:1024]
        pl.comp("vector", lambda v: v.tensor_copy(PQROWB, PQROW),
                reads=[("pqrow",)], writes=[("pqrowb",), ("mskd",)])
        row_to_fm(PQROWB, [("pqrowb",)], PQ, "pq")

        for d in range(8):
            cast(Wt_d[d * 128:(d + 1) * 128, :], AM[:, d * 1024:(d + 1) * 1024],
                 [("WSM", d)])
        cast_flush()

        def ld_x3(t):
            slot = XF[:, (t % 2) * 1024:(t % 2 + 1) * 1024]
            pl.dma("sync", f"l3{t % 2}",
                   lambda s, slot=slot, t=t: s.dma_start(
                       slot, x_d[t * 128:(t + 1) * 128, :]),
                   writes=[("xf3", t % 2)])

        def ld_cnt3(t):
            slot = AM[:, 10240 + (t % 2) * 1024: 10240 + (t % 2) * 1024 + 1024]
            pl.dma("sync", f"cr{t % 2}",
                   lambda s, slot=slot, t=t: s.dma_start(slot, cnt_d[t]),
                   reads=[("cntd", t)], writes=[("cntr", t % 2)])

        for c in range(CH):
            wb = WB[:, c * 16:(c + 1) * 16]
            pl.comp("vector", lambda v, wb=wb, c=c: v.tensor_scalar(
                wb, wb, PQ[:, c:c + 1], None, ALU.mult),
                reads=[("pq", c), ("WB", c)], writes=[("WB", c)])
        for dc in range(CH):
            for fc in range(CH):
                lt = AB2[:, fc * 1024 + dc * 128: fc * 1024 + (dc + 1) * 128]
                rh = WB[:, fc * 16:(fc + 1) * 16]
                pl.comp("tensor", lambda te, lt=lt, rh=rh, dc=dc, fc=fc: te.matmul(
                    PS[4][:, dc * 16:(dc + 1) * 16], lt, rh,
                    start=(fc == 0), stop=(fc == 7)),
                    reads=a2keys(fc * 1024 + dc * 128, 128) + [("WB", fc)],
                    writes=[("ps", 4)])
        pl.comp("vector", lambda v: v.tensor_copy(WKB[:], PS[4][:, 0:128]),
                reads=[("ps", 4)], writes=[("WKB",)])
        BKFB = SQ[:, 16:24]
        for c in range(CH):
            lt = BK[:, c * 128:(c + 1) * 128]
            ot = PS[5][:, c:c + 1]
            pl.comp("tensor", lambda te, ot=ot, lt=lt: te.matmul(
                ot, lt, BIAS3[32:33, 1088:1089], start=True, stop=True),
                reads=[("bk",), ("ones3",)], writes=[("ps", 5)])
        pl.comp("vector", lambda v: v.tensor_copy(BKFB, PS[5][:, 0:8]),
                reads=[("ps", 5)], writes=[("bkfb",), ("pqrowb",)])
        for c in range(CH):
            pl.comp("tensor", lambda te, c=c: te.matmul(
                PS[5][0:16, 0:1], WB[:, c * 16:(c + 1) * 16], BKFB[:, c:c + 1],
                start=(c == 0), stop=False),
                reads=[("WB", c), ("bkfb",)], writes=[("ps", 5)])
        pl.comp("tensor", lambda te: te.matmul(
            PS[5][0:16, 0:1], BKA, BIAS3[32:33, 1088:1089], start=False, stop=True),
            reads=[("bka",), ("ones3",)], writes=[("ps", 5)])
        BIASK = RS[:, 4:5]
        pl.comp("vector", lambda v: v.tensor_scalar(
            BIASK, PS[5][0:16, 0:1], SCALE, None, ALU.mult),
            reads=[("ps", 5)], writes=[("rs",)])
        p0_xnload(12, extra=[("SC1",)])
        p0_xnload(13, extra=[("SC1",)])
        for b in range(NB):
            if b < 12:
                base = 8192 + b * 2048
                xnb = AB2[:, base: base + 2048]
                kxn = lambda dc, base=base: a2keys(base + dc * 256, 256)
            else:
                xnb = SCA[:, (b % 2) * 2048:(b % 2) * 2048 + 2048]
                kxn = lambda dc, b=b: [("xnb", b % 2)]
            for dc in range(CH):
                lt = WKB[:, dc * 16:(dc + 1) * 16]
                rh = xnb[:, dc * 256:(dc + 1) * 256]
                pl.comp("tensor", lambda te, lt=lt, rh=rh, dc=dc: te.matmul(
                    PS[4][0:16, 0:256], lt, rh, start=(dc == 0), stop=(dc == 7)),
                    reads=[("WKB",)] + kxn(dc), writes=[("ps", 4)])
            pl.comp("vector", lambda v, b=b: v.tensor_scalar(
                SC2[:, b * 256:(b + 1) * 256], PS[4][0:16, 0:256],
                SCALE, BIASK, ALU.mult, ALU.add),
                reads=[("ps", 4), ("rs",)], writes=[("SC2k", b)])
            if b in (12, 13):
                p0_xnload(b + 2, extra=[("SC1",)])
        for bb in range(8):
            base = 8192 + bb * 2048
            pl.dma("gpsimd", f"lm{bb // 4}",
                   lambda g, bb=bb, base=base: g.dma_start(
                       AB2[:, base: base + 2048], mq_d[bb]),
                   reads=[("mqd", bb)], writes=a2keys(base, 2048))
        pl.patch_total([("A2", 16 + 4 * bb + o) for bb in range(4)
                        for o in range(4)], "gpsimd", "lm0")
        pl.patch_total([("A2", 16 + 4 * bb + o) for bb in range(4, 8)
                        for o in range(4)], "gpsimd", "lm1")
        for t in range(2):
            ld_x3(t)
            ld_cnt3(t)

        for d in range(8):
            cast(Wk_d[d * 128:(d + 1) * 128, :],
                 AB2[:, 24576 + d * 1024: 24576 + (d + 1) * 1024],
                 a2keys(24576 + d * 1024, 1024))
        cast_flush()

        softmax_en(SC2, [("SC2k", b) for b in range(NB)], SC1,
                   [("SC1e",), ("SC1",), ("xnb", 0), ("xnb", 1)])
        pool_phase("k", AB2, 24576, a2keys(24576, 8192))
        PKROW = YS[0:1, 0:1024]
        pl.comp("vector", lambda v: v.tensor_tensor(
            PKROW[:, 0:512], PS[5][0:1, 0:512], BK[:, 0:512], ALU.add),
            reads=[("ps", 5), ("bk",)], writes=[("pkrow",)])
        pl.comp("vector", lambda v: v.tensor_tensor(
            PKROW[:, 512:1024], PS[4][0:1, 0:512], BK[:, 512:1024], ALU.add),
            reads=[("ps", 4), ("bk",)], writes=[("pkrow",)])
        pl.comp("vector", lambda v: v.tensor_mul(PKROW, PKROW, PQROW),
                reads=[("pkrow",), ("pqrow",)], writes=[("pkrow",)])
        PKROWB = SQ[0:1, 0:1024]
        pl.comp("vector", lambda v: v.tensor_copy(PKROWB, PKROW),
                reads=[("pkrow",)], writes=[("pkrowb",), ("mskd",), ("bkfrow",)])
        row_to_fm(PKROWB, [("pkrowb",)], PK, "pk")
        MQ_SLOT_KEYS = {12: [("xnb", 0), ("SC1e",), ("SC1",)],
                        13: [("xnb", 1), ("SC1e",), ("SC1",)],
                        14: [("xtb", 0)] + [("SC2k", b_) for b_ in range(8)],
                        15: [("xtb", 1)] + [("SC2k", b_) for b_ in range(8, 16)]}
        for bb in range(8, NB):
            if bb < 12:
                base = 8192 + bb * 2048
                dst, wkeys = AB2[:, base: base + 2048], a2keys(base, 2048)
            else:
                dst = SCA[:, (bb - 12) * 2048:(bb - 11) * 2048]
                wkeys = MQ_SLOT_KEYS[bb]
            pl.dma("gpsimd", f"lm{bb // 4}",
                   lambda g, bb=bb, dst=dst: g.dma_start(dst, mq_d[bb]),
                   reads=[("mqd", bb)], writes=wkeys)
        pl.patch_total([("A2", 16 + 4 * bb + o) for bb in range(8, 12)
                        for o in range(4)], "gpsimd", "lm2")
        pl.patch_total([k for bb in range(12, NB) for k in MQ_SLOT_KEYS[bb]],
                       "gpsimd", "lm3")
        for d in range(8):
            cast(Wt_d[d * 128:(d + 1) * 128, :], AM[:, d * 1024:(d + 1) * 1024],
                 [("WSM", d)])
        cast_flush()
        for f in range(8):
            wt = AM[:, f * 1024:(f + 1) * 1024]
            pl.comp("vector", lambda v, wt=wt, f=f: v.tensor_scalar(
                wt, wt, PK[:, f:f + 1], None, ALU.mult),
                reads=[("pk", f), ("WSM", f)], writes=[("WSM", f)])

        pl.barrier()

        def mq_ap(bb):
            if bb < 12:
                return AB2[:, 8192 + bb * 2048: 8192 + (bb + 1) * 2048]
            return SCA[:, (bb - 12) * 2048:(bb - 11) * 2048]

        def mq_keys(bb):
            if bb < 12:
                return a2keys(8192 + bb * 2048, 2048)
            return [MQ_SLOT_KEYS[bb][0]]

        for t in range(NT3):
            bb = t // 2
            kmqb = mq_keys(bb)[0]

            def mqsl(c, t=t, bb=bb):
                return mq_ap(bb)[:, c * 256 + (t % 2) * 128: c * 256 + (t % 2) * 128 + 128]
            pset = 0 if t % 2 == 0 else 2
            b0, b1 = PS[pset], PS[pset + 1]
            kb0, kb1 = ("ps", pset), ("ps", pset + 1)
            for c in range(CH):
                lt = mqsl(c)
                for nh in range(2):
                    bank, kb = (b0, kb0) if nh == 0 else (b1, kb1)
                    rh = AM[:, c * 1024 + nh * 512: c * 1024 + nh * 512 + 512]
                    pl.comp("tensor", lambda te, bank=bank, lt=lt, rh=rh, c=c: te.matmul(
                        bank[:], lt, rh, start=(c == 0), stop=False),
                        reads=[kmqb, ("WSM", c)], writes=[kb])
            for c in range(CH):
                bank, kb = (b0, kb0) if c < 4 else (b1, kb1)
                reg = bank[:, (c % 4) * 128:(c % 4) * 128 + 128]
                lt = mqsl(c)
                pl.comp("tensor", lambda te, reg=reg, lt=lt: te.matmul(
                    reg, lt, I128[:], start=False, stop=False),
                    reads=[kmqb, ("I128",)], writes=[kb])
            for nh in range(2):
                bank, kb = (b0, kb0) if nh == 0 else (b1, kb1)
                rh = BT[:, nh * 512:(nh + 1) * 512]
                pl.comp("tensor", lambda te, bank=bank, rh=rh: te.matmul(
                    bank[:], BIAS3[64:65, 1088:1216], rh, start=False, stop=True),
                    reads=[("bt",), ("ones3",)], writes=[kb])
            cslot = AM[:, 10240 + (t % 2) * 1024: 10240 + (t % 2) * 1024 + 1024]
            gbuf = YS[:, (t % 2) * 1024:(t % 2) * 1024 + 1024]
            galias = ("pkrow",) if t % 2 == 0 else ("pqrow",)
            pl.comp("scalar", lambda sc, cslot=cslot, gbuf=gbuf: sc.activation(
                gbuf, cslot, AF.Silu, scale=0.2),
                reads=[("cntr", t % 2)], writes=[("g3", t % 2), galias])
            ot = OB[:, (t % 2) * 1024:(t % 2 + 1) * 1024]
            for nh in range(2):
                bank, kb = (b0, kb0) if nh == 0 else (b1, kb1)
                pl.comp("vector", lambda v, ot=ot, bank=bank, gbuf=gbuf, nh=nh:
                        v.tensor_mul(ot[:, nh * 512:(nh + 1) * 512], bank[:],
                                     gbuf[:, nh * 512:(nh + 1) * 512]),
                        reads=[kb, ("g3", t % 2)],
                        writes=[("ob3", t % 2), ("stg", t % 2)])
            xt = XF[:, (t % 2) * 1024:(t % 2 + 1) * 1024]
            pl.comp("vector", lambda v, ot=ot, xt=xt: v.tensor_add(ot, ot, xt),
                    reads=[("xf3", t % 2), ("ob3", t % 2)], writes=[("ob3", t % 2)])
            pl.dma("gpsimd", f"so{t % 2}",
                   lambda g, ot=ot, t=t: g.dma_start(out_d[t * 128:(t + 1) * 128, :], ot),
                   reads=[("ob3", t % 2)], writes=[("outd", t)])
            if t + 2 < NT3:
                ld_x3(t + 2)
                ld_cnt3(t + 2)

        pl.wait_all("sync", list(SY_SEMS))
        pl.wait_all("gpsimd", list(GP_SEMS))
        pl.wait_all("scalar", [s_ for s_ in SC_SEMS if s_ != "act"])

        def replay(eng_name):
            def body(eng):
                for waits, fn, sem, n in pl.ops[eng_name]:
                    for s_, v_ in waits:
                        eng.wait_ge(sems[s_], v_)
                    if fn is not None:
                        inst = fn(eng)
                        inst.then_inc(sems[sem], n)
            return body

        block.sync(replay("sync"))
        block.tensor(replay("tensor"))
        block.scalar(replay("scalar"))
        block.vector(replay("vector"))
        block.gpsimd(replay("gpsimd"))

    return nc


_NC_CACHE = None


def kernel(**inputs):
    global _NC_CACHE
    x = np.ascontiguousarray(inputs["L0_attn_hidden_states"], dtype=np.float32)
    Wk = np.ascontiguousarray(inputs["L0_attn_Wk"], dtype=np.float32)
    wmap = {
        "Wq": np.ascontiguousarray(inputs["L0_attn_Wq"], dtype=np.float32),
        "Wk": Wk,
        "WkT": np.ascontiguousarray(Wk.T),
        "Wt": np.ascontiguousarray(inputs["L0_attn_Wt"], dtype=np.float32),
        "Wqa": np.ascontiguousarray(inputs["L0_attn_Wqa"], dtype=np.float32),
        "Wka": np.ascontiguousarray(inputs["L0_attn_Wka"], dtype=np.float32),
        "WfcT": np.ascontiguousarray(np.asarray(inputs["L0_ffn_W_fc"], dtype=np.float32).T),
        "WprojT": np.ascontiguousarray(np.asarray(inputs["L0_ffn_W_proj"], dtype=np.float32).T),
        "bq": np.ascontiguousarray(np.asarray(inputs["L0_attn_bq"], dtype=np.float32).reshape(1, D)),
        "bk": np.ascontiguousarray(np.asarray(inputs["L0_attn_bk"], dtype=np.float32).reshape(1, D)),
        "bt": np.ascontiguousarray(np.asarray(inputs["L0_attn_bt"], dtype=np.float32).reshape(1, D)),
        "bqa": np.ascontiguousarray(np.asarray(inputs["L0_attn_bqa"], dtype=np.float32).reshape(1, H)),
        "bka": np.ascontiguousarray(np.asarray(inputs["L0_attn_bka"], dtype=np.float32).reshape(1, H)),
    }
    if _NC_CACHE is None:
        _NC_CACHE = build_nc()
    nc = _NC_CACHE
    in_maps = [dict(wmap, x=np.ascontiguousarray(x[b])) for b in range(8)]
    res = run_bass_kernel_spmd(nc, in_maps, core_ids=list(range(8)))
    return np.stack([np.asarray(r["out"], dtype=np.float32) for r in res.results])


# version 27
# speedup vs baseline: 1.0718x; 1.0274x over previous
import numpy as np
import ml_dtypes
from contextlib import ExitStack

import concourse.bass as bass
import concourse.mybir as mybir
from concourse.bass_utils import run_bass_kernel_spmd

F32 = mybir.dt.float32
BF16 = mybir.dt.bfloat16
AF = mybir.ActivationFunctionType
ALU = mybir.AluOpType
AX = mybir.AxisListType

S, D, H, DH, CH = 4096, 1024, 16, 64, 8
SCALE = 1.0 / 8.0
EPS = 1e-5

_a = 0.0
for _ in range(5):
    _a = _a * 0.9 + 1.0
TAUS = [c / _a for c in (1.0, 1.9, 2.81, 3.71, 4.439)]

NB, T0 = 16, 256
NT3 = 32

ENG_SEM = {"tensor": "pe", "scalar": "act", "vector": "dve", "gpsimd": "pool"}
SEM_NAMES = ("pe", "act", "dve", "pool", "lc",
             "lw0", "lw1", "lw2", "lw3", "lw4", "lw5", "lw6", "lw7",
             "ln0", "ln1", "ln2", "ln3", "ln4", "ln5", "ln6", "ln7",
             "l30", "l31", "l32",
             "lb0", "lb1",
             "ltm0", "ltm1",
             "lmq0", "lmq1",
             "lm0", "lm1", "lm2", "lm3",
             "lqs",
             "cr0", "cr1",
             "sx0", "sx1",
             "tm0", "tm1",
             "mq0", "mq1",
             "sq",
             "sc0", "sc1",
             "so0", "so1")
DMA_SEMS = set(SEM_NAMES[4:])


class Plan:
    def __init__(self):
        self.ops = {k: [] for k in ("tensor", "scalar", "vector", "gpsimd", "sync")}
        self.cnt = {k: 0 for k in SEM_NAMES}
        self.res = {}
        self.hiwater = {k: {} for k in self.ops}

    def _need_wait(self, eng, sem, val):
        hw = self.hiwater[eng]
        if hw.get(sem, 0) >= val:
            return False
        hw[sem] = val
        return True

    STRICT = {"lns", "rs", "pq", "pk", "ceps", "ones3",
              "bq", "bk", "bt", "bqa", "bka", "bstg"}

    def _op(self, eng, fn, reads, writes, sem, n):
        waits = {}
        my_val = self.cnt[sem] + n

        def add(dep_eng, dep_sem, dep_val, strict):
            if dep_eng == eng and dep_sem == "pe" and sem == "pe":
                return
            if dep_eng == eng and dep_sem in DMA_SEMS and sem == dep_sem:
                return
            waits[dep_sem] = max(waits.get(dep_sem, 0), dep_val)

        for k in reads:
            st = self.res.get(k)
            if st and st["w"]:
                add(*st["w"], k[0] in self.STRICT)
        for k in writes:
            st = self.res.get(k)
            if st:
                strict = k[0] in self.STRICT
                if st["w"]:
                    add(*st["w"], strict)
                for r in st["r"]:
                    add(*r, strict)
        for k in reads:
            st = self.res.get(k)
            if st is not None:
                st["r"].append((eng, sem, my_val))
            else:
                self.res[k] = {"w": None, "r": [(eng, sem, my_val)]}
        for k in writes:
            self.res[k] = {"w": (eng, sem, my_val), "r": []}
        self.cnt[sem] = my_val
        real_waits = [(s, v) for s, v in waits.items() if self._need_wait(eng, s, v)]
        self.ops[eng].append((real_waits, fn, sem, n))
        return my_val

    def comp(self, eng, fn, reads=(), writes=()):
        return self._op(eng, fn, reads, writes, ENG_SEM[eng], 1)

    def dma(self, queue, sem, fn, reads=(), writes=()):
        return self._op(queue, fn, reads, writes, sem, 16)

    def patch_total(self, keys, eng, sem):
        tot = (eng, sem, self.cnt[sem])
        for k in keys:
            if k in self.res:
                self.res[k]["w"] = tot

    def wait_all(self, eng, sems_):
        for s in sems_:
            v = self.cnt[s]
            if v and self._need_wait(eng, s, v):
                self.ops[eng].append(([(s, v)], None, None, 0))

    def barrier(self):
        for e in self.ops:
            self.wait_all(e, [s for s in self.cnt if s != ENG_SEM.get(e)])


def build_nc(dbg=False):
    nc = bass.Bass("TRN2", target_bir_lowering=False)

    x_d = nc.dram_tensor("x", [S, D], F32, kind="ExternalInput")
    Wq_d = nc.dram_tensor("Wq", [D, D], F32, kind="ExternalInput")
    Wk_d = nc.dram_tensor("Wk", [D, D], F32, kind="ExternalInput")
    WkT_d = nc.dram_tensor("WkT", [D, D], F32, kind="ExternalInput")
    Wt_d = nc.dram_tensor("Wt", [D, D], F32, kind="ExternalInput")
    Wqa_d = nc.dram_tensor("Wqa", [D, H], F32, kind="ExternalInput")
    Wka_d = nc.dram_tensor("Wka", [D, H], F32, kind="ExternalInput")
    WfcT_d = nc.dram_tensor("WfcT", [D, 4 * D], F32, kind="ExternalInput")
    WprojT_d = nc.dram_tensor("WprojT", [4 * D, D], F32, kind="ExternalInput")
    bq_d = nc.dram_tensor("bq", [1, D], F32, kind="ExternalInput")
    bk_d = nc.dram_tensor("bk", [1, D], F32, kind="ExternalInput")
    bt_d = nc.dram_tensor("bt", [1, D], F32, kind="ExternalInput")
    bqa_d = nc.dram_tensor("bqa", [1, H], F32, kind="ExternalInput")
    bka_d = nc.dram_tensor("bka", [1, H], F32, kind="ExternalInput")
    out_d = nc.dram_tensor("out", [S, D], F32, kind="ExternalOutput")

    xn_d = nc.dram_tensor("xn_scratch", [NB, 128, CH * T0], BF16, kind="Internal")
    xtm_d = nc.dram_tensor("xtm_scratch", [NT3, 128, D], BF16, kind="Internal")
    mq_d = nc.dram_tensor("mq_scratch", [NB, 128, CH * T0], BF16, kind="Internal")
    qs_d = nc.dram_tensor("qs_scratch", [H, S], BF16, kind="Internal")
    cnt_d = nc.dram_tensor("cnt_scratch", [NT3, 128, D], BF16, kind="Internal")

    I128_d = nc.inline_tensor(np.eye(128, dtype=ml_dtypes.bfloat16), "ident128")
    ONES_d = nc.inline_tensor(np.ones((16, 512), dtype=ml_dtypes.bfloat16), "ones16")
    mask_np = np.zeros((H, D), dtype=ml_dtypes.bfloat16)
    for f in range(D):
        mask_np[f // DH, f] = 1.0
    MASK_d = nc.inline_tensor(mask_np, "diagmask")

    with ExitStack() as ctx:
        e = ctx.enter_context
        AB1 = e(nc.sbuf_tensor("AB1", [128, 32768], BF16))
        AB2 = e(nc.sbuf_tensor("AB2", [128, 32768], BF16))
        AM = e(nc.sbuf_tensor("AM", [128, 12288], BF16))
        SCA = e(nc.sbuf_tensor("SCA", [128, 8192], BF16))
        XF = e(nc.sbuf_tensor("XF", [128, 3072], F32))
        SQ = e(nc.sbuf_tensor("SQ", [128, 1024], BF16))
        RB = e(nc.sbuf_tensor("RB", [128, 512], F32))
        YS = e(nc.sbuf_tensor("YS", [128, 2048], F32))
        OB = e(nc.sbuf_tensor("OB", [128, 2048], F32))
        LNS = e(nc.sbuf_tensor("LNS", [128, 16], F32))
        RS = e(nc.sbuf_tensor("RS", [16, 8], F32))
        PQ = e(nc.sbuf_tensor("PQ", [128, 8], F32))
        PK = e(nc.sbuf_tensor("PK", [128, 8], F32))
        CEPS = e(nc.sbuf_tensor("CEPS", [128, 1], F32))
        WA = e(nc.sbuf_tensor("WA", [128, 128], BF16))
        WB = e(nc.sbuf_tensor("WB", [128, 128], BF16))
        WKB = e(nc.sbuf_tensor("WKB", [128, 128], BF16))
        BIAS3 = e(nc.sbuf_tensor("BIAS3", [128, 1600], BF16))
        I128 = e(nc.sbuf_tensor("I128", [128, 128], BF16))
        ONES = e(nc.sbuf_tensor("ONES", [16, 512], BF16))
        MASK = e(nc.sbuf_tensor("MASK", [16, 1024], BF16))
        QSS = e(nc.sbuf_tensor("QSS", [16, 256], BF16))
        PS = [e(nc.psum_tensor(f"PS{i}", [128, 512], F32)) for i in range(8)]
        sems = {n: e(nc.semaphore(n)) for n in SEM_NAMES}
        GP_SEMS = ("pool", "lw0", "lw1", "lw4", "lw6", "so0", "so1",
                   "ln0", "ln2", "ln4", "ln6", "lmq0", "lmq1",
                   "lm0", "lm1", "lm2", "lm3")
        SY_SEMS = ("lc", "lw2", "lw3", "lw5", "lw7", "l30", "l31", "l32",
                   "lb0", "lb1", "ltm0", "ltm1", "lqs",
                   "ln1", "ln3", "ln5", "ln7",
                   "cr0", "cr1")
        SC_SEMS = ("act", "sx0", "sx1", "tm0", "tm1", "mq0", "mq1", "sq",
                   "sc0", "sc1")
        for eng_obj, own in ((nc.tensor, ("pe",)), (nc.scalar, SC_SEMS),
                             (nc.vector, ("dve",)), (nc.gpsimd, GP_SEMS),
                             (nc.sync, SY_SEMS)):
            for s_ in own:
                eng_obj.sem_clear(sems[s_])
        nc.all_engine_barrier()
        block = e(nc.Block())

        pl = Plan()
        ONE5 = ONES[0:1, :]

        BQ = BIAS3[0:1, 0:1024]
        BK = BIAS3[32:33, 0:1024]
        BT = BIAS3[64:65, 0:1024]
        BQA = BIAS3[0:1, 1024:1040]
        BKA = BIAS3[32:33, 1024:1040]
        pl.dma("sync", "lc", lambda s: s.dma_start(I128[:], I128_d.ap()), writes=[("I128",)])
        pl.dma("sync", "lc", lambda s: s.dma_start(ONES[:], ONES_d.ap()), writes=[("ONES",)])
        pl.dma("sync", "lc", lambda s: s.dma_start(MASK[:], MASK_d.ap()), writes=[("MASK",)])
        bias_jobs = []
        for nm, dsrc, dst, w, stg, alias in (
                ("bq", bq_d, BQ, 1024, OB[0:1, 0:1024], ("stg", 0)),
                ("bk", bk_d, BK, 1024, OB[0:1, 1024:2048], ("stg", 1)),
                ("bt", bt_d, BT, 1024, YS[0:1, 0:1024], ("ys", 0)),
                ("bqa", bqa_d, BQA, 16, YS[0:1, 1024:1040], ("ys", 1)),
                ("bka", bka_d, BKA, 16, YS[0:1, 1056:1072], ("ys", 1))):
            pl.dma("sync", "lc", lambda s, dsrc=dsrc, stg=stg: s.dma_start(stg, dsrc.ap()),
                   writes=[("bstg", nm)])
            bias_jobs.append((nm, dst, w, stg, alias))
        lc_tot = ("sync", "lc", pl.cnt["lc"])
        for k in [("I128",), ("ONES",), ("MASK",)] + [("bstg", nm) for nm, *_ in bias_jobs]:
            pl.res[k]["w"] = lc_tot
        BIAS_FINE = {"bq": [("stgp", 0), ("stgp", 2)] + [("stgn", i) for i in range(4)],
                     "bk": [("stgp", 4), ("stgp", 6)] + [("stgn", i) for i in range(4, 8)],
                     "bt": [("ysp", 1), ("ysp", 3)], "bqa": [("ysp", 5)],
                     "bka": [("ysp", 5)]}
        for nm, dst, w, stg, alias in bias_jobs:
            pl.comp("vector", lambda v, dst=dst, stg=stg, w=w: v.tensor_copy(dst[:, 0:w], stg),
                    reads=[("bstg", nm)] + BIAS_FINE[nm], writes=[(nm,), alias])
        pl.comp("vector", lambda v: v.memset(BIAS3[:, 1088:1600], 1.0),
                writes=[("ones3",)])
        pl.comp("vector", lambda v: v.memset(CEPS[:], EPS), writes=[("ceps",)])

        ppc = [0]
        cast_pend = []
        CAST_SLOTS = [
            ("gpsimd", "lw0", OB[:, 0:512], ("stg", 0)),
            ("sync", "lw2", YS[:, 0:512], ("ys", 0)),
            ("sync", "lw5", OB[:, 512:1024], ("stg", 0)),
            ("gpsimd", "lw4", YS[:, 512:1024], ("ys", 0)),
            ("gpsimd", "lw1", OB[:, 1024:1536], ("stg", 1)),
            ("sync", "lw3", YS[:, 1024:1536], ("ys", 1)),
            ("sync", "lw7", OB[:, 1536:2048], ("stg", 1)),
            ("gpsimd", "lw6", YS[:, 1536:2048], ("ys", 1)),
        ]
        XFS = [XF[:, 2048:2560], XF[:, 2560:3072]]
        CAST_SLOTS_N = [
            ("gpsimd", "ln0", OB[:, 0:512], ("stg", 0), [("stgp", 0)]),
            ("sync", "ln1", OB[:, 512:1024], ("stg", 0), [("stgp", 2)]),
            ("gpsimd", "ln2", OB[:, 1024:1536], ("stg", 1), [("stgp", 4)]),
            ("sync", "ln3", OB[:, 1536:2048], ("stg", 1), [("stgp", 6)]),
            ("gpsimd", "ln4", XFS[0], ("xfs", 0), []),
            ("sync", "ln5", XFS[1], ("xfs", 1), []),
        ]
        cast_nslots = [4]
        cast_ceng = [True]

        def cast_piece(dsrc_ap, dst_ap, dst_keys):
            mode = cast_nslots[0]
            if mode == 8:
                sl = ppc[0] % 8
                queue, sem, stg_full, skey = CAST_SLOTS[sl]
                wkeys = [(skey[0] + "p", sl)]
            else:
                rot = (0, 1, 2, 3, 4, 5) if mode == 4 else (0, 2, 4)
                sl = rot[ppc[0] % len(rot)]
                queue, sem, stg_full, skey, okeys = CAST_SLOTS_N[sl]
                wkeys = [("stgn", sl)] + okeys
            stg = stg_full[:, 0:dsrc_ap.shape[1]]
            while len(cast_pend) >= min(4, 8 if mode == 8 else
                                        (6 if mode == 4 else 3)):
                cast_pend.pop(0)()
            pl.dma(queue, sem, lambda g, a=dsrc_ap, s=stg: g.dma_start(s, a),
                   reads=[], writes=wkeys)
            ceng = ("vector", "scalar")[ppc[0] % 2] if cast_ceng[0] is True else \
                   ("scalar", "gpsimd")[ppc[0] % 2] if cast_ceng[0] is False else \
                   "gpsimd"
            meth = (lambda e_, s_, d_: e_.copy(d_, s_)) if ceng == "scalar" else \
                   (lambda e_, s_, d_: e_.tensor_copy(d_, s_))
            cast_pend.append(lambda s=stg, d=dst_ap, k=list(dst_keys),
                             rk=wkeys[0], en=ceng, m=meth, sk=skey: pl.comp(
                en, lambda eng, s=s, d=d, m=m: m(eng, s, d),
                reads=[rk], writes=k + [sk]))
            ppc[0] += 1

        def cast(dsrc_ap, dst_ap, dst_keys):
            w = dsrc_ap.shape[1]
            if w > 512:
                for o in range(0, w, 512):
                    cast_piece(dsrc_ap[:, o:o + 512], dst_ap[:, o:o + 512], dst_keys)
            else:
                cast_piece(dsrc_ap, dst_ap, dst_keys)

        def cast_flush():
            while cast_pend:
                cast_pend.pop(0)()

        def a1(col, sz):
            return AB1[:, col: col + sz]

        def a1keys(col, sz):
            return [("A1", i) for i in range(col // 512, (col + sz + 511) // 512)]

        def a2keys(col, sz):
            return [("A2", i) for i in range(col // 512, (col + sz + 511) // 512)]

        def wsmkeys(col, sz):
            return [("WSM", i) for i in range(col // 1024, (col + sz + 1023) // 1024)]

        def pa_xload(gt):
            slot = XF[:, (gt % 2) * 1024:(gt % 2) * 1024 + 1024]
            pl.dma("sync", f"l3{gt % 2}",
                   lambda s, slot=slot, gt=gt: s.dma_start(
                       slot, x_d[gt * 128:(gt + 1) * 128, :]),
                   writes=[("x0", gt % 2)])

        def pa_ln_thunks(gt):
            th = []
            b, t = gt // 2, gt % 2
            xt = XF[:, (gt % 2) * 1024:(gt % 2) * 1024 + 1024]
            kx = ("x0", gt % 2)
            xtb = SCA[:, 4096 + (b % 2) * 2048: 4096 + (b % 2) * 2048 + 2048]
            kxtb = ("xtb", b % 2)
            par = t
            s1 = LNS[:, par * 8 + 0: par * 8 + 1]
            m_ = LNS[:, par * 8 + 1: par * 8 + 2]
            ssq = LNS[:, par * 8 + 2: par * 8 + 3]
            m2 = LNS[:, par * 8 + 3: par * 8 + 4]
            v_ = LNS[:, par * 8 + 4: par * 8 + 5]
            lv = LNS[:, par * 8 + 5: par * 8 + 6]
            r_ = LNS[:, par * 8 + 6: par * 8 + 7]
            kl = ("lns", par)
            th.append(lambda: pl.comp("vector", lambda v: v.reduce_sum(s1, xt, axis=AX.X),
                      reads=[kx], writes=[kl]))
            th.append(lambda: pl.comp("vector", lambda v: v.tensor_scalar(
                m_, s1, 1.0 / D, None, ALU.mult), reads=[kl], writes=[kl]))
            th.append(lambda: pl.comp("scalar", lambda sc: sc.activation(SQ[:], xt, AF.Square),
                      reads=[kx], writes=[("sq",)]))
            th.append(lambda: pl.comp("vector", lambda v: v.reduce_sum(ssq, SQ[:], axis=AX.X),
                      reads=[("sq",)], writes=[kl]))
            th.append(lambda: pl.comp("vector", lambda v: v.tensor_mul(m2, m_, m_),
                      reads=[kl], writes=[kl]))
            th.append(lambda: pl.comp("vector", lambda v: v.scalar_tensor_tensor(
                v_, ssq, 1.0 / D, m2, ALU.mult, ALU.subtract), reads=[kl], writes=[kl]))
            th.append(lambda: pl.comp("scalar", lambda sc: sc.activation(
                lv, v_, AF.Ln, bias=CEPS[:]), reads=[kl, ("ceps",)], writes=[kl]))
            th.append(lambda: pl.comp("scalar", lambda sc: sc.activation(
                r_, lv, AF.Exp, scale=-0.5), reads=[kl], writes=[kl]))
            xtt = xtb[:, t * 1024:(t + 1) * 1024]
            th.append(lambda: pl.comp("vector", lambda v: v.tensor_scalar(
                xtt, xt, m_, r_, ALU.subtract, ALU.mult),
                reads=[kx, kl], writes=[kxtb]))
            return th

        def pa_tr_thunks(b):
            th = []
            xnb = SCA[:, (b % 2) * 2048:(b % 2) * 2048 + 2048]
            xtb = SCA[:, 4096 + (b % 2) * 2048: 4096 + (b % 2) * 2048 + 2048]
            kxnb = ("xnb", b % 2)
            kxtb = ("xtb", b % 2)

            def mkpair(cp):
                def go():
                    bank = PS[6 + cp % 2]
                    kps = ("ps", 6 + cp % 2)
                    for ci in range(2):
                        c = 2 * cp + ci
                        for t in range(2):
                            lt = xtb[:, t * 1024 + c * 128: t * 1024 + (c + 1) * 128]
                            ot = bank[:, ci * 256 + t * 128: ci * 256 + (t + 1) * 128]
                            pl.comp("tensor", lambda te, ot=ot, lt=lt: te.matmul(
                                ot, lt, I128[:], start=True, stop=True),
                                reads=[kxtb, ("I128",)], writes=[kps])
                    dst = xnb[:, 2 * cp * 256: 2 * cp * 256 + 512]
                    pl.comp("vector", lambda v, dst=dst, cp=cp: v.tensor_copy(
                        dst, PS[6 + cp % 2][:]), reads=[("ps", 6 + cp % 2)],
                        writes=[kxnb])
                return go
            for cp in range(4):
                th.append(mkpair(cp))

            def stores():
                pl.dma("scalar", f"sx{b % 2}",
                       lambda g: g.dma_start(xn_d[b], xnb),
                       reads=[kxnb], writes=[("xnd", b)])
                for t in range(2):
                    src_ = xtb[:, t * 1024:(t + 1) * 1024]
                    gt = 2 * b + t
                    pl.dma("scalar", f"tm{gt % 2}",
                           lambda g, src_=src_, gt=gt: g.dma_start(xtm_d[gt], src_),
                           reads=[kxtb], writes=[("xtmd", gt)])
            th.append(stores)
            return th

        def pa_mq_group(b, eo):
            bank = PS[6 + eo % 2]
            kb = ("ps", 6 + eo % 2)
            xnb = SCA[:, (b % 2) * 2048:(b % 2) * 2048 + 2048]
            kxnb = ("xnb", b % 2)
            for d in range(8):
                lt = AM[:, d * 1024 + eo * 128: d * 1024 + (eo + 1) * 128]
                rh = xnb[:, d * 256:(d + 1) * 256]
                pl.comp("tensor", lambda te, bank=bank, lt=lt, rh=rh, d=d: te.matmul(
                    bank[:, 0:256], lt, rh, start=(d == 0), stop=False),
                    reads=wsmkeys(d * 1024 + eo * 128, 128) + [kxnb], writes=[kb])
            lt = BQ[:, eo * 128:(eo + 1) * 128]
            pl.comp("tensor", lambda te, bank=bank, lt=lt: te.matmul(
                bank[:, 0:256], lt, BIAS3[0:1, 1088:1344], start=False, stop=True),
                reads=[("bq",), ("ones3",)], writes=[kb])
            mqs = SCA[:, 4096 + (b % 2) * 2048 + eo * 256: 4096 + (b % 2) * 2048 + (eo + 1) * 256]
            pl.comp("vector", lambda v, mqs=mqs, bank=bank: v.tensor_copy(mqs, bank[:, 0:256]),
                    reads=[kb], writes=[("mqs", b % 2, eo), ("xtb", b % 2)])

        def pa_qs(b):
            xts = SCA[:, 4096 + (b % 2) * 2048: 4096 + (b % 2) * 2048 + 2048]
            for eo in range(CH):
                lt = WA[:, eo * 16:(eo + 1) * 16]
                rh = xts[:, eo * 256:(eo + 1) * 256]
                pl.comp("tensor", lambda te, lt=lt, rh=rh, eo=eo: te.matmul(
                    PS[6][0:16, 0:256], lt, rh, start=(eo == 0), stop=False),
                    reads=[("WA", eo), ("mqs", b % 2, eo), ("xtb", b % 2)],
                    writes=[("ps", 6)])
            pl.comp("tensor", lambda te: te.matmul(
                PS[6][0:16, 0:256], BQA, BIAS3[0:1, 1088:1344], start=False, stop=True),
                reads=[("bqa",), ("ones3",)], writes=[("ps", 6)])
            pl.comp("scalar", lambda sc: sc.activation(
                QSS[:], PS[6][0:16, 0:256], AF.Copy, scale=SCALE),
                reads=[("ps", 6)], writes=[("qss",)])
            pl.dma("scalar", "sq",
                   lambda g, b=b: g.dma_start(qs_d[:, b * 256:(b + 1) * 256], QSS[:]),
                   reads=[("qss",)], writes=[("qsd", b)])
            pl.dma("scalar", f"mq{b % 2}",
                   lambda g, b=b: g.dma_start(mq_d[b], xts),
                   reads=[("mqs", b % 2, eo_) for eo_ in range(CH)] + [("xtb", b % 2)],
                   writes=[("mqd", b)])

        pa_xload(0)
        pa_xload(1)
        for t_ in pa_ln_thunks(0):
            t_()
        pa_xload(2)
        for t_ in pa_ln_thunks(1):
            t_()
        pa_xload(3)
        for t_ in pa_tr_thunks(0):
            t_()

        cast_q = []

        def cast_startup_weights():
            for q in range(4):
                emit = (lambda f: f()) if q == 0 else cast_q.append
                for c in range(CH):
                    emit(lambda c=c, q=q: cast(
                        WfcT_d[c * 128:(c + 1) * 128, q * 1024:(q + 1) * 1024],
                        AB1[:, c * 4096 + q * 1024: c * 4096 + (q + 1) * 1024],
                        a1keys(c * 4096 + q * 1024, 1024)))
                for j in range(q * 8, q * 8 + 8):
                    emit(lambda j=j: cast(
                        WprojT_d[j * 128:(j + 1) * 128, :],
                        AB2[:, j * 1024:(j + 1) * 1024], a2keys(j * 1024, 1024)))
                if q == 0:
                    cast_flush()
                    cast_nslots[0] = 4
                    cast_ceng[0] = False

        cast_nslots[0] = 8
        for d in range(8):
            cast(Wq_d[d * 128:(d + 1) * 128, :], AM[:, d * 1024:(d + 1) * 1024],
                 [("WSM", d)])
        for wsrc, dst, nm in ((Wqa_d, WA, "WA"), (Wka_d, WB, "WB")):
            for d in range(8):
                cast(wsrc[d * 128:(d + 1) * 128, :], dst[:, d * 16:(d + 1) * 16],
                     [(nm, d)])
        cast_flush()
        cast_startup_weights()

        for eo in range(CH):
            pa_mq_group(0, eo)
        pa_qs(0)

        H2_COLS = (8192, 8448, 11776, 12032)

        def p0_xnload(b, extra=()):
            xnb = SCA[:, (b % 2) * 2048:(b % 2) * 2048 + 2048]
            pl.dma("sync", f"ltm{b % 2}",
                   lambda s, b=b: s.dma_start(xnb, xn_d[b]),
                   reads=[("xnd", b)], writes=[("xnb", b % 2)] + list(extra))

        def p0_hmm(b, j):
            hb = PS[4 + j % 2]
            khb = ("ps", 4 + j % 2)
            kxnb = ("xnb", b % 2)
            xnb = SCA[:, (b % 2) * 2048:(b % 2) * 2048 + 2048]
            for c in range(CH):
                lt = AB1[:, c * 4096 + j * 128: c * 4096 + (j + 1) * 128]
                rh = xnb[:, c * 256:(c + 1) * 256]
                pl.comp("tensor", lambda te, hb=hb, lt=lt, rh=rh, c=c: te.matmul(
                    hb[:, 0:256], lt, rh, start=(c == 0), stop=(c == 7)),
                    reads=a1keys(c * 4096 + j * 128, 128) + [kxnb], writes=[khb])
            rr = RB[:, (j % 2) * 256:(j % 2) * 256 + 256]
            krr = ("rb", j % 2)
            pl.comp("scalar", lambda sc, rr=rr, hb=hb: sc.activation(
                rr, hb[:, 0:256], AF.Relu), reads=[khb], writes=[krr])
            g3 = (b * 32 + j) % 4
            h2 = AM[:, H2_COLS[g3]: H2_COLS[g3] + 256]
            pl.comp("vector", lambda v, h2=h2, rr=rr: v.tensor_mul(h2, rr, rr),
                    reads=[krr], writes=[("h2", g3)])

        def p0_ymm(b, j):
            g3 = (b * 32 + j) % 4
            h2 = AM[:, H2_COLS[g3]: H2_COLS[g3] + 256]
            kh2 = ("h2", g3)
            for t in range(2):
                lt = h2[:, t * 128:(t + 1) * 128]
                for nh in range(2):
                    yb = PS[2 * t + nh]
                    rh = AB2[:, j * 1024 + nh * 512: j * 1024 + (nh + 1) * 512]
                    pl.comp("tensor", lambda te, yb=yb, lt=lt, rh=rh, j=j: te.matmul(
                        yb[:], lt, rh, start=(j == 0), stop=(j == 31)),
                        reads=[kh2] + a2keys(j * 1024 + nh * 512, 512),
                        writes=[("ps", 2 * t + nh)])

        def p0_yevac(b):
            for t in range(2):
                yt = YS[:, t * 1024:(t + 1) * 1024]
                for nh in range(2):
                    pl.comp("scalar", lambda sc, yt=yt, t=t, nh=nh: sc.copy(
                        yt[:, nh * 512:(nh + 1) * 512], PS[2 * t + nh][:]),
                        reads=[("ps", 2 * t + nh)], writes=[("ys", t)])

        def p0_spike_thunks(b):
            th = []
            for t in range(2):
                yt = YS[:, t * 1024:(t + 1) * 1024]
                kyt = ("ys", t)
                tile = 2 * b + t
                cslot = AM[:, 8704 + (tile % 2) * 1024: 8704 + (tile % 2) * 1024 + 1024]
                kc = ("cnt", tile % 2)
                spk = AM[:, 10752:11776]
                ks_ = ("spk",)
                th.append(lambda yt=yt, cslot=cslot, kyt=kyt, kc=kc: pl.comp(
                    "vector", lambda v, cslot=cslot, yt=yt: v.tensor_scalar(
                        cslot, yt, TAUS[0], None, ALU.is_gt), reads=[kyt], writes=[kc]))
                for i in range(4):
                    o = spk if i % 2 == 0 else cslot
                    inn = cslot if i % 2 == 0 else spk
                    ki = kc if i % 2 == 0 else ks_
                    ko = ks_ if i % 2 == 0 else kc
                    th.append(lambda yt=yt, o=o, inn=inn, kyt=kyt, ki=ki, ko=ko,
                              tau=TAUS[i + 1]: pl.comp(
                        "vector", lambda v, o=o, yt=yt, inn=inn, tau=tau:
                        v.scalar_tensor_tensor(o, yt, tau, inn, ALU.is_gt, ALU.add),
                        reads=[kyt, ki], writes=[ko]))
                th.append(lambda cslot=cslot, tile=tile, kc=kc: pl.dma(
                    "scalar", f"sc{tile % 2}",
                    lambda g, cslot=cslot, tile=tile: g.dma_start(cnt_d[tile], cslot),
                    reads=[kc], writes=[("cntd", tile)]))
            return th

        def prep_thunks(m):
            th = []
            th += pa_ln_thunks(2 * m)
            if 2 * m + 2 < 2 * NB:
                th.append(lambda m=m: pa_xload(2 * m + 2))
            th += pa_ln_thunks(2 * m + 1)
            if 2 * m + 3 < 2 * NB:
                th.append(lambda m=m: pa_xload(2 * m + 3))
            th += pa_tr_thunks(m)
            th += [lambda m=m, eo=eo_: pa_mq_group(m, eo) for eo_ in range(CH)]
            th.append(lambda m=m: pa_qs(m))
            return th

        pa_q = []
        pending = []
        seq = [(b, j) for b in range(NB) for j in range(32)]
        for gi, (b, j) in enumerate(seq):
            if j == 0 and b + 1 < NB:
                pa_q.extend(prep_thunks(b + 1))
            p0_hmm(b, j)
            for _ in range(3):
                if cast_q:
                    cast_q.pop(0)()
                elif cast_pend:
                    cast_pend.pop(0)()
            if pending and j % 2 == 1 and 2 < j < 29:
                pending.pop(0)()
            if pa_q:
                pa_q.pop(0)()
            if pa_q and j % 2 == 0 and j < 14:
                pa_q.pop(0)()
            if gi > 2:
                pb, pj = seq[gi - 3]
                p0_ymm(pb, pj)
                if pj == 31:
                    p0_yevac(pb)
                    pending = p0_spike_thunks(pb)
        while pa_q:
            pa_q.pop(0)()
        while cast_q:
            cast_q.pop(0)()
        cast_flush()
        for (pb, pj) in seq[-3:]:
            p0_ymm(pb, pj)
            if pj == 31:
                p0_yevac(pb)
                pending = p0_spike_thunks(pb)
        for th in pending:
            th()
        pl.patch_total([("xnd", b) for b in range(0, NB, 2)], "scalar", "sx0")
        pl.patch_total([("xnd", b) for b in range(1, NB, 2)], "scalar", "sx1")
        pl.patch_total([("xtmd", g) for g in range(0, NT3, 2)], "scalar", "tm0")
        pl.patch_total([("xtmd", g) for g in range(1, NT3, 2)], "scalar", "tm1")
        pl.patch_total([("mqd", b) for b in range(0, NB, 2)], "scalar", "mq0")
        pl.patch_total([("mqd", b) for b in range(1, NB, 2)], "scalar", "mq1")
        pl.patch_total([("cntd", t) for t in range(0, NT3, 2)], "scalar", "sc0")
        pl.patch_total([("cntd", t) for t in range(1, NT3, 2)], "scalar", "sc1")

        SC1 = SCA[0:16, 0:4096]
        SC2 = SCA[0:16, 4096:8192]
        pl.dma("sync", "lqs", lambda s: s.dma_start(SC1, qs_d.ap()),
               reads=[("qsd", b) for b in range(NB)],
               writes=[("SC1",), ("xnb", 0), ("xnb", 1)])
        cast_ceng[0] = "gp"
        cast_nslots[0] = 2
        for d in range(8):
            cast(WkT_d[d * 128:(d + 1) * 128, :], AB2[:, d * 1024:(d + 1) * 1024],
                 a2keys(d * 1024, 1024))
        cast_flush()
        cast_nslots[0] = 4
        for g in range(NT3):
            dst = AB1[:, g * 1024:(g + 1) * 1024]
            pl.dma("sync", f"ltm{g % 2}",
                   lambda s, dst=dst, g=g: s.dma_start(dst, xtm_d[g]),
                   reads=[("xtmd", g)], writes=a1keys(g * 1024, 1024))
        pl.patch_total([("A1", 2 * g + o) for g in range(0, NT3, 2) for o in (0, 1)],
                       "sync", "ltm0")
        pl.patch_total([("A1", 2 * g + o) for g in range(1, NT3, 2) for o in (0, 1)],
                       "sync", "ltm1")
        for i in range(12):
            dst = AB2[:, 8192 + i * 2048: 8192 + (i + 1) * 2048]
            pl.dma("scalar", f"lb{i % 2}",
                   lambda s, dst=dst, i=i: s.dma_start(dst, xn_d[i]),
                   reads=[("xnd", i)], writes=a2keys(8192 + i * 2048, 2048))
        pl.patch_total([("A2", 16 + 4 * i + o) for i in range(0, 12, 2)
                        for o in range(4)], "scalar", "lb0")
        pl.patch_total([("A2", 16 + 4 * i + o) for i in range(1, 12, 2)
                        for o in range(4)], "scalar", "lb1")

        ENT = AM[:, 8192:8704]
        PBF = AM[0:16, 8704:9728]
        PTT = AM[:, 9728:9856]

        def softmax_en(sc_src, sc_srckeys, sc_tmp, sc_tmpkeys, ent_extra=()):
            rmax, nmax, rsum, rrec = RS[:, 0:1], RS[:, 1:2], RS[:, 2:3], RS[:, 3:4]
            pl.comp("vector", lambda v: v.reduce_max(rmax, sc_src, axis=AX.X),
                    reads=sc_srckeys, writes=[("rs",)])
            pl.comp("vector", lambda v: v.tensor_scalar(nmax, rmax, -1.0, None, ALU.mult),
                    reads=[("rs",)], writes=[("rs",)])
            pl.comp("scalar", lambda sc: sc.activation(sc_tmp, sc_src, AF.Exp, bias=nmax),
                    reads=sc_srckeys + [("rs",)], writes=sc_tmpkeys)
            pl.comp("vector", lambda v: v.reduce_sum(rsum, sc_tmp, axis=AX.X),
                    reads=sc_tmpkeys, writes=[("rs",)])
            pl.comp("vector", lambda v: v.reciprocal(rrec, rsum),
                    reads=[("rs",)], writes=[("rs",)])
            pl.comp("vector", lambda v: v.tensor_scalar(sc_tmp, sc_tmp, rrec, None, ALU.mult),
                    reads=[("rs",)] + sc_tmpkeys, writes=sc_tmpkeys)
            for i in range(NT3):
                lt = sc_tmp[:, i * 128:(i + 1) * 128]
                ot = PS[6][:, i * 16:(i + 1) * 16]
                pl.comp("tensor", lambda te, ot=ot, lt=lt: te.matmul(
                    ot, lt, I128[0:16, 0:16], start=True, stop=True),
                    reads=sc_tmpkeys + [("I128",)], writes=[("ps", 6)])
            pl.comp("vector", lambda v: v.tensor_copy(ENT[:], PS[6][:]),
                    reads=[("ps", 6)], writes=[("ent",)] + list(ent_extra))

        def pool_phase(tag, wrhs_ap, wrhs_col, wrhs_keys):
            for i in range(NT3):
                lt = ENT[:, i * 16:(i + 1) * 16]
                for nh in range(2):
                    rh = AB1[:, i * 1024 + nh * 512: i * 1024 + (nh + 1) * 512]
                    pl.comp("tensor", lambda te, lt=lt, rh=rh, i=i, nh=nh: te.matmul(
                        PS[2 + nh][0:16, :], lt, rh,
                        start=(i == 0), stop=(i == NT3 - 1)),
                        reads=[("ent",)] + a1keys(i * 1024 + nh * 512, 512),
                        writes=[("ps", 2 + nh)])
            for nh in range(2):
                pl.comp("vector", lambda v, nh=nh: v.tensor_copy(
                    PBF[:, nh * 512:(nh + 1) * 512], PS[2 + nh][0:16, :]),
                    reads=[("ps", 2 + nh)], writes=[("pbf",), ("cnt", 0)])
            for c in range(CH):
                lt = PBF[:, c * 128:(c + 1) * 128]
                ot = PS[7][:, c * 16:(c + 1) * 16]
                pl.comp("tensor", lambda te, ot=ot, lt=lt: te.matmul(
                    ot, lt, I128[0:16, 0:16], start=True, stop=True),
                    reads=[("pbf",), ("I128",)], writes=[("ps", 7)])
            pl.comp("vector", lambda v: v.tensor_copy(PTT[:], PS[7][:, 0:128]),
                    reads=[("ps", 7)], writes=[("ptt",), ("cnt", 1)])
            for dc in range(CH):
                lt = PTT[:, dc * 16:(dc + 1) * 16]
                for nh in range(2):
                    rh = wrhs_ap[:, wrhs_col + dc * 1024 + nh * 512: wrhs_col + dc * 1024 + nh * 512 + 512]
                    pl.comp("tensor", lambda te, lt=lt, rh=rh, dc=dc, nh=nh: te.matmul(
                        PS[2 + nh][0:16, :], lt, rh, start=(dc == 0), stop=(dc == 7)),
                        reads=[("ptt",)] + wrhs_keys, writes=[("ps", 2 + nh)])
            for nh in range(2):
                pl.comp("vector", lambda v, nh=nh: v.tensor_tensor(
                    SQ[0:16, nh * 512:(nh + 1) * 512], PS[2 + nh][0:16, :],
                    MASK[:, nh * 512:(nh + 1) * 512], ALU.mult),
                    reads=[("ps", 2 + nh), ("MASK",)], writes=[("mskd",)])
            pl.comp("tensor", lambda te: te.matmul(
                PS[5][0:1, 0:512], ONES[:, 0:1], SQ[0:16, 0:512],
                start=True, stop=True),
                reads=[("mskd",), ("ONES",)], writes=[("ps", 5)])
            pl.comp("tensor", lambda te: te.matmul(
                PS[4][0:1, 0:512], ONES[:, 0:1], SQ[0:16, 512:1024],
                start=True, stop=True),
                reads=[("mskd",), ("ONES",)], writes=[("ps", 4)])

        def row_to_fm(srcrow_bf, srckeys, dst, dstname):
            for c in range(CH):
                lt = srcrow_bf[:, c * 128:(c + 1) * 128]
                ot = PS[7][:, c:c + 1]
                pl.comp("tensor", lambda te, ot=ot, lt=lt: te.matmul(
                    ot, lt, BIAS3[0:1, 1088:1089], start=True, stop=True),
                    reads=srckeys + [("ones3",)], writes=[("ps", 7)])
            pl.comp("vector", lambda v: v.tensor_copy(dst[:], PS[7][:, 0:8]),
                    reads=[("ps", 7)], writes=[(dstname, c_) for c_ in range(CH)])

        softmax_en(SC1, [("SC1",)], SC2, [("SC2",), ("xtb", 0), ("xtb", 1)])
        pool_phase("q", AM, 0, [("WSM", d) for d in range(8)])
        PQROW = YS[0:1, 1024:2048]
        pl.comp("vector", lambda v: v.tensor_tensor(
            PQROW[:, 0:512], PS[5][0:1, 0:512], BQ[:, 0:512], ALU.add),
            reads=[("ps", 5), ("bq",)], writes=[("pqrow",)])
        pl.comp("vector", lambda v: v.tensor_tensor(
            PQROW[:, 512:1024], PS[4][0:1, 0:512], BQ[:, 512:1024], ALU.add),
            reads=[("ps", 4), ("bq",)], writes=[("pqrow",)])
        PQROWB = SQ[0:1, 0:1024]
        pl.comp("vector", lambda v: v.tensor_copy(PQROWB, PQROW),
                reads=[("pqrow",)], writes=[("pqrowb",), ("mskd",)])
        row_to_fm(PQROWB, [("pqrowb",)], PQ, "pq")

        for d in range(8):
            cast(Wt_d[d * 128:(d + 1) * 128, :], AM[:, d * 1024:(d + 1) * 1024],
                 [("WSM", d)])
        cast_flush()

        def ld_x3(t):
            slot = XF[:, (t % 2) * 1024:(t % 2 + 1) * 1024]
            pl.dma("sync", f"l3{t % 2}",
                   lambda s, slot=slot, t=t: s.dma_start(
                       slot, x_d[t * 128:(t + 1) * 128, :]),
                   writes=[("xf3", t % 2), ("x0", t % 2)])

        def ld_cnt3(t):
            slot = AM[:, 10240 + (t % 2) * 1024: 10240 + (t % 2) * 1024 + 1024]
            alias = [("spk",)] if t % 2 == 0 else [("spk",), ("h2", 2), ("h2", 3)]
            pl.dma("sync", f"cr{t % 2}",
                   lambda s, slot=slot, t=t: s.dma_start(slot, cnt_d[t]),
                   reads=[("cntd", t)], writes=[("cntr", t % 2)] + alias)

        for c in range(CH):
            wb = WB[:, c * 16:(c + 1) * 16]
            pl.comp("vector", lambda v, wb=wb, c=c: v.tensor_scalar(
                wb, wb, PQ[:, c:c + 1], None, ALU.mult),
                reads=[("pq", c), ("WB", c)], writes=[("WB", c)])
        for dc in range(CH):
            for fc in range(CH):
                lt = AB2[:, fc * 1024 + dc * 128: fc * 1024 + (dc + 1) * 128]
                rh = WB[:, fc * 16:(fc + 1) * 16]
                pl.comp("tensor", lambda te, lt=lt, rh=rh, dc=dc, fc=fc: te.matmul(
                    PS[4][:, dc * 16:(dc + 1) * 16], lt, rh,
                    start=(fc == 0), stop=(fc == 7)),
                    reads=a2keys(fc * 1024 + dc * 128, 128) + [("WB", fc)],
                    writes=[("ps", 4)])
        pl.comp("vector", lambda v: v.tensor_copy(WKB[:], PS[4][:, 0:128]),
                reads=[("ps", 4)], writes=[("WKB",)])
        BKFB = SQ[:, 16:24]
        for c in range(CH):
            lt = BK[:, c * 128:(c + 1) * 128]
            ot = PS[5][:, c:c + 1]
            pl.comp("tensor", lambda te, ot=ot, lt=lt: te.matmul(
                ot, lt, BIAS3[32:33, 1088:1089], start=True, stop=True),
                reads=[("bk",), ("ones3",)], writes=[("ps", 5)])
        pl.comp("vector", lambda v: v.tensor_copy(BKFB, PS[5][:, 0:8]),
                reads=[("ps", 5)], writes=[("bkfb",), ("pqrowb",)])
        for c in range(CH):
            pl.comp("tensor", lambda te, c=c: te.matmul(
                PS[5][0:16, 0:1], WB[:, c * 16:(c + 1) * 16], BKFB[:, c:c + 1],
                start=(c == 0), stop=False),
                reads=[("WB", c), ("bkfb",)], writes=[("ps", 5)])
        pl.comp("tensor", lambda te: te.matmul(
            PS[5][0:16, 0:1], BKA, BIAS3[32:33, 1088:1089], start=False, stop=True),
            reads=[("bka",), ("ones3",)], writes=[("ps", 5)])
        BIASK = RS[:, 4:5]
        pl.comp("vector", lambda v: v.tensor_scalar(
            BIASK, PS[5][0:16, 0:1], SCALE, None, ALU.mult),
            reads=[("ps", 5)], writes=[("rs",)])
        for d in range(8):
            cast(Wk_d[d * 128:(d + 1) * 128, :],
                 AB2[:, d * 1024:(d + 1) * 1024], a2keys(d * 1024, 1024))
        cast_flush()
        p0_xnload(12, extra=[("SC1",)])
        p0_xnload(13, extra=[("SC1",)])
        for b in range(NB):
            if b < 12:
                base = 8192 + b * 2048
                xnb = AB2[:, base: base + 2048]
                kxn = lambda dc, base=base: a2keys(base + dc * 256, 256)
            else:
                xnb = SCA[:, (b % 2) * 2048:(b % 2) * 2048 + 2048]
                kxn = lambda dc, b=b: [("xnb", b % 2)]
            for dc in range(CH):
                lt = WKB[:, dc * 16:(dc + 1) * 16]
                rh = xnb[:, dc * 256:(dc + 1) * 256]
                pl.comp("tensor", lambda te, lt=lt, rh=rh, dc=dc: te.matmul(
                    PS[4][0:16, 0:256], lt, rh, start=(dc == 0), stop=(dc == 7)),
                    reads=[("WKB",)] + kxn(dc), writes=[("ps", 4)])
            pl.comp("vector", lambda v, b=b: v.tensor_scalar(
                SC2[:, b * 256:(b + 1) * 256], PS[4][0:16, 0:256],
                SCALE, BIASK, ALU.mult, ALU.add),
                reads=[("ps", 4), ("rs",)], writes=[("SC2k", b)])
            if b in (12, 13):
                p0_xnload(b + 2, extra=[("SC1",)])
        for bb in range(12):
            base = 8192 + bb * 2048
            pl.dma("gpsimd", f"lm{bb // 4}",
                   lambda g, bb=bb, base=base: g.dma_start(
                       AB2[:, base: base + 2048], mq_d[bb]),
                   reads=[("mqd", bb)], writes=a2keys(base, 2048))
        for cl in range(3):
            pl.patch_total([("A2", 16 + 4 * bb + o) for bb in range(4 * cl, 4 * cl + 4)
                            for o in range(4)], "gpsimd", f"lm{cl}")
        for t in range(2):
            ld_x3(t)
            ld_cnt3(t)

        softmax_en(SC2, [("SC2k", b) for b in range(NB)], SC1,
                   [("SC1e",), ("SC1",), ("xnb", 0), ("xnb", 1)])
        pool_phase("k", AB2, 0, a2keys(0, 8192))
        PKROW = YS[0:1, 0:1024]
        pl.comp("vector", lambda v: v.tensor_tensor(
            PKROW[:, 0:512], PS[5][0:1, 0:512], BK[:, 0:512], ALU.add),
            reads=[("ps", 5), ("bk",)], writes=[("pkrow",)])
        pl.comp("vector", lambda v: v.tensor_tensor(
            PKROW[:, 512:1024], PS[4][0:1, 0:512], BK[:, 512:1024], ALU.add),
            reads=[("ps", 4), ("bk",)], writes=[("pkrow",)])
        pl.comp("vector", lambda v: v.tensor_mul(PKROW, PKROW, PQROW),
                reads=[("pkrow",), ("pqrow",)], writes=[("pkrow",)])
        PKROWB = SQ[0:1, 0:1024]
        pl.comp("vector", lambda v: v.tensor_copy(PKROWB, PKROW),
                reads=[("pkrow",)], writes=[("pkrowb",), ("mskd",), ("bkfrow",)])
        row_to_fm(PKROWB, [("pkrowb",)], PK, "pk")
        MQ_SLOT_KEYS = {12: [("xnb", 0), ("SC1e",), ("SC1",)],
                        13: [("xnb", 1), ("SC1e",), ("SC1",)],
                        14: [("xtb", 0)] + [("SC2k", b_) for b_ in range(8)],
                        15: [("xtb", 1)] + [("SC2k", b_) for b_ in range(8, 16)]}
        for bb in range(12, NB):
            dst = SCA[:, (bb - 12) * 2048:(bb - 11) * 2048]
            pl.dma("gpsimd", "lm3",
                   lambda g, bb=bb, dst=dst: g.dma_start(dst, mq_d[bb]),
                   reads=[("mqd", bb)], writes=MQ_SLOT_KEYS[bb])
        pl.patch_total([k for bb in range(12, NB) for k in MQ_SLOT_KEYS[bb]],
                       "gpsimd", "lm3")
        for d in range(8):
            cast(Wt_d[d * 128:(d + 1) * 128, :], AM[:, d * 1024:(d + 1) * 1024],
                 [("WSM", d)])
        cast_flush()
        for f in range(8):
            wt = AM[:, f * 1024:(f + 1) * 1024]
            pl.comp("vector", lambda v, wt=wt, f=f: v.tensor_scalar(
                wt, wt, PK[:, f:f + 1], None, ALU.mult),
                reads=[("pk", f), ("WSM", f)], writes=[("WSM", f)])

        def mq_ap(bb):
            if bb < 12:
                return AB2[:, 8192 + bb * 2048: 8192 + (bb + 1) * 2048]
            return SCA[:, (bb - 12) * 2048:(bb - 11) * 2048]

        def mq_keys(bb):
            if bb < 12:
                return a2keys(8192 + bb * 2048, 2048)
            return [MQ_SLOT_KEYS[bb][0]]

        for t in range(NT3):
            bb = t // 2
            kmqb = mq_keys(bb)[0]

            def mqsl(c, t=t, bb=bb):
                return mq_ap(bb)[:, c * 256 + (t % 2) * 128: c * 256 + (t % 2) * 128 + 128]
            pset = 0 if t % 2 == 0 else 2
            b0, b1 = PS[pset], PS[pset + 1]
            kb0, kb1 = ("ps", pset), ("ps", pset + 1)
            for c in range(CH):
                lt = mqsl(c)
                for nh in range(2):
                    bank, kb = (b0, kb0) if nh == 0 else (b1, kb1)
                    rh = AM[:, c * 1024 + nh * 512: c * 1024 + nh * 512 + 512]
                    pl.comp("tensor", lambda te, bank=bank, lt=lt, rh=rh, c=c: te.matmul(
                        bank[:], lt, rh, start=(c == 0), stop=False),
                        reads=[kmqb, ("WSM", c)], writes=[kb])
            for c in range(CH):
                bank, kb = (b0, kb0) if c < 4 else (b1, kb1)
                reg = bank[:, (c % 4) * 128:(c % 4) * 128 + 128]
                lt = mqsl(c)
                pl.comp("tensor", lambda te, reg=reg, lt=lt: te.matmul(
                    reg, lt, I128[:], start=False, stop=False),
                    reads=[kmqb, ("I128",)], writes=[kb])
            for nh in range(2):
                bank, kb = (b0, kb0) if nh == 0 else (b1, kb1)
                rh = BT[:, nh * 512:(nh + 1) * 512]
                pl.comp("tensor", lambda te, bank=bank, rh=rh: te.matmul(
                    bank[:], BIAS3[64:65, 1088:1216], rh, start=False, stop=True),
                    reads=[("bt",), ("ones3",)], writes=[kb])
            cslot = AM[:, 10240 + (t % 2) * 1024: 10240 + (t % 2) * 1024 + 1024]
            gbuf = YS[:, (t % 2) * 1024:(t % 2) * 1024 + 1024]
            galias = ("pkrow",) if t % 2 == 0 else ("pqrow",)
            pl.comp("scalar", lambda sc, cslot=cslot, gbuf=gbuf: sc.activation(
                gbuf, cslot, AF.Silu, scale=0.2),
                reads=[("cntr", t % 2)],
                writes=[("g3", t % 2), galias, ("ys", t % 2)])
            ot = OB[:, (t % 2) * 1024:(t % 2 + 1) * 1024]
            for nh in range(2):
                bank, kb = (b0, kb0) if nh == 0 else (b1, kb1)
                pl.comp("vector", lambda v, ot=ot, bank=bank, gbuf=gbuf, nh=nh:
                        v.tensor_mul(ot[:, nh * 512:(nh + 1) * 512], bank[:],
                                     gbuf[:, nh * 512:(nh + 1) * 512]),
                        reads=[kb, ("g3", t % 2)],
                        writes=[("ob3", t % 2), ("stg", t % 2)])
            xt = XF[:, (t % 2) * 1024:(t % 2 + 1) * 1024]
            pl.comp("vector", lambda v, ot=ot, xt=xt: v.tensor_add(ot, ot, xt),
                    reads=[("xf3", t % 2), ("ob3", t % 2)], writes=[("ob3", t % 2)])
            pl.dma("gpsimd", f"so{t % 2}",
                   lambda g, ot=ot, t=t: g.dma_start(out_d[t * 128:(t + 1) * 128, :], ot),
                   reads=[("ob3", t % 2)], writes=[("outd", t)])
            if t + 2 < NT3:
                ld_x3(t + 2)
                ld_cnt3(t + 2)

        pl.wait_all("sync", list(SY_SEMS))
        pl.wait_all("gpsimd", list(GP_SEMS))
        pl.wait_all("scalar", [s_ for s_ in SC_SEMS if s_ != "act"])

        def replay(eng_name):
            def body(eng):
                for waits, fn, sem, n in pl.ops[eng_name]:
                    for s_, v_ in waits:
                        eng.wait_ge(sems[s_], v_)
                    if fn is not None:
                        inst = fn(eng)
                        inst.then_inc(sems[sem], n)
            return body

        block.sync(replay("sync"))
        block.tensor(replay("tensor"))
        block.scalar(replay("scalar"))
        block.vector(replay("vector"))
        block.gpsimd(replay("gpsimd"))

    return nc


_NC_CACHE = None


def kernel(**inputs):
    global _NC_CACHE
    x = np.ascontiguousarray(inputs["L0_attn_hidden_states"], dtype=np.float32)
    Wk = np.ascontiguousarray(inputs["L0_attn_Wk"], dtype=np.float32)
    wmap = {
        "Wq": np.ascontiguousarray(inputs["L0_attn_Wq"], dtype=np.float32),
        "Wk": Wk,
        "WkT": np.ascontiguousarray(Wk.T),
        "Wt": np.ascontiguousarray(inputs["L0_attn_Wt"], dtype=np.float32),
        "Wqa": np.ascontiguousarray(inputs["L0_attn_Wqa"], dtype=np.float32),
        "Wka": np.ascontiguousarray(inputs["L0_attn_Wka"], dtype=np.float32),
        "WfcT": np.ascontiguousarray(np.asarray(inputs["L0_ffn_W_fc"], dtype=np.float32).T),
        "WprojT": np.ascontiguousarray(np.asarray(inputs["L0_ffn_W_proj"], dtype=np.float32).T),
        "bq": np.ascontiguousarray(np.asarray(inputs["L0_attn_bq"], dtype=np.float32).reshape(1, D)),
        "bk": np.ascontiguousarray(np.asarray(inputs["L0_attn_bk"], dtype=np.float32).reshape(1, D)),
        "bt": np.ascontiguousarray(np.asarray(inputs["L0_attn_bt"], dtype=np.float32).reshape(1, D)),
        "bqa": np.ascontiguousarray(np.asarray(inputs["L0_attn_bqa"], dtype=np.float32).reshape(1, H)),
        "bka": np.ascontiguousarray(np.asarray(inputs["L0_attn_bka"], dtype=np.float32).reshape(1, H)),
    }
    if _NC_CACHE is None:
        _NC_CACHE = build_nc()
    nc = _NC_CACHE
    in_maps = [dict(wmap, x=np.ascontiguousarray(x[b])) for b in range(8)]
    res = run_bass_kernel_spmd(nc, in_maps, core_ids=list(range(8)))
    return np.stack([np.asarray(r["out"], dtype=np.float32) for r in res.results])
